# version 47
# baseline (speedup 1.0000x reference)
import contextlib
import sys

sys.path.insert(0, "/opt/trn_rl_repo")

import numpy as np
import ml_dtypes
import concourse.bass as bass
import concourse.mybir as mybir
from concourse.bass_utils import run_bass_kernel_spmd

F32 = mybir.dt.float32
BF16 = mybir.dt.bfloat16
AF = mybir.ActivationFunctionType
ALU = mybir.AluOpType
AX = mybir.AxisListType

P = 128
D = 1024
CH = D // P
NSEQ = 16
HPAIRS = 8

WSPECS = {
    "wqkv": (3072, 1024, 512),
    "wproj": (1024, 1024, 1024),
    "w1": (4096, 1024, 1024),
    "w2": (4096, 1024, 1024),
    "h1": (4096, 4096, 256),
    "h2": (4096, 4096, 256),
    "w3": (1024, 4096, 256),
}

ENGS = ("sync", "gpsimd", "scalar", "vector", "tensor")


class Sem:
    def __init__(self, es, nc, name):
        self.h = es.enter_context(nc.semaphore(name))
        self.n = 0

    def inc(self, k=1):
        self.n += k
        h = self.h
        return lambda instr: instr.then_inc(h, k)


class Emit:
    def __init__(self, block):
        self.block = block
        self.q = {e: [] for e in ENGS}

    def add(self, eng, *steps):
        self.q[eng].extend(s for s in steps if s is not None)

    def flush(self):
        for eng in ENGS:
            steps = self.q[eng]
            if steps:
                ss = tuple(steps)

                def body(e, ss=ss):
                    for f in ss:
                        f(e)
                getattr(self.block, eng)(body)
                self.q[eng] = []


def build_nc(b_core=256, T=512):
    TOK = b_core * NSEQ
    NT = TOK // T
    G = T // P
    NCH = G * HPAIRS
    assert T % P == 0 and TOK % T == 0 and T <= 512

    nc = bass.Bass("TRN2", target_bir_lowering=False, debug=False)
    es = contextlib.ExitStack()

    x_in = nc.dram_tensor("x", [TOK, D], F32, kind="ExternalInput").ap()
    out_d = nc.dram_tensor("out", [TOK, D], F32, kind="ExternalOutput").ap()
    w_in, w_sc = {}, {}
    for name, (O, K, OU) in WSPECS.items():
        w_in[name] = nc.dram_tensor(name, [O // OU, P, (K // P) * OU], F32, kind="ExternalInput").ap()
        w_sc[name] = nc.dram_tensor(name + "_bf", [O // OU, P, (K // P) * OU], BF16, kind="Internal").ap()
    VECN = {"bqkv": 24, "bproj": 8, "w1s": 32, "w2s": 32, "h1s": 32, "h2s": 32, "w3s": 8}
    vecs_in = {k: nc.dram_tensor(k, [P, n], F32, kind="ExternalInput").ap() for k, n in VECN.items()}

    Tn = lambda *a: es.enter_context(nc.sbuf_tensor(*a))
    xt = Tn("xt", [P, 2, D], F32)
    xt_o = Tn("xt_o", [P, 2, D], F32)
    ot = Tn("ot", [P, 2, D], F32)
    xn_tm = Tn("xn_tm", [P, 4, D], BF16)
    xn_f = Tn("xn_f", [P, CH, T], BF16)
    Ar = Tn("Ar", [P, 32, T], BF16)
    Br = Tn("Br", [P, 32, T], BF16)
    Ao = Tn("Ao", [P, CH, T], BF16)
    Vtm = Tn("Vtm", [P, 4, 1024], BF16)
    sum_f = Tn("sum_f", [P, CH, T], F32)
    wbuf = Tn("wbuf", [P, 3, 8192], BF16)
    t1 = Tn("t1", [P, 4, 512], BF16)
    vs_b = {k: Tn("v_" + k, [P, n], F32) for k, n in VECN.items()}
    stats6 = Tn("stats6", [P, 2, 12], F32)
    mv = Tn("mv", [P, 2, 2], F32)
    rstd = Tn("rstdb", [P, 2, 1], F32)
    lnt = Tn("lnt", [P, 2, 1], F32)
    epsv = Tn("epsv", [P, 1], F32)
    id_bf = Tn("id_bf", [P, P], BF16)
    id_f32 = Tn("id_f32", [P, P], F32)
    m16 = Tn("m16", [16, 16], F32)
    mask = Tn("mask", [P, P], F32)
    sm = Tn("smb", [P, 3, 2, P], F32)
    ee = Tn("eeb", [P, 3, 2, P], F32)
    an = Tn("anb", [P, 3, 2, P], BF16)
    at_sb = Tn("at_sb", [P, 5, 2, P], BF16)
    vt_sb = Tn("vt_sb", [P, 2, P], BF16)
    nmax = Tn("nmax", [P, 3, 2], F32)
    rsum = Tn("rsum", [P, 3, 2], F32)
    rrec = Tn("rrec", [P, 3, 2], F32)

    PSF = es.enter_context(nc.psum_tensor("PSF", [P, 7, 512], F32))
    PSB = es.enter_context(nc.psum_tensor("PSB", [P, 1, 1024], BF16))

    with nc.Block() as block, es:
        ts = Sem(es, nc, "ts")
        vs = Sem(es, nc, "vs")
        as_ = Sem(es, nc, "as_")
        ps = Sem(es, nc, "ps")
        ws6 = [Sem(es, nc, "ws%d" % i) for i in range(6)]
        xlA, xlB = Sem(es, nc, "xlA"), Sem(es, nc, "xlB")
        osA, osB = Sem(es, nc, "osA"), Sem(es, nc, "osB")
        xoA, xoB, xoC, xoD = (Sem(es, nc, "xoA"), Sem(es, nc, "xoB"),
                              Sem(es, nc, "xoC"), Sem(es, nc, "xoD"))
        wbA, wbB, wbC = Sem(es, nc, "wbA"), Sem(es, nc, "wbB"), Sem(es, nc, "wbC")
        wcA, wcB, wcC = Sem(es, nc, "wcA"), Sem(es, nc, "wcB"), Sem(es, nc, "wcC")
        lv = Sem(es, nc, "lv")
        mk = Sem(es, nc, "mk")
        wbs = [wbA, wbB, wbC]
        wsc = [wcA, wcB, wcC]
        xls = [xlA, xlB]
        oss = [osA, osB]
        xos = [xoA, xoB, xoC, xoD]

        E = Emit(block)

        def W(sem, val):
            return lambda e: e.wait_ge(sem.h, val)

        bank_mark = {b: None for b in range(7)}
        psb_mark = {b: None for b in range(1)}
        mark = {}

        def WB(b, table=None):
            m = (table if table is not None else bank_mark)[b]
            return None if m is None else W(m[0], m[1])

        def WM(key):
            m = mark.get(key)
            return None if m is None else W(m[0], m[1])

        steps = []
        for k in vecs_in:
            a = lv.inc(16)
            steps.append(lambda g, a=a, k=k: a(g.dma_start(out=vs_b[k][:, :], in_=vecs_in[k])))
        for tile, val in ((id_bf, 0.0), (id_f32, 0.0), (epsv, 1e-5), (m16, 0.0), (mask, -1e30)):
            a = ps.inc()
            steps.append(lambda g, a=a, tile=tile, val=val: a(g.memset(tile[:, :], val)))
        steps.append(W(ps, ps.n))
        for tile, cmp_, fill, pat in (
                (id_bf, ALU.not_equal, 1.0, [[-1, P]]),
                (id_f32, ALU.not_equal, 1.0, [[-1, P]]),
                (m16, ALU.is_ge, -1e30, [[-4, 4], [0, 4]])):
            a = ps.inc()
            steps.append(lambda g, a=a, tile=tile, cmp_=cmp_, fill=fill, pat=pat: a(
                g.affine_select(out=tile[:, :], in_=tile[:, :], compare_op=cmp_, fill=fill,
                                base=0, pattern=pat, channel_multiplier=1)))
        steps.append(W(ps, ps.n))
        for j in range(8):
            a = mk.inc(16)
            steps.append(lambda g, a=a, j=j: a(
                g.dma_start(out=mask[16 * j:16 * (j + 1), 16 * j:16 * (j + 1)], in_=m16[:, :])))
        E.add("gpsimd", *steps)
        setup_ps, setup_lv, setup_mk = ps.n, lv.n, mk.n

        E.flush()

        NU = sum(O // OU for (O, K, OU) in WSPECS.values())
        iu_ctr = {"i": 0}
        unit_consumed = {}
        wb_val = {}
        last_wb = {0: 0, 1: 0, 2: 0}

        def stream_one(name, j):
            O, K, OU = WSPECS[name]
            iu = iu_ctr["i"]
            slot = iu % 3
            sz = (K // P) * OU
            wbw = (W(wbs[slot], last_wb[slot])
                   if (iu >= 3 and iu - 3 < NU) else None)
            if iu < NU:
                a = wsc[slot].inc(16)
                E.add("gpsimd",
                      W(ts, unit_consumed[iu - 3]) if iu >= 3 else None,
                      wbw,
                      lambda e, a=a, name=name, j=j, slot=slot, sz=sz: a(
                          e.dma_start(out=wbuf[:, slot, 0:sz], in_=w_in[name][j, :, :])))
                wv = (wsc[slot], wsc[slot].n)
                aw = wbs[slot].inc(16)
                E.add("sync",
                      W(*wv),
                      lambda g, aw=aw, name=name, j=j, slot=slot, sz=sz: aw(
                          g.dma_start(out=w_sc[name][j, :, :], in_=wbuf[:, slot, 0:sz])))
                last_wb[slot] = wbs[slot].n
                wb_val[(name, j)] = (wbs[slot], wbs[slot].n)
            else:
                sem = ws6[iu % 6]
                a = sem.inc(16)
                E.add("sync" if iu % 2 == 0 else "gpsimd",
                      W(ts, unit_consumed[iu - 3]),
                      wbw,
                      W(*wb_val[(name, j)]) if iu < 2 * NU else None,
                      lambda e, a=a, name=name, j=j, slot=slot, sz=sz: a(
                          e.dma_start(out=wbuf[:, slot, 0:sz], in_=w_sc[name][j, :, :])))
                wv = (sem, sem.n)
            iu_ctr["i"] += 1
            return iu, slot, wv, n_oc_of(OU), K, OU

        def n_oc_of(OU):
            return OU // P

        def lhsT_of(slot, kc, jj, OU):
            return wbuf[:, slot, kc * OU + jj * P: kc * OU + (jj + 1) * P]

        def MM(attach, out, lhsT, rhs, start, stop, tile_position=None):
            return lambda e: attach(e.matmul(out=out, lhsT=lhsT, rhs=rhs, start=start,
                                             stop=stop, tile_position=tile_position))

        def MM_noinc(out, lhsT, rhs, start, stop, tile_position=None):
            return lambda e: e.matmul(out=out, lhsT=lhsT, rhs=rhs, start=start,
                                      stop=stop, tile_position=tile_position)

        def mm_group(slot, jj, K, OU, out, rhs_tile, waits, kc_waits=None):
            steps = list(waits)
            nk = K // P
            for kc in range(nk):
                if kc_waits is not None and kc_waits[kc] is not None:
                    steps.append(kc_waits[kc])
                a = ts.inc() if kc == nk - 1 else None
                steps.append(MM(a, out, lhsT_of(slot, kc, jj, OU), rhs_tile[kc],
                                kc == 0, kc == nk - 1) if a else
                             MM_noinc(out, lhsT_of(slot, kc, jj, OU), rhs_tile[kc],
                                      kc == 0, kc == nk - 1))
            E.add("tensor", *steps)
            return ts.n

        def phase_L_pre_group(t, g):
                slot = (t * G + g) % 2
                a = xls[slot].inc(16)
                E.add("gpsimd", WM(("xt", slot)),
                      lambda e, a=a, t=t, g=g, slot=slot: a(e.dma_start(
                          out=xt[:, slot, :],
                          in_=x_in[(t * T + g * P):(t * T + (g + 1) * P), :])))
                xl_val = xls[slot].n

                a1, a2 = vs.inc(), vs.inc()
                wmid = W(vs, vs.n)
                a3 = vs.inc()
                E.add("vector",
                      W(xls[slot], xl_val), WM(("stats6", slot)),
                      lambda v, a=a1, slot=slot: a(v.bn_stats(out=stats6[:, slot, 0:6],
                                                              in_=xt[:, slot, 0:512])),
                      lambda v, a=a2, slot=slot: a(v.bn_stats(out=stats6[:, slot, 6:12],
                                                              in_=xt[:, slot, 512:1024])),
                      wmid, WM(("mv", slot)),
                      lambda v, a=a3, slot=slot: a(v.bn_aggr(out=mv[:, slot, :],
                                                             in_=stats6[:, slot, :])))
                mark[("stats6", slot)] = (vs, vs.n)
                mv_done = vs.n

                a1 = as_.inc()
                wmid = W(as_, as_.n)
                a2 = as_.inc()
                E.add("scalar",
                      W(vs, mv_done), W(ps, setup_ps), WM(("rstd", slot)),
                      lambda s, a=a1, slot=slot: a(s.activation(
                          out=lnt[:, slot, :], in_=mv[:, slot, 1:2], func=AF.Ln, bias=epsv[:, :])),
                      wmid,
                      lambda s, a=a2, slot=slot: a(s.activation(
                          out=rstd[:, slot, :], in_=lnt[:, slot, :], func=AF.Exp, scale=-0.5)))
                rstd_done = as_.n

                xslot = g % 4
                a1 = vs.inc()
                E.add("vector",
                      W(as_, rstd_done), WM(("xn_tm", xslot)),
                      lambda v, a=a1, slot=slot, xslot=xslot: a(v.tensor_scalar(
                          out=xn_tm[:, xslot, :], in0=xt[:, slot, :], scalar1=mv[:, slot, 0:1],
                          scalar2=rstd[:, slot, 0:1], op0=ALU.subtract, op1=ALU.mult)))
                mark[("xt", slot)] = (vs, vs.n)
                mark[("mv", slot)] = (vs, vs.n)
                mark[("rstd", slot)] = (vs, vs.n)
                return vs.n

        def phase_L_pre(t):
            return [phase_L_pre_group(t, g) for g in range(G)]

        def phase_L_tr_group(t, applies, g):
                xslot = g % 4
                apply_done = applies[g]
                pb = 0
                steps = [W(vs, apply_done), W(ps, setup_ps), WB(pb, psb_mark)]
                for j in range(CH):
                    a = ts.inc() if j == CH - 1 else None
                    steps.append(lambda e, a=a, j=j, pb=pb, xslot=xslot: (
                        a(e.transpose(out=PSB[:, pb, j * P:(j + 1) * P],
                                      in_=xn_tm[:, xslot, j * P:(j + 1) * P],
                                      identity=id_bf[:, :])) if a else
                        e.transpose(out=PSB[:, pb, j * P:(j + 1) * P],
                                    in_=xn_tm[:, xslot, j * P:(j + 1) * P],
                                    identity=id_bf[:, :])))
                E.add("tensor", *steps)
                mark[("xn_tm", xslot)] = (ts, ts.n)
                tr_done = ts.n

                a1 = vs.inc()
                E.add("vector",
                      W(ts, tr_done), WM("xn_f"),
                      lambda v, a=a1, g=g, pb=pb: a(v.tensor_copy(
                          out=xn_f[:, :, g * P:(g + 1) * P],
                          in_=PSB[:, pb, :].rearrange("p (c t) -> p c t", c=CH))))
                psb_mark[pb] = (vs, vs.n)
                return vs.n

        def phase_L_tr(t, applies):
            for g in range(G):
                phase_L_tr_group(t, applies, g)
            return vs.n

        def phase_Q(t, ln_all):
            for j in range(4):
                iu, slot, wval, n_oc, K, OU = stream_one("wqkv", j)
                for jj in range(n_oc):
                    oc = j * n_oc + jj
                    b = oc % 6
                    waits = [W(*wval)] if jj == 0 else []
                    if j == 0 and jj == 0:
                        waits.append(W(vs, ln_all))
                    waits.append(WB(b))
                    mm_group(slot, jj, K, OU, PSF[:, b, 0:T],
                             [xn_f[:, kc, 0:T] for kc in range(K // P)], waits)
                unit_consumed[iu] = ts.n
                mm_done = ts.n

                steps = [W(ts, mm_done)]
                if j == 0:
                    steps += [W(lv, setup_lv), WM("Br")]
                for jj in range(n_oc):
                    oc = j * n_oc + jj
                    b = oc % 6
                    a = vs.inc()
                    if oc < 8:
                        steps.append(lambda v, a=a, oc=oc, b=b: a(v.tensor_scalar(
                            out=Br[:, 16 + oc, 0:T], in0=PSF[:, b, 0:T],
                            scalar1=vs_b["bqkv"][:, oc:oc + 1], scalar2=0.125,
                            op0=ALU.add, op1=ALU.mult)))
                    else:
                        steps.append(lambda v, a=a, oc=oc, b=b: a(v.tensor_scalar(
                            out=Br[:, 16 + oc, 0:T], in0=PSF[:, b, 0:T],
                            scalar1=vs_b["bqkv"][:, oc:oc + 1], scalar2=None, op0=ALU.add)))
                    bank_mark[b] = (vs, vs.n)
                E.add("vector", *steps)
            for j in range(4, 6):
                iu, slot, wval, n_oc, K, OU = stream_one("wqkv", j)
                vh = j - 4
                for g in range(G):
                    b = (g % 2)
                    waits = [W(*wval)] if g == 0 else []
                    waits.append(WB(b))
                    if j == 4 and g == 0:
                        waits.append(WM("Vtm"))
                    nk = K // P
                    steps = list(w for w in waits if w is not None)
                    for kc in range(nk):
                        aa = ts.inc() if kc == nk - 1 else None
                        lh = xn_f[:, kc, g * P:(g + 1) * P]
                        rh = wbuf[:, slot, kc * OU: (kc + 1) * OU]
                        if aa:
                            steps.append(lambda e, aa=aa, lh=lh, rh=rh, b=b, kc=kc, nk=nk: aa(
                                e.matmul(out=PSF[:, b, 0:512], lhsT=lh, rhs=rh,
                                         start=(kc == 0), stop=(kc == nk - 1))))
                        else:
                            steps.append(lambda e, lh=lh, rh=rh, b=b, kc=kc, nk=nk:
                                e.matmul(out=PSF[:, b, 0:512], lhsT=lh, rhs=rh,
                                         start=(kc == 0), stop=(kc == nk - 1)))
                    E.add("tensor", *steps)
                    mmv = ts.n
                    a = vs.inc()
                    E.add("vector", W(ts, mmv),
                          lambda v, a=a, g=g, vh=vh, b=b: a(v.tensor_copy(
                              out=Vtm[:, g, vh * 512:(vh + 1) * 512], in_=PSF[:, b, 0:512])))
                    bank_mark[b] = (vs, vs.n)
                unit_consumed[iu] = ts.n
            return vs.n

        def plan_scores(c, qkv_done):
            g, hp = divmod(c, HPAIRS)
            tok = slice(g * P, (g + 1) * P)
            waits = []
            if c == 0:
                waits += [W(vs, qkv_done), W(mk, setup_mk)]
            B0 = 2 + 2 * (c % 2)
            waits.append(WB(B0))
            waits.append(WB(B0 + 1))
            a1, a2 = ts.inc(), ts.inc()
            E.add("tensor", *waits,
                  MM(a1, PSF[:, B0, 0:128], Br[0:64, 16 + hp, tok],
                     Br[0:64, 24 + hp, tok], True, True),
                  MM(a2, PSF[:, B0 + 1, 0:128], Br[64:128, 16 + hp, tok],
                     Br[64:128, 24 + hp, tok], True, True))
            return ts.n

        def plan_smmax(c, sc_done):
            pr = c % 3
            B0 = 2 + 2 * (c % 2)
            for h in range(2):
                a1 = vs.inc()
                wmid = W(vs, vs.n)
                a2 = vs.inc()
                E.add("vector",
                      W(ts, sc_done - (1 - h)), WM(("sm", pr, h)),
                      lambda v, a=a1, pr=pr, h=h, b=B0 + h: a(v.tensor_tensor(
                          out=sm[:, pr, h, :], in0=PSF[:, b, 0:128],
                          in1=mask[:, :], op=ALU.add)),
                      wmid, WM(("nmax", pr, h)),
                      lambda v, a=a2, pr=pr, h=h: a(v.reduce_max(
                          out=nmax[:, pr, h:h + 1], in_=sm[:, pr, h, :], axis=AX.X, negate=True)))
                bank_mark[B0 + h] = (vs, vs.n - 1)
            sm_done = vs.n
            for h in range(2):
                a3 = as_.inc()
                E.add("scalar",
                      W(vs, sm_done) if h == 0 else None,
                      WM(("ee", pr, h)), WM(("rsum", pr, h)),
                      lambda s, a=a3, pr=pr, h=h: a(s.activation(
                          out=ee[:, pr, h, :], in_=sm[:, pr, h, :], func=AF.Exp,
                          bias=nmax[:, pr, h:h + 1], accum_out=rsum[:, pr, h:h + 1])))
                mark[("sm", pr, h)] = (as_, as_.n)
                mark[("nmax", pr, h)] = (as_, as_.n)
            return as_.n

        def plan_norm(c, exp_done):
            pr = c % 3
            for h in range(2):
                a4 = vs.inc()
                wmid = W(vs, vs.n)
                a5 = vs.inc()
                E.add("vector",
                      W(as_, exp_done) if h == 0 else None,
                      WM(("rrec", pr, h)),
                      lambda v, a=a4, pr=pr, h=h: a(v.reciprocal(
                          out=rrec[:, pr, h:h + 1], in_=rsum[:, pr, h:h + 1])),
                      wmid, WM(("an", pr, h)),
                      lambda v, a=a5, pr=pr, h=h: a(v.tensor_scalar(
                          out=an[:, pr, h, :], in0=ee[:, pr, h, :],
                          scalar1=rrec[:, pr, h:h + 1], scalar2=None, op0=ALU.mult)))
                mark[("ee", pr, h)] = (vs, vs.n)
                mark[("rsum", pr, h)] = (vs, vs.n)
                mark[("rrec", pr, h)] = (vs, vs.n)
            return vs.n

        def plan_tr3(c, norm_done):
            pr = c % 3
            off = (c % 2) * 2 * P
            a3 = ts.inc()
            E.add("tensor",
                  W(vs, norm_done), WB(0, psb_mark),
                  lambda e, pr=pr, off=off: e.transpose(
                      out=PSB[:, 0, off:off + P], in_=an[:, pr, 0, :], identity=id_bf[:, :]),
                  lambda e, a=a3, pr=pr, off=off: a(e.transpose(
                      out=PSB[:, 0, off + P:off + 2 * P], in_=an[:, pr, 1, :],
                      identity=id_bf[:, :])))
            mark[("an", pr, 0)] = (ts, ts.n)
            mark[("an", pr, 1)] = (ts, ts.n)
            return ts.n

        def plan_trevac(c, tr3_done):
            pr = c % 5
            off = (c % 2) * 2 * P
            a1 = vs.inc()
            E.add("vector",
                  W(ts, tr3_done), WM(("at", pr)),
                  lambda v, a=a1, pr=pr, off=off: a(v.tensor_copy(
                      out=at_sb[:, pr, :, :],
                      in_=PSB[:, 0, off:off + 2 * P].rearrange("p (h t) -> p h t", h=2))))
            psb_mark[0] = (vs, vs.n)
            return vs.n

        def plan_av(c, ev_done):
            pr = c % 5
            g, hp = divmod(c, HPAIRS)
            BA = 6
            a2 = ts.inc()
            E.add("tensor",
                  W(vs, ev_done), WB(BA),
                  MM_noinc(PSF[0:64, BA, 0:128], Vtm[:, g, hp * 128: hp * 128 + 64],
                           at_sb[:, pr, 0, :], True, True, (0, 0)),
                  MM(a2, PSF[64:128, BA, 0:128], Vtm[:, g, hp * 128 + 64: hp * 128 + 128],
                     at_sb[:, pr, 1, :], True, True, (0, 64)))
            mark[("at", pr)] = (ts, ts.n)
            return ts.n

        def plan_avevac(c, av_done):
            g, hp = divmod(c, HPAIRS)
            a1 = vs.inc()
            E.add("vector",
                  W(ts, av_done), WM("attnout") if c == 0 else None,
                  lambda v, a=a1, g=g, hp=hp, BA=6: a(v.tensor_scalar(
                      out=Ao[:, hp, g * P:(g + 1) * P], in0=PSF[:, BA, 0:128],
                      scalar1=vs_b["bqkv"][:, 16 + hp:17 + hp], scalar2=None,
                      op0=ALU.add)))
            bank_mark[6] = (vs, vs.n)
            return vs.n

        def phase_F1(t):
            NOC = 32
            f1_state = {}
            cur = {}
            for k in range(2 * NOC):
                a_ss, w_ss = k // 8, k % 8
                if w_ss == 0 and a_ss % 2 == 0:
                    cur["u1"] = stream_one("w1", a_ss // 2)
                    cur["u2"] = stream_one("w2", a_ss // 2)
                (iu1, slot1, wval1, n_oc, K, OU) = cur["u1"]
                (iu2, slot2, wval2, _, _, _) = cur["u2"]
                half = 4 * (a_ss % 2)

                if w_ss < 4:
                    oc = a_ss * 4 + w_ss
                    b = oc % 4
                    waits = [W(*wval1)] if (w_ss == 0 and half == 0) else []
                    waits.append(WB(b))
                    mmv = mm_group(slot1, half + w_ss, K, OU, PSF[:, b, 0:T],
                                   [xn_f[:, kc, 0:T] for kc in range(K // P)], waits)
                    f1_state[("u1", oc)] = mmv
                    if w_ss == 3:
                        if half == 4:
                            unit_consumed[iu1] = ts.n
                        steps = []
                        for q in range(4):
                            occ = a_ss * 4 + q
                            a1 = as_.inc()
                            steps += [W(ts, f1_state[("u1", occ)]),
                                      WM(("t1", occ % 4)),
                                      lambda s, a=a1, occ=occ, bq=occ % 4: a(s.activation(
                                          out=t1[:, bq, 0:T], in_=PSF[:, bq, 0:T],
                                          func=AF.Silu, scale=vs_b["w1s"][:, occ:occ + 1]))]
                            f1_state[("silu", occ)] = as_.n
                            bank_mark[occ % 4] = (as_, as_.n)
                        E.add("scalar", *steps)
                else:
                    oc = a_ss * 4 + (w_ss - 4)
                    b = oc % 4
                    waits = [W(*wval2)] if (w_ss == 4 and half == 0) else []
                    waits.append(W(as_, f1_state[("silu", oc)]))
                    waits.append(WB(b))
                    mmv = mm_group(slot2, half + w_ss - 4, K, OU, PSF[:, b, 0:T],
                                   [xn_f[:, kc, 0:T] for kc in range(K // P)], waits)
                    if w_ss == 7 and half == 4:
                        unit_consumed[iu2] = ts.n
                    a2 = vs.inc()
                    E.add("vector",
                          W(ts, mmv), W(as_, f1_state[("silu", oc)]),
                          lambda v, a=a2, oc=oc, b=b: a(v.scalar_tensor_tensor(
                              out=Ar[:, oc, 0:T], in0=PSF[:, b, 0:T],
                              scalar=vs_b["w2s"][:, oc:oc + 1], in1=t1[:, oc % 4, 0:T],
                              op0=ALU.mult, op1=ALU.mult)))
                    bank_mark[b] = (vs, vs.n)
                    mark[("t1", oc % 4)] = (vs, vs.n)
            return vs.n, ts.n

        def phase_H1A(t, qkv_done, u_done, proj_mm_prev, h2_mm_prev):
            applies_next = []
            gelu_marks = []
            """h1 (Br = gelu(h1 @ Ar), 32 groups, banks 0-3) with the NCH
            attention chains woven in (banks 4-5, DVE/ACT otherwise idle
            here). q/k live in Br[16:32] and are guarded per chunk against
            h1's gelu evacs; all chains finish by group ~19."""
            mark["attnout"] = (ts, proj_mm_prev)
            O, K, OU = WSPECS["h1"]
            sc_d, exp_d, norm_d, tr3_d, ev_d, av_d = {}, {}, {}, {}, {}, {}
            cur = [None]
            for gi in range(32):
                jj = gi % 2
                if jj == 0:
                    cur[0] = stream_one("h1", gi // 2)
                iu, slot, wval, n_oc, K, OU = cur[0]

                prc = [c for c in (2 * (gi - 2), 2 * (gi - 2) + 1) if 0 <= c < NCH]
                for c in prc:
                    tr3_d[c] = plan_tr3(c, norm_d[c])
                for c in prc:
                    ev_d[c] = plan_trevac(c, tr3_d[prc[-1]])
                for c in (2 * (gi - 1), 2 * (gi - 1) + 1):
                    if 0 <= c < NCH:
                        norm_d[c] = plan_norm(c, exp_d[c])
                for c in (2 * gi, 2 * gi + 1):
                    if c < NCH:
                        sc_d[c] = plan_scores(c, qkv_done)
                        exp_d[c] = plan_smmax(c, sc_d[c])
                c_odd = 2 * (gi - 4) + 1
                if 0 <= c_odd < NCH:
                    av_d[c_odd] = plan_av(c_odd, ev_d[c_odd])
                    plan_avevac(c_odd, av_d[c_odd])

                if gi in (20, 23, 26, 29) and t + 1 < NT:
                    applies_next.append(phase_L_pre_group(t + 1, (gi - 20) // 3))

                oc = gi
                b = oc % 2
                waits = [W(*wval)] if jj == 0 else []
                if gi == 0:
                    waits.append(W(vs, u_done))
                waits.append(WB(b))
                mmv = mm_group(slot, jj, K, OU, PSF[:, b, 0:T],
                               [Ar[:, kc, 0:T] for kc in range(K // P)], waits)
                if jj == n_oc - 1:
                    unit_consumed[iu] = ts.n
                c_ev = 2 * (gi - 3)
                if 0 <= c_ev < NCH:
                    av_d[c_ev] = plan_av(c_ev, ev_d[c_ev])
                    plan_avevac(c_ev, av_d[c_ev])
                steps = [W(ts, mmv)]
                if gi == 0:
                    steps.append(W(ts, h2_mm_prev))
                if oc >= 16:
                    steps.append(W(ts, sc_d[24 + (oc - 16) % 8]))
                a = as_.inc()
                steps.append(lambda s, a=a, oc=oc, b=b: a(
                    s.activation(out=Br[:, oc, 0:T], in_=PSF[:, b, 0:T], func=AF.Gelu,
                                 scale=vs_b["h1s"][:, oc:oc + 1])))
                E.add("scalar", *steps)
                bank_mark[b] = (as_, as_.n)
                gelu_marks.append((as_, as_.n))
            for c in range(NCH):
                if c not in av_d:
                    av_d[c] = plan_av(c, ev_d[c])
                    plan_avevac(c, av_d[c])
            return vs.n, ts.n, as_.n, applies_next, gelu_marks

        def phase_P(t, attn_done, fillers=()):
            fillers = list(fillers)
            for j in range(WSPECS["wproj"][0] // WSPECS["wproj"][2]):
                iu, slot, wval, n_oc, K, OU = stream_one("wproj", j)
                for jj in range(n_oc):
                    oc = j * n_oc + jj
                    b = oc % 6
                    waits = [W(*wval)] if jj == 0 else []
                    if j == 0 and jj == 0:
                        waits.append(W(vs, attn_done))
                    waits.append(WB(b))
                    mmv = mm_group(slot, jj, K, OU, PSF[:, b, 0:T],
                                   [Ao[:, kc, 0:T] for kc in range(K // P)], waits)
                    if fillers:
                        fillers.pop(0)()
                    steps = [W(ts, mmv)]
                    if j == 0 and jj == 0:
                        steps.append(WM("sum"))
                    a = vs.inc()
                    steps.append(lambda v, a=a, oc=oc, b=b: a(v.tensor_scalar(
                        out=sum_f[:, oc, :], in0=PSF[:, b, 0:T],
                        scalar1=vs_b["bproj"][:, oc:oc + 1], scalar2=None, op0=ALU.add)))
                    bank_mark[b] = (vs, vs.n)
                    E.add("vector", *steps)
                unit_consumed[iu] = ts.n
            return ts.n

        def phase_H(t, wname, sname, src, dst, src_ready, dst_free,
                    src_chunk_marks=None):
            O, K, OU = WSPECS[wname]
            for j in range(O // OU):
                iu, slot, wval, n_oc, K, OU = stream_one(wname, j)
                for jj in range(n_oc):
                    oc = j * n_oc + jj
                    b = oc % 6
                    waits = [W(*wval)] if jj == 0 else []
                    kcw = None
                    if j == 0 and jj == 0:
                        if src_chunk_marks is not None:
                            kcw = [W(src_chunk_marks[kc][0], src_chunk_marks[kc][1])
                                   for kc in range(K // P)]
                        else:
                            waits.append(W(src_ready[0], src_ready[1]))
                    waits.append(WB(b))
                    mmv = mm_group(slot, jj, K, OU, PSF[:, b, 0:T],
                                   [src[:, kc, 0:T] for kc in range(K // P)], waits,
                                   kc_waits=kcw)
                    steps = [W(ts, mmv)]
                    if j == 0 and jj == 0:
                        steps.append(W(dst_free[0], dst_free[1]))
                    a = as_.inc()
                    steps.append(lambda s, a=a, oc=oc, b=b, sname=sname, dst=dst: a(
                        s.activation(out=dst[:, oc, 0:T], in_=PSF[:, b, 0:T], func=AF.Gelu,
                                     scale=vs_b[sname][:, oc:oc + 1])))
                    E.add("scalar", *steps)
                    bank_mark[b] = (as_, as_.n)
                unit_consumed[iu] = ts.n
            return ts.n, as_.n

        def phase_W3(t, src_ready):
            for j in range(1024 // 256):
                iu, slot, wval, n_oc, K, OU = stream_one("w3", j)
                for jj in range(n_oc):
                    oc = j * n_oc + jj
                    b = oc % 6
                    waits = [W(*wval)] if jj == 0 else []
                    if j == 0 and jj == 0:
                        waits.append(W(src_ready[0], src_ready[1]))
                    waits.append(WB(b))
                    mmv = mm_group(slot, jj, K, OU, PSF[:, b, 0:T],
                                   [Ar[:, kc, 0:T] for kc in range(K // P)], waits)
                    a = vs.inc()
                    E.add("vector", W(ts, mmv),
                          lambda v, a=a, oc=oc, b=b: a(v.scalar_tensor_tensor(
                              out=sum_f[:, oc, :], in0=PSF[:, b, 0:T],
                              scalar=vs_b["w3s"][:, oc:oc + 1], in1=sum_f[:, oc, :],
                              op0=ALU.mult, op1=ALU.add)))
                    bank_mark[b] = (vs, vs.n)
                unit_consumed[iu] = ts.n
            return ts.n, vs.n

        def plan_xo_load(t, g):
            slot = g % 2
            a = xos[slot].inc(16)
            E.add("scalar", WM(("xt_o", slot)),
                  lambda e, a=a, t=t, g=g, slot=slot: a(e.dma_start(
                      out=xt_o[:, slot, :],
                      in_=x_in[(t * T + g * P):(t * T + (g + 1) * P), :])))
            return xos[slot].n

        def plan_xo_loads(t):
            return [plan_xo_load(t, g) for g in range(2)]

        def phase_O(t, sum_done, xo_vals):
            for g in range(G):
                slot = g % 2
                b0 = 2 + (g % 2) * 2
                xl_val = xo_vals[g]

                steps = [W(vs, sum_done), WB(b0), WB(b0 + 1)]
                for j in range(CH):
                    b = b0 + j // 4
                    a1 = ts.inc() if j == CH - 1 else None
                    steps.append(lambda e, a=a1, j=j, b=b, g=g: (
                        a(e.transpose(out=PSF[:, b, (j % 4) * P:(j % 4 + 1) * P],
                                      in_=sum_f[:, j, g * P:(g + 1) * P],
                                      identity=id_f32[:, :])) if a else
                        e.transpose(out=PSF[:, b, (j % 4) * P:(j % 4 + 1) * P],
                                    in_=sum_f[:, j, g * P:(g + 1) * P],
                                    identity=id_f32[:, :])))
                E.add("tensor", *steps)
                otr_done = ts.n

                oslot = g % 2
                a1, a2 = vs.inc(), vs.inc()
                E.add("vector",
                      W(ts, otr_done), W(xos[slot], xl_val), WM(("ot", oslot)),
                      lambda v, a=a1, slot=slot, oslot=oslot, b0=b0: a(v.tensor_tensor(
                          out=ot[:, oslot, 0:512], in0=PSF[:, b0, 0:512],
                          in1=xt_o[:, slot, 0:512], op=ALU.add)),
                      lambda v, a=a2, slot=slot, oslot=oslot, b0=b0: a(v.tensor_tensor(
                          out=ot[:, oslot, 512:1024], in0=PSF[:, b0 + 1, 0:512],
                          in1=xt_o[:, slot, 512:1024], op=ALU.add)))
                mark[("xt_o", slot)] = (vs, vs.n)
                bank_mark[b0] = (vs, vs.n - 1)
                bank_mark[b0 + 1] = (vs, vs.n)
                add_done = vs.n
                if g + 2 < G:
                    xo_vals.append(plan_xo_load(t, g + 2))

                a = oss[oslot].inc(16)
                E.add("scalar",
                      W(vs, add_done),
                      lambda e, a=a, t=t, g=g, oslot=oslot: a(e.dma_start(
                          out=out_d[(t * T + g * P):(t * T + (g + 1) * P), :],
                          in_=ot[:, oslot, :])))
                mark[("ot", oslot)] = (oss[oslot], oss[oslot].n)
            mark["sum"] = (ts, ts.n)

        proj_mm_prev = 0
        h2_mm_prev = 0
        ln_next = phase_L_tr(0, phase_L_pre(0))
        for t in range(NT):
            ln_all = ln_next
            qkv_done = phase_Q(t, ln_all)
            u_done, f1_mm_done = phase_F1(t)
            mark["xn_f"] = (ts, f1_mm_done)
            attn_done, h1_mm, h1_ev, applies_next, gelu_marks = phase_H1A(
                t, qkv_done, u_done, proj_mm_prev, h2_mm_prev)
            if t + 1 < NT:
                fil = [lambda g=g: phase_L_tr_group(t + 1, applies_next, g)
                       for g in range(G)]
            else:
                fil = []
            proj_mm_prev = phase_P(t, attn_done, fil)
            if t + 1 < NT:
                ln_next = vs.n
            xo_vals = plan_xo_loads(t)
            h2_mm, h2_ev = phase_H(t, "h2", "h2s", Br, Ar, (as_, h1_ev), (ts, h1_mm),
                                   src_chunk_marks=gelu_marks)
            h2_mm_prev = h2_mm
            mark["Br"] = (ts, h2_mm)
            w3_mm_prev, sum_done = phase_W3(t, (as_, h2_ev))
            phase_O(t, sum_done, xo_vals)
            E.flush()

        a_fin = None
        E.add("scalar", W(osA, osA.n), W(osB, osB.n))
        E.flush()

    return nc


def _layout_weight(w, OU):
    O, K = w.shape
    return np.ascontiguousarray(
        w.reshape(O // OU, OU, K // P, P).transpose(0, 3, 2, 1).reshape(O // OU, P, (K // P) * OU))


def _layout_vec(v):
    n = v.shape[0] // P
    return np.ascontiguousarray(v.reshape(n, P).T)


def kernel(L0_attn_x, L0_attn_Wqkv, L0_attn_bqkv, L0_attn_Wproj, L0_attn_bproj,
           L0_ffn_w1_w, L0_ffn_w1_s, L0_ffn_w2_w, L0_ffn_w2_s, L0_ffn_h1_w,
           L0_ffn_h1_s, L0_ffn_h2_w, L0_ffn_h2_s, L0_ffn_w3_w, L0_ffn_w3_s):
    NCORES = 8
    x = np.asarray(L0_attn_x)
    n, b, d = x.shape
    b_core = b // NCORES
    nc = build_nc(b_core=b_core, T=512)

    wmats = {
        "wqkv": np.asarray(L0_attn_Wqkv), "wproj": np.asarray(L0_attn_Wproj),
        "w1": np.asarray(L0_ffn_w1_w), "w2": np.asarray(L0_ffn_w2_w),
        "h1": np.asarray(L0_ffn_h1_w), "h2": np.asarray(L0_ffn_h2_w),
        "w3": np.asarray(L0_ffn_w3_w),
    }
    wblocked = {k: _layout_weight(w, WSPECS[k][2]) for k, w in wmats.items()}
    vecs = {
        "bqkv": _layout_vec(np.asarray(L0_attn_bqkv)),
        "bproj": _layout_vec(np.asarray(L0_attn_bproj)),
        "w1s": _layout_vec(np.asarray(L0_ffn_w1_s)),
        "w2s": _layout_vec(np.asarray(L0_ffn_w2_s)),
        "h1s": _layout_vec(np.asarray(L0_ffn_h1_s)),
        "h2s": _layout_vec(np.asarray(L0_ffn_h2_s)),
        "w3s": _layout_vec(np.asarray(L0_ffn_w3_s)),
    }
    in_maps = []
    for c in range(NCORES):
        xc = np.ascontiguousarray(
            x[:, c * b_core:(c + 1) * b_core, :].transpose(1, 0, 2).reshape(b_core * n, d))
        m = {"x": xc}
        m.update(wblocked)
        m.update(vecs)
        in_maps.append(m)

    res = run_bass_kernel_spmd(nc, in_maps, core_ids=list(range(NCORES)))
    out = np.empty((n, b, d), np.float32)
    for c in range(NCORES):
        oc = res.results[c]["out"].reshape(b_core, n, d).transpose(1, 0, 2)
        out[:, c * b_core:(c + 1) * b_core, :] = oc
    return out


# version 49
# speedup vs baseline: 1.0250x; 1.0250x over previous
import contextlib
import sys

sys.path.insert(0, "/opt/trn_rl_repo")

import numpy as np
import ml_dtypes
import concourse.bass as bass
import concourse.mybir as mybir
from concourse.bass_utils import run_bass_kernel_spmd

F32 = mybir.dt.float32
BF16 = mybir.dt.bfloat16
AF = mybir.ActivationFunctionType
ALU = mybir.AluOpType
AX = mybir.AxisListType

P = 128
D = 1024
CH = D // P
NSEQ = 16
HPAIRS = 8

WSPECS = {
    "wqkv": (3072, 1024, 512),
    "wproj": (1024, 1024, 1024),
    "w1": (4096, 1024, 1024),
    "w2": (4096, 1024, 1024),
    "h1": (4096, 4096, 256),
    "h2": (4096, 4096, 256),
    "w3": (1024, 4096, 256),
}

ENGS = ("sync", "gpsimd", "scalar", "vector", "tensor")


class Sem:
    def __init__(self, es, nc, name):
        self.h = es.enter_context(nc.semaphore(name))
        self.n = 0

    def inc(self, k=1):
        self.n += k
        h = self.h
        return lambda instr: instr.then_inc(h, k)


class Emit:
    def __init__(self, block):
        self.block = block
        self.q = {e: [] for e in ENGS}

    def add(self, eng, *steps):
        self.q[eng].extend(s for s in steps if s is not None)

    def flush(self):
        for eng in ENGS:
            steps = self.q[eng]
            if steps:
                ss = tuple(steps)

                def body(e, ss=ss):
                    for f in ss:
                        f(e)
                getattr(self.block, eng)(body)
                self.q[eng] = []


def build_nc(b_core=256, T=512):
    TOK = b_core * NSEQ
    NT = TOK // T
    G = T // P
    NCH = G * HPAIRS
    assert T % P == 0 and TOK % T == 0 and T <= 512

    nc = bass.Bass("TRN2", target_bir_lowering=False, debug=False)
    es = contextlib.ExitStack()

    x_in = nc.dram_tensor("x", [TOK, D], F32, kind="ExternalInput").ap()
    xf_in = nc.dram_tensor("xf", [D, TOK], F32, kind="ExternalInput").ap()
    out_d = nc.dram_tensor("out", [D, TOK], F32, kind="ExternalOutput").ap()
    w_in, w_sc = {}, {}
    for name, (O, K, OU) in WSPECS.items():
        w_in[name] = nc.dram_tensor(name, [O // OU, P, (K // P) * OU], F32, kind="ExternalInput").ap()
        w_sc[name] = nc.dram_tensor(name + "_bf", [O // OU, P, (K // P) * OU], BF16, kind="Internal").ap()
    VECN = {"bqkv": 24, "bproj": 8, "w1s": 32, "w2s": 32, "h1s": 32, "h2s": 32, "w3s": 8}
    vecs_in = {k: nc.dram_tensor(k, [P, n], F32, kind="ExternalInput").ap() for k, n in VECN.items()}

    Tn = lambda *a: es.enter_context(nc.sbuf_tensor(*a))
    xt = Tn("xt", [P, 2, D], F32)
    xf = Tn("xf_sb", [P, CH, T], F32)
    xn_tm = Tn("xn_tm", [P, 4, D], BF16)
    xn_f = Tn("xn_f", [P, CH, T], BF16)
    Ar = Tn("Ar", [P, 32, T], BF16)
    Br = Tn("Br", [P, 32, T], BF16)
    Ao = Tn("Ao", [P, CH, T], BF16)
    Vtm = Tn("Vtm", [P, 4, 1024], BF16)
    sum_f = Tn("sum_f", [P, CH, T], F32)
    wbuf = Tn("wbuf", [P, 3, 8192], BF16)
    t1 = Tn("t1", [P, 4, 512], BF16)
    vs_b = {k: Tn("v_" + k, [P, n], F32) for k, n in VECN.items()}
    stats6 = Tn("stats6", [P, 2, 12], F32)
    mv = Tn("mv", [P, 2, 2], F32)
    rstd = Tn("rstdb", [P, 2, 1], F32)
    lnt = Tn("lnt", [P, 2, 1], F32)
    epsv = Tn("epsv", [P, 1], F32)
    id_bf = Tn("id_bf", [P, P], BF16)
    m16 = Tn("m16", [16, 16], F32)
    mask = Tn("mask", [P, P], F32)
    sm = Tn("smb", [P, 3, 2, P], F32)
    ee = Tn("eeb", [P, 3, 2, P], F32)
    an = Tn("anb", [P, 3, 2, P], BF16)
    at_sb = Tn("at_sb", [P, 5, 2, P], BF16)
    vt_sb = Tn("vt_sb", [P, 2, P], BF16)
    nmax = Tn("nmax", [P, 3, 2], F32)
    rsum = Tn("rsum", [P, 3, 2], F32)
    rrec = Tn("rrec", [P, 3, 2], F32)

    PSF = es.enter_context(nc.psum_tensor("PSF", [P, 7, 512], F32))
    PSB = es.enter_context(nc.psum_tensor("PSB", [P, 1, 1024], BF16))

    with nc.Block() as block, es:
        ts = Sem(es, nc, "ts")
        vs = Sem(es, nc, "vs")
        as_ = Sem(es, nc, "as_")
        ps = Sem(es, nc, "ps")
        ws6 = [Sem(es, nc, "ws%d" % i) for i in range(6)]
        xlA, xlB = Sem(es, nc, "xlA"), Sem(es, nc, "xlB")
        osA, osB = Sem(es, nc, "osA"), Sem(es, nc, "osB")
        xoA, xoB, xoC, xoD = (Sem(es, nc, "xoA"), Sem(es, nc, "xoB"),
                              Sem(es, nc, "xoC"), Sem(es, nc, "xoD"))
        wbA, wbB, wbC = Sem(es, nc, "wbA"), Sem(es, nc, "wbB"), Sem(es, nc, "wbC")
        wcA, wcB, wcC = Sem(es, nc, "wcA"), Sem(es, nc, "wcB"), Sem(es, nc, "wcC")
        lv = Sem(es, nc, "lv")
        mk = Sem(es, nc, "mk")
        wbs = [wbA, wbB, wbC]
        wsc = [wcA, wcB, wcC]
        xls = [xlA, xlB]
        oss = [osA, osB]
        xos = [xoA, xoB, xoC, xoD]

        E = Emit(block)

        def W(sem, val):
            return lambda e: e.wait_ge(sem.h, val)

        bank_mark = {b: None for b in range(7)}
        psb_mark = {b: None for b in range(1)}
        mark = {}

        def WB(b, table=None):
            m = (table if table is not None else bank_mark)[b]
            return None if m is None else W(m[0], m[1])

        def WM(key):
            m = mark.get(key)
            return None if m is None else W(m[0], m[1])

        steps = []
        for k in vecs_in:
            a = lv.inc(16)
            steps.append(lambda g, a=a, k=k: a(g.dma_start(out=vs_b[k][:, :], in_=vecs_in[k])))
        for tile, val in ((id_bf, 0.0), (epsv, 1e-5), (m16, 0.0), (mask, -1e30)):
            a = ps.inc()
            steps.append(lambda g, a=a, tile=tile, val=val: a(g.memset(tile[:, :], val)))
        steps.append(W(ps, ps.n))
        for tile, cmp_, fill, pat in (
                (id_bf, ALU.not_equal, 1.0, [[-1, P]]),
                (m16, ALU.is_ge, -1e30, [[-4, 4], [0, 4]])):
            a = ps.inc()
            steps.append(lambda g, a=a, tile=tile, cmp_=cmp_, fill=fill, pat=pat: a(
                g.affine_select(out=tile[:, :], in_=tile[:, :], compare_op=cmp_, fill=fill,
                                base=0, pattern=pat, channel_multiplier=1)))
        steps.append(W(ps, ps.n))
        for j in range(8):
            a = mk.inc(16)
            steps.append(lambda g, a=a, j=j: a(
                g.dma_start(out=mask[16 * j:16 * (j + 1), 16 * j:16 * (j + 1)], in_=m16[:, :])))
        E.add("gpsimd", *steps)
        setup_ps, setup_lv, setup_mk = ps.n, lv.n, mk.n

        E.flush()

        NU = sum(O // OU for (O, K, OU) in WSPECS.values())
        iu_ctr = {"i": 0}
        unit_consumed = {}
        wb_val = {}
        last_wb = {0: 0, 1: 0, 2: 0}

        def stream_one(name, j):
            O, K, OU = WSPECS[name]
            iu = iu_ctr["i"]
            slot = iu % 3
            sz = (K // P) * OU
            wbw = (W(wbs[slot], last_wb[slot])
                   if (iu >= 3 and iu - 3 < NU) else None)
            if iu < NU:
                a = wsc[slot].inc(16)
                E.add("gpsimd",
                      W(ts, unit_consumed[iu - 3]) if iu >= 3 else None,
                      wbw,
                      lambda e, a=a, name=name, j=j, slot=slot, sz=sz: a(
                          e.dma_start(out=wbuf[:, slot, 0:sz], in_=w_in[name][j, :, :])))
                wv = (wsc[slot], wsc[slot].n)
                aw = wbs[slot].inc(16)
                E.add("sync",
                      W(*wv),
                      lambda g, aw=aw, name=name, j=j, slot=slot, sz=sz: aw(
                          g.dma_start(out=w_sc[name][j, :, :], in_=wbuf[:, slot, 0:sz])))
                last_wb[slot] = wbs[slot].n
                wb_val[(name, j)] = (wbs[slot], wbs[slot].n)
            else:
                sem = ws6[iu % 6]
                a = sem.inc(16)
                E.add("sync" if iu % 2 == 0 else "gpsimd",
                      W(ts, unit_consumed[iu - 3]),
                      wbw,
                      W(*wb_val[(name, j)]) if iu < 2 * NU else None,
                      lambda e, a=a, name=name, j=j, slot=slot, sz=sz: a(
                          e.dma_start(out=wbuf[:, slot, 0:sz], in_=w_sc[name][j, :, :])))
                wv = (sem, sem.n)
            iu_ctr["i"] += 1
            return iu, slot, wv, n_oc_of(OU), K, OU

        def n_oc_of(OU):
            return OU // P

        def lhsT_of(slot, kc, jj, OU):
            return wbuf[:, slot, kc * OU + jj * P: kc * OU + (jj + 1) * P]

        def MM(attach, out, lhsT, rhs, start, stop, tile_position=None):
            return lambda e: attach(e.matmul(out=out, lhsT=lhsT, rhs=rhs, start=start,
                                             stop=stop, tile_position=tile_position))

        def MM_noinc(out, lhsT, rhs, start, stop, tile_position=None):
            return lambda e: e.matmul(out=out, lhsT=lhsT, rhs=rhs, start=start,
                                      stop=stop, tile_position=tile_position)

        def mm_group(slot, jj, K, OU, out, rhs_tile, waits, kc_waits=None):
            steps = list(waits)
            nk = K // P
            for kc in range(nk):
                if kc_waits is not None and kc_waits[kc] is not None:
                    steps.append(kc_waits[kc])
                a = ts.inc() if kc == nk - 1 else None
                steps.append(MM(a, out, lhsT_of(slot, kc, jj, OU), rhs_tile[kc],
                                kc == 0, kc == nk - 1) if a else
                             MM_noinc(out, lhsT_of(slot, kc, jj, OU), rhs_tile[kc],
                                      kc == 0, kc == nk - 1))
            E.add("tensor", *steps)
            return ts.n

        def phase_L_pre_group(t, g):
                slot = (t * G + g) % 2
                a = xls[slot].inc(16)
                E.add("gpsimd", WM(("xt", slot)),
                      lambda e, a=a, t=t, g=g, slot=slot: a(e.dma_start(
                          out=xt[:, slot, :],
                          in_=x_in[(t * T + g * P):(t * T + (g + 1) * P), :])))
                xl_val = xls[slot].n

                a1, a2 = vs.inc(), vs.inc()
                wmid = W(vs, vs.n)
                a3 = vs.inc()
                E.add("vector",
                      W(xls[slot], xl_val), WM(("stats6", slot)),
                      lambda v, a=a1, slot=slot: a(v.bn_stats(out=stats6[:, slot, 0:6],
                                                              in_=xt[:, slot, 0:512])),
                      lambda v, a=a2, slot=slot: a(v.bn_stats(out=stats6[:, slot, 6:12],
                                                              in_=xt[:, slot, 512:1024])),
                      wmid, WM(("mv", slot)),
                      lambda v, a=a3, slot=slot: a(v.bn_aggr(out=mv[:, slot, :],
                                                             in_=stats6[:, slot, :])))
                mark[("stats6", slot)] = (vs, vs.n)
                mv_done = vs.n

                a1 = as_.inc()
                wmid = W(as_, as_.n)
                a2 = as_.inc()
                E.add("scalar",
                      W(vs, mv_done), W(ps, setup_ps), WM(("rstd", slot)),
                      lambda s, a=a1, slot=slot: a(s.activation(
                          out=lnt[:, slot, :], in_=mv[:, slot, 1:2], func=AF.Ln, bias=epsv[:, :])),
                      wmid,
                      lambda s, a=a2, slot=slot: a(s.activation(
                          out=rstd[:, slot, :], in_=lnt[:, slot, :], func=AF.Exp, scale=-0.5)))
                rstd_done = as_.n

                xslot = g % 4
                a1 = vs.inc()
                E.add("vector",
                      W(as_, rstd_done), WM(("xn_tm", xslot)),
                      lambda v, a=a1, slot=slot, xslot=xslot: a(v.tensor_scalar(
                          out=xn_tm[:, xslot, :], in0=xt[:, slot, :], scalar1=mv[:, slot, 0:1],
                          scalar2=rstd[:, slot, 0:1], op0=ALU.subtract, op1=ALU.mult)))
                mark[("xt", slot)] = (vs, vs.n)
                mark[("mv", slot)] = (vs, vs.n)
                mark[("rstd", slot)] = (vs, vs.n)
                return vs.n

        def phase_L_pre(t):
            return [phase_L_pre_group(t, g) for g in range(G)]

        def phase_L_tr_group(t, applies, g):
                xslot = g % 4
                apply_done = applies[g]
                pb = 0
                steps = [W(vs, apply_done), W(ps, setup_ps), WB(pb, psb_mark)]
                for j in range(CH):
                    a = ts.inc() if j == CH - 1 else None
                    steps.append(lambda e, a=a, j=j, pb=pb, xslot=xslot: (
                        a(e.transpose(out=PSB[:, pb, j * P:(j + 1) * P],
                                      in_=xn_tm[:, xslot, j * P:(j + 1) * P],
                                      identity=id_bf[:, :])) if a else
                        e.transpose(out=PSB[:, pb, j * P:(j + 1) * P],
                                    in_=xn_tm[:, xslot, j * P:(j + 1) * P],
                                    identity=id_bf[:, :])))
                E.add("tensor", *steps)
                mark[("xn_tm", xslot)] = (ts, ts.n)
                tr_done = ts.n

                a1 = vs.inc()
                E.add("vector",
                      W(ts, tr_done), WM("xn_f"),
                      lambda v, a=a1, g=g, pb=pb: a(v.tensor_copy(
                          out=xn_f[:, :, g * P:(g + 1) * P],
                          in_=PSB[:, pb, :].rearrange("p (c t) -> p c t", c=CH))))
                psb_mark[pb] = (vs, vs.n)
                return vs.n

        def phase_L_tr(t, applies):
            for g in range(G):
                phase_L_tr_group(t, applies, g)
            return vs.n

        def phase_Q(t, ln_all):
            for j in range(4):
                iu, slot, wval, n_oc, K, OU = stream_one("wqkv", j)
                for jj in range(n_oc):
                    oc = j * n_oc + jj
                    b = oc % 6
                    waits = [W(*wval)] if jj == 0 else []
                    if j == 0 and jj == 0:
                        waits.append(W(vs, ln_all))
                    waits.append(WB(b))
                    mm_group(slot, jj, K, OU, PSF[:, b, 0:T],
                             [xn_f[:, kc, 0:T] for kc in range(K // P)], waits)
                unit_consumed[iu] = ts.n
                mm_done = ts.n

                steps = [W(ts, mm_done)]
                if j == 0:
                    steps += [W(lv, setup_lv), WM("Br")]
                for jj in range(n_oc):
                    oc = j * n_oc + jj
                    b = oc % 6
                    a = vs.inc()
                    if oc < 8:
                        steps.append(lambda v, a=a, oc=oc, b=b: a(v.tensor_scalar(
                            out=Br[:, 16 + oc, 0:T], in0=PSF[:, b, 0:T],
                            scalar1=vs_b["bqkv"][:, oc:oc + 1], scalar2=0.125,
                            op0=ALU.add, op1=ALU.mult)))
                    else:
                        steps.append(lambda v, a=a, oc=oc, b=b: a(v.tensor_scalar(
                            out=Br[:, 16 + oc, 0:T], in0=PSF[:, b, 0:T],
                            scalar1=vs_b["bqkv"][:, oc:oc + 1], scalar2=None, op0=ALU.add)))
                    bank_mark[b] = (vs, vs.n)
                E.add("vector", *steps)
            for j in range(4, 6):
                iu, slot, wval, n_oc, K, OU = stream_one("wqkv", j)
                vh = j - 4
                for g in range(G):
                    b = (g % 2)
                    waits = [W(*wval)] if g == 0 else []
                    waits.append(WB(b))
                    if j == 4 and g == 0:
                        waits.append(WM("Vtm"))
                    nk = K // P
                    steps = list(w for w in waits if w is not None)
                    for kc in range(nk):
                        aa = ts.inc() if kc == nk - 1 else None
                        lh = xn_f[:, kc, g * P:(g + 1) * P]
                        rh = wbuf[:, slot, kc * OU: (kc + 1) * OU]
                        if aa:
                            steps.append(lambda e, aa=aa, lh=lh, rh=rh, b=b, kc=kc, nk=nk: aa(
                                e.matmul(out=PSF[:, b, 0:512], lhsT=lh, rhs=rh,
                                         start=(kc == 0), stop=(kc == nk - 1))))
                        else:
                            steps.append(lambda e, lh=lh, rh=rh, b=b, kc=kc, nk=nk:
                                e.matmul(out=PSF[:, b, 0:512], lhsT=lh, rhs=rh,
                                         start=(kc == 0), stop=(kc == nk - 1)))
                    E.add("tensor", *steps)
                    mmv = ts.n
                    a = vs.inc()
                    E.add("vector", W(ts, mmv),
                          lambda v, a=a, g=g, vh=vh, b=b: a(v.tensor_copy(
                              out=Vtm[:, g, vh * 512:(vh + 1) * 512], in_=PSF[:, b, 0:512])))
                    bank_mark[b] = (vs, vs.n)
                unit_consumed[iu] = ts.n
            return vs.n

        def plan_scores(c, qkv_done):
            g, hp = divmod(c, HPAIRS)
            tok = slice(g * P, (g + 1) * P)
            waits = []
            if c == 0:
                waits += [W(vs, qkv_done), W(mk, setup_mk)]
            B0 = 2 + 2 * (c % 2)
            waits.append(WB(B0))
            waits.append(WB(B0 + 1))
            a1, a2 = ts.inc(), ts.inc()
            E.add("tensor", *waits,
                  MM(a1, PSF[:, B0, 0:128], Br[0:64, 16 + hp, tok],
                     Br[0:64, 24 + hp, tok], True, True),
                  MM(a2, PSF[:, B0 + 1, 0:128], Br[64:128, 16 + hp, tok],
                     Br[64:128, 24 + hp, tok], True, True))
            return ts.n

        def plan_smmax(c, sc_done):
            pr = c % 3
            B0 = 2 + 2 * (c % 2)
            for h in range(2):
                a1 = vs.inc()
                wmid = W(vs, vs.n)
                a2 = vs.inc()
                E.add("vector",
                      W(ts, sc_done - (1 - h)), WM(("sm", pr, h)),
                      lambda v, a=a1, pr=pr, h=h, b=B0 + h: a(v.tensor_tensor(
                          out=sm[:, pr, h, :], in0=PSF[:, b, 0:128],
                          in1=mask[:, :], op=ALU.add)),
                      wmid, WM(("nmax", pr, h)),
                      lambda v, a=a2, pr=pr, h=h: a(v.reduce_max(
                          out=nmax[:, pr, h:h + 1], in_=sm[:, pr, h, :], axis=AX.X, negate=True)))
                bank_mark[B0 + h] = (vs, vs.n - 1)
            sm_done = vs.n
            for h in range(2):
                a3 = as_.inc()
                E.add("scalar",
                      W(vs, sm_done) if h == 0 else None,
                      WM(("ee", pr, h)), WM(("rsum", pr, h)),
                      lambda s, a=a3, pr=pr, h=h: a(s.activation(
                          out=ee[:, pr, h, :], in_=sm[:, pr, h, :], func=AF.Exp,
                          bias=nmax[:, pr, h:h + 1], accum_out=rsum[:, pr, h:h + 1])))
                mark[("sm", pr, h)] = (as_, as_.n)
                mark[("nmax", pr, h)] = (as_, as_.n)
            return as_.n

        def plan_norm(c, exp_done):
            pr = c % 3
            for h in range(2):
                a4 = vs.inc()
                wmid = W(vs, vs.n)
                a5 = vs.inc()
                E.add("vector",
                      W(as_, exp_done) if h == 0 else None,
                      WM(("rrec", pr, h)),
                      lambda v, a=a4, pr=pr, h=h: a(v.reciprocal(
                          out=rrec[:, pr, h:h + 1], in_=rsum[:, pr, h:h + 1])),
                      wmid, WM(("an", pr, h)),
                      lambda v, a=a5, pr=pr, h=h: a(v.tensor_scalar(
                          out=an[:, pr, h, :], in0=ee[:, pr, h, :],
                          scalar1=rrec[:, pr, h:h + 1], scalar2=None, op0=ALU.mult)))
                mark[("ee", pr, h)] = (vs, vs.n)
                mark[("rsum", pr, h)] = (vs, vs.n)
                mark[("rrec", pr, h)] = (vs, vs.n)
            return vs.n

        def plan_tr3(c, norm_done):
            pr = c % 3
            off = (c % 2) * 2 * P
            a3 = ts.inc()
            E.add("tensor",
                  W(vs, norm_done), WB(0, psb_mark),
                  lambda e, pr=pr, off=off: e.transpose(
                      out=PSB[:, 0, off:off + P], in_=an[:, pr, 0, :], identity=id_bf[:, :]),
                  lambda e, a=a3, pr=pr, off=off: a(e.transpose(
                      out=PSB[:, 0, off + P:off + 2 * P], in_=an[:, pr, 1, :],
                      identity=id_bf[:, :])))
            mark[("an", pr, 0)] = (ts, ts.n)
            mark[("an", pr, 1)] = (ts, ts.n)
            return ts.n

        def plan_trevac(c, tr3_done):
            pr = c % 5
            off = (c % 2) * 2 * P
            a1 = vs.inc()
            E.add("vector",
                  W(ts, tr3_done), WM(("at", pr)),
                  lambda v, a=a1, pr=pr, off=off: a(v.tensor_copy(
                      out=at_sb[:, pr, :, :],
                      in_=PSB[:, 0, off:off + 2 * P].rearrange("p (h t) -> p h t", h=2))))
            psb_mark[0] = (vs, vs.n)
            return vs.n

        def plan_av(c, ev_done):
            pr = c % 5
            g, hp = divmod(c, HPAIRS)
            BA = 6
            a2 = ts.inc()
            E.add("tensor",
                  W(vs, ev_done), WB(BA),
                  MM_noinc(PSF[0:64, BA, 0:128], Vtm[:, g, hp * 128: hp * 128 + 64],
                           at_sb[:, pr, 0, :], True, True, (0, 0)),
                  MM(a2, PSF[64:128, BA, 0:128], Vtm[:, g, hp * 128 + 64: hp * 128 + 128],
                     at_sb[:, pr, 1, :], True, True, (0, 64)))
            mark[("at", pr)] = (ts, ts.n)
            return ts.n

        def plan_avevac(c, av_done):
            g, hp = divmod(c, HPAIRS)
            a1 = vs.inc()
            E.add("vector",
                  W(ts, av_done), WM("attnout") if c == 0 else None,
                  lambda v, a=a1, g=g, hp=hp, BA=6: a(v.tensor_scalar(
                      out=Ao[:, hp, g * P:(g + 1) * P], in0=PSF[:, BA, 0:128],
                      scalar1=vs_b["bqkv"][:, 16 + hp:17 + hp], scalar2=None,
                      op0=ALU.add)))
            bank_mark[6] = (vs, vs.n)
            return vs.n

        def phase_F1(t):
            NOC = 32
            f1_state = {}
            cur = {}
            for k in range(2 * NOC):
                a_ss, w_ss = k // 8, k % 8
                if w_ss == 0 and a_ss % 2 == 0:
                    cur["u1"] = stream_one("w1", a_ss // 2)
                    cur["u2"] = stream_one("w2", a_ss // 2)
                (iu1, slot1, wval1, n_oc, K, OU) = cur["u1"]
                (iu2, slot2, wval2, _, _, _) = cur["u2"]
                half = 4 * (a_ss % 2)

                if w_ss < 4:
                    oc = a_ss * 4 + w_ss
                    b = oc % 4
                    waits = [W(*wval1)] if (w_ss == 0 and half == 0) else []
                    waits.append(WB(b))
                    mmv = mm_group(slot1, half + w_ss, K, OU, PSF[:, b, 0:T],
                                   [xn_f[:, kc, 0:T] for kc in range(K // P)], waits)
                    f1_state[("u1", oc)] = mmv
                    if w_ss == 3:
                        if half == 4:
                            unit_consumed[iu1] = ts.n
                        steps = []
                        for q in range(4):
                            occ = a_ss * 4 + q
                            a1 = as_.inc()
                            steps += [W(ts, f1_state[("u1", occ)]),
                                      WM(("t1", occ % 4)),
                                      lambda s, a=a1, occ=occ, bq=occ % 4: a(s.activation(
                                          out=t1[:, bq, 0:T], in_=PSF[:, bq, 0:T],
                                          func=AF.Silu, scale=vs_b["w1s"][:, occ:occ + 1]))]
                            f1_state[("silu", occ)] = as_.n
                            bank_mark[occ % 4] = (as_, as_.n)
                        E.add("scalar", *steps)
                else:
                    oc = a_ss * 4 + (w_ss - 4)
                    b = oc % 4
                    waits = [W(*wval2)] if (w_ss == 4 and half == 0) else []
                    waits.append(W(as_, f1_state[("silu", oc)]))
                    waits.append(WB(b))
                    mmv = mm_group(slot2, half + w_ss - 4, K, OU, PSF[:, b, 0:T],
                                   [xn_f[:, kc, 0:T] for kc in range(K // P)], waits)
                    if w_ss == 7 and half == 4:
                        unit_consumed[iu2] = ts.n
                    a2 = vs.inc()
                    E.add("vector",
                          W(ts, mmv), W(as_, f1_state[("silu", oc)]),
                          lambda v, a=a2, oc=oc, b=b: a(v.scalar_tensor_tensor(
                              out=Ar[:, oc, 0:T], in0=PSF[:, b, 0:T],
                              scalar=vs_b["w2s"][:, oc:oc + 1], in1=t1[:, oc % 4, 0:T],
                              op0=ALU.mult, op1=ALU.mult)))
                    bank_mark[b] = (vs, vs.n)
                    mark[("t1", oc % 4)] = (vs, vs.n)
            return vs.n, ts.n

        def phase_H1A(t, qkv_done, u_done, proj_mm_prev, h2_mm_prev):
            applies_next = []
            gelu_marks = []
            """h1 (Br = gelu(h1 @ Ar), 32 groups, banks 0-3) with the NCH
            attention chains woven in (banks 4-5, DVE/ACT otherwise idle
            here). q/k live in Br[16:32] and are guarded per chunk against
            h1's gelu evacs; all chains finish by group ~19."""
            mark["attnout"] = (ts, proj_mm_prev)
            O, K, OU = WSPECS["h1"]
            sc_d, exp_d, norm_d, tr3_d, ev_d, av_d = {}, {}, {}, {}, {}, {}
            cur = [None]
            for gi in range(32):
                jj = gi % 2
                if jj == 0:
                    cur[0] = stream_one("h1", gi // 2)
                iu, slot, wval, n_oc, K, OU = cur[0]

                prc = [c for c in (2 * (gi - 2), 2 * (gi - 2) + 1) if 0 <= c < NCH]
                for c in prc:
                    tr3_d[c] = plan_tr3(c, norm_d[c])
                for c in prc:
                    ev_d[c] = plan_trevac(c, tr3_d[prc[-1]])
                for c in (2 * (gi - 1), 2 * (gi - 1) + 1):
                    if 0 <= c < NCH:
                        norm_d[c] = plan_norm(c, exp_d[c])
                for c in (2 * gi, 2 * gi + 1):
                    if c < NCH:
                        sc_d[c] = plan_scores(c, qkv_done)
                        exp_d[c] = plan_smmax(c, sc_d[c])
                c_odd = 2 * (gi - 4) + 1
                if 0 <= c_odd < NCH:
                    av_d[c_odd] = plan_av(c_odd, ev_d[c_odd])
                    plan_avevac(c_odd, av_d[c_odd])

                if gi in (20, 23, 26, 29) and t + 1 < NT:
                    applies_next.append(phase_L_pre_group(t + 1, (gi - 20) // 3))

                oc = gi
                b = oc % 2
                waits = [W(*wval)] if jj == 0 else []
                if gi == 0:
                    waits.append(W(vs, u_done))
                waits.append(WB(b))
                mmv = mm_group(slot, jj, K, OU, PSF[:, b, 0:T],
                               [Ar[:, kc, 0:T] for kc in range(K // P)], waits)
                if jj == n_oc - 1:
                    unit_consumed[iu] = ts.n
                c_ev = 2 * (gi - 3)
                if 0 <= c_ev < NCH:
                    av_d[c_ev] = plan_av(c_ev, ev_d[c_ev])
                    plan_avevac(c_ev, av_d[c_ev])
                steps = [W(ts, mmv)]
                if gi == 0:
                    steps.append(W(ts, h2_mm_prev))
                if oc >= 16:
                    steps.append(W(ts, sc_d[24 + (oc - 16) % 8]))
                a = as_.inc()
                steps.append(lambda s, a=a, oc=oc, b=b: a(
                    s.activation(out=Br[:, oc, 0:T], in_=PSF[:, b, 0:T], func=AF.Gelu,
                                 scale=vs_b["h1s"][:, oc:oc + 1])))
                E.add("scalar", *steps)
                bank_mark[b] = (as_, as_.n)
                gelu_marks.append((as_, as_.n))
            for c in range(NCH):
                if c not in av_d:
                    av_d[c] = plan_av(c, ev_d[c])
                    plan_avevac(c, av_d[c])
            return vs.n, ts.n, as_.n, applies_next, gelu_marks

        def phase_P(t, attn_done, fillers=()):
            fillers = list(fillers)
            for j in range(WSPECS["wproj"][0] // WSPECS["wproj"][2]):
                iu, slot, wval, n_oc, K, OU = stream_one("wproj", j)
                for jj in range(n_oc):
                    oc = j * n_oc + jj
                    b = oc % 6
                    waits = [W(*wval)] if jj == 0 else []
                    if j == 0 and jj == 0:
                        waits.append(W(vs, attn_done))
                    waits.append(WB(b))
                    mmv = mm_group(slot, jj, K, OU, PSF[:, b, 0:T],
                                   [Ao[:, kc, 0:T] for kc in range(K // P)], waits)
                    if fillers:
                        fillers.pop(0)()
                    steps = [W(ts, mmv)]
                    if j == 0 and jj == 0:
                        steps.append(WM("sum"))
                    a = vs.inc()
                    steps.append(lambda v, a=a, oc=oc, b=b: a(v.tensor_scalar(
                        out=sum_f[:, oc, :], in0=PSF[:, b, 0:T],
                        scalar1=vs_b["bproj"][:, oc:oc + 1], scalar2=None, op0=ALU.add)))
                    bank_mark[b] = (vs, vs.n)
                    E.add("vector", *steps)
                unit_consumed[iu] = ts.n
            return ts.n

        def phase_H(t, wname, sname, src, dst, src_ready, dst_free,
                    src_chunk_marks=None):
            O, K, OU = WSPECS[wname]
            for j in range(O // OU):
                iu, slot, wval, n_oc, K, OU = stream_one(wname, j)
                for jj in range(n_oc):
                    oc = j * n_oc + jj
                    b = oc % 6
                    waits = [W(*wval)] if jj == 0 else []
                    kcw = None
                    if j == 0 and jj == 0:
                        if src_chunk_marks is not None:
                            kcw = [W(src_chunk_marks[kc][0], src_chunk_marks[kc][1])
                                   for kc in range(K // P)]
                        else:
                            waits.append(W(src_ready[0], src_ready[1]))
                    waits.append(WB(b))
                    mmv = mm_group(slot, jj, K, OU, PSF[:, b, 0:T],
                                   [src[:, kc, 0:T] for kc in range(K // P)], waits,
                                   kc_waits=kcw)
                    steps = [W(ts, mmv)]
                    if j == 0 and jj == 0:
                        steps.append(W(dst_free[0], dst_free[1]))
                    a = as_.inc()
                    steps.append(lambda s, a=a, oc=oc, b=b, sname=sname, dst=dst: a(
                        s.activation(out=dst[:, oc, 0:T], in_=PSF[:, b, 0:T], func=AF.Gelu,
                                     scale=vs_b[sname][:, oc:oc + 1])))
                    E.add("scalar", *steps)
                    bank_mark[b] = (as_, as_.n)
                unit_consumed[iu] = ts.n
            return ts.n, as_.n

        def phase_W3(t, src_ready):
            for j in range(1024 // 256):
                iu, slot, wval, n_oc, K, OU = stream_one("w3", j)
                for jj in range(n_oc):
                    oc = j * n_oc + jj
                    b = oc % 6
                    waits = [W(*wval)] if jj == 0 else []
                    if j == 0 and jj == 0:
                        waits.append(W(src_ready[0], src_ready[1]))
                    waits.append(WB(b))
                    mmv = mm_group(slot, jj, K, OU, PSF[:, b, 0:T],
                                   [Ar[:, kc, 0:T] for kc in range(K // P)], waits)
                    a = vs.inc()
                    E.add("vector", W(ts, mmv),
                          lambda v, a=a, oc=oc, b=b: a(v.scalar_tensor_tensor(
                              out=sum_f[:, oc, :], in0=PSF[:, b, 0:T],
                              scalar=vs_b["w3s"][:, oc:oc + 1], in1=sum_f[:, oc, :],
                              op0=ALU.mult, op1=ALU.add)))
                    bank_mark[b] = (vs, vs.n)
                unit_consumed[iu] = ts.n
            return ts.n, vs.n

        def plan_xf_load(t):
            a = xos[t % 2].inc(16)
            E.add("scalar", WM("xf"),
                  lambda e, a=a, t=t: a(e.dma_start(
                      out=xf[:, :, :],
                      in_=xf_in[:, t * T:(t + 1) * T].rearrange(
                          "(c p) t -> p c t", p=P))))
            return (xos[t % 2], xos[t % 2].n)

        def phase_O(t, sum_done, xf_val):
            steps = [W(vs, sum_done), W(*xf_val)]
            for oc in range(CH):
                a = vs.inc()
                steps.append(lambda v, a=a, oc=oc: a(v.tensor_tensor(
                    out=xf[:, oc, :], in0=sum_f[:, oc, :], in1=xf[:, oc, :],
                    op=ALU.add)))
            E.add("vector", *steps)
            adds_done = vs.n
            mark["sum"] = (vs, adds_done)

            oslot = t % 2
            a = oss[oslot].inc(16)
            E.add("scalar",
                  W(vs, adds_done),
                  lambda e, a=a, t=t: a(e.dma_start(
                      out=out_d[:, t * T:(t + 1) * T].rearrange(
                          "(c p) t -> p c t", p=P),
                      in_=xf[:, :, :])))
            mark["xf"] = (oss[oslot], oss[oslot].n)

        proj_mm_prev = 0
        h2_mm_prev = 0
        ln_next = phase_L_tr(0, phase_L_pre(0))
        for t in range(NT):
            ln_all = ln_next
            qkv_done = phase_Q(t, ln_all)
            u_done, f1_mm_done = phase_F1(t)
            mark["xn_f"] = (ts, f1_mm_done)
            attn_done, h1_mm, h1_ev, applies_next, gelu_marks = phase_H1A(
                t, qkv_done, u_done, proj_mm_prev, h2_mm_prev)
            if t + 1 < NT:
                fil = [lambda g=g: phase_L_tr_group(t + 1, applies_next, g)
                       for g in range(G)]
            else:
                fil = []
            proj_mm_prev = phase_P(t, attn_done, fil)
            if t + 1 < NT:
                ln_next = vs.n
            xf_val = plan_xf_load(t)
            h2_mm, h2_ev = phase_H(t, "h2", "h2s", Br, Ar, (as_, h1_ev), (ts, h1_mm),
                                   src_chunk_marks=gelu_marks)
            h2_mm_prev = h2_mm
            mark["Br"] = (ts, h2_mm)
            w3_mm_prev, sum_done = phase_W3(t, (as_, h2_ev))
            phase_O(t, sum_done, xf_val)
            E.flush()

        a_fin = None
        E.add("scalar", W(osA, osA.n), W(osB, osB.n))
        E.flush()

    return nc


def _layout_weight(w, OU):
    O, K = w.shape
    return np.ascontiguousarray(
        w.reshape(O // OU, OU, K // P, P).transpose(0, 3, 2, 1).reshape(O // OU, P, (K // P) * OU))


def _layout_vec(v):
    n = v.shape[0] // P
    return np.ascontiguousarray(v.reshape(n, P).T)


def kernel(L0_attn_x, L0_attn_Wqkv, L0_attn_bqkv, L0_attn_Wproj, L0_attn_bproj,
           L0_ffn_w1_w, L0_ffn_w1_s, L0_ffn_w2_w, L0_ffn_w2_s, L0_ffn_h1_w,
           L0_ffn_h1_s, L0_ffn_h2_w, L0_ffn_h2_s, L0_ffn_w3_w, L0_ffn_w3_s):
    NCORES = 8
    x = np.asarray(L0_attn_x)
    n, b, d = x.shape
    b_core = b // NCORES
    nc = build_nc(b_core=b_core, T=512)

    wmats = {
        "wqkv": np.asarray(L0_attn_Wqkv), "wproj": np.asarray(L0_attn_Wproj),
        "w1": np.asarray(L0_ffn_w1_w), "w2": np.asarray(L0_ffn_w2_w),
        "h1": np.asarray(L0_ffn_h1_w), "h2": np.asarray(L0_ffn_h2_w),
        "w3": np.asarray(L0_ffn_w3_w),
    }
    wblocked = {k: _layout_weight(w, WSPECS[k][2]) for k, w in wmats.items()}
    vecs = {
        "bqkv": _layout_vec(np.asarray(L0_attn_bqkv)),
        "bproj": _layout_vec(np.asarray(L0_attn_bproj)),
        "w1s": _layout_vec(np.asarray(L0_ffn_w1_s)),
        "w2s": _layout_vec(np.asarray(L0_ffn_w2_s)),
        "h1s": _layout_vec(np.asarray(L0_ffn_h1_s)),
        "h2s": _layout_vec(np.asarray(L0_ffn_h2_s)),
        "w3s": _layout_vec(np.asarray(L0_ffn_w3_s)),
    }
    in_maps = []
    for c in range(NCORES):
        xc = np.ascontiguousarray(
            x[:, c * b_core:(c + 1) * b_core, :].transpose(1, 0, 2).reshape(b_core * n, d))
        m = {"x": xc, "xf": np.ascontiguousarray(xc.T)}
        m.update(wblocked)
        m.update(vecs)
        in_maps.append(m)

    res = run_bass_kernel_spmd(nc, in_maps, core_ids=list(range(NCORES)))
    out = np.empty((n, b, d), np.float32)
    for c in range(NCORES):
        oc = np.ascontiguousarray(res.results[c]["out"].T).reshape(
            b_core, n, d).transpose(1, 0, 2)
        out[:, c * b_core:(c + 1) * b_core, :] = oc
    return out


# version 50
# speedup vs baseline: 1.0251x; 1.0001x over previous
import contextlib
import sys

sys.path.insert(0, "/opt/trn_rl_repo")

import numpy as np
import ml_dtypes
import concourse.bass as bass
import concourse.mybir as mybir
from concourse.bass_utils import run_bass_kernel_spmd

F32 = mybir.dt.float32
BF16 = mybir.dt.bfloat16
AF = mybir.ActivationFunctionType
ALU = mybir.AluOpType
AX = mybir.AxisListType

P = 128
D = 1024
CH = D // P
NSEQ = 16
HPAIRS = 8

WSPECS = {
    "wqkv": (3072, 1024, 512),
    "wproj": (1024, 1024, 1024),
    "w1": (4096, 1024, 1024),
    "w2": (4096, 1024, 1024),
    "h1": (4096, 4096, 256),
    "h2": (4096, 4096, 256),
    "w3": (1024, 4096, 256),
}

ENGS = ("sync", "gpsimd", "scalar", "vector", "tensor")


class Sem:
    def __init__(self, es, nc, name):
        self.h = es.enter_context(nc.semaphore(name))
        self.n = 0

    def inc(self, k=1):
        self.n += k
        h = self.h
        return lambda instr: instr.then_inc(h, k)


class Emit:
    def __init__(self, block):
        self.block = block
        self.q = {e: [] for e in ENGS}

    def add(self, eng, *steps):
        self.q[eng].extend(s for s in steps if s is not None)

    def flush(self):
        for eng in ENGS:
            steps = self.q[eng]
            if steps:
                ss = tuple(steps)

                def body(e, ss=ss):
                    for f in ss:
                        f(e)
                getattr(self.block, eng)(body)
                self.q[eng] = []


def build_nc(b_core=256, T=512):
    TOK = b_core * NSEQ
    NT = TOK // T
    G = T // P
    NCH = G * HPAIRS
    assert T % P == 0 and TOK % T == 0 and T <= 512

    nc = bass.Bass("TRN2", target_bir_lowering=False, debug=False)
    es = contextlib.ExitStack()

    x_in = nc.dram_tensor("x", [TOK, D], F32, kind="ExternalInput").ap()
    xf_in = nc.dram_tensor("xf", [D, TOK], F32, kind="ExternalInput").ap()
    out_d = nc.dram_tensor("out", [D, TOK], F32, kind="ExternalOutput").ap()
    w_in, w_sc = {}, {}
    for name, (O, K, OU) in WSPECS.items():
        w_in[name] = nc.dram_tensor(name, [O // OU, P, (K // P) * OU], F32, kind="ExternalInput").ap()
        w_sc[name] = nc.dram_tensor(name + "_bf", [O // OU, P, (K // P) * OU], BF16, kind="Internal").ap()
    VECN = {"bqkv": 24, "bproj": 8, "w1s": 32, "w2s": 32, "h1s": 32, "h2s": 32, "w3s": 8}
    vecs_in = {k: nc.dram_tensor(k, [P, n], F32, kind="ExternalInput").ap() for k, n in VECN.items()}

    Tn = lambda *a: es.enter_context(nc.sbuf_tensor(*a))
    xt = Tn("xt", [P, 2, D], F32)
    xf = Tn("xf_sb", [P, CH, T], F32)
    xn_tm = Tn("xn_tm", [P, 4, D], BF16)
    xn_f = Tn("xn_f", [P, CH, T], BF16)
    Ar = Tn("Ar", [P, 32, T], BF16)
    Br = Tn("Br", [P, 32, T], BF16)
    Ao = Tn("Ao", [P, CH, T], BF16)
    Vtm = Tn("Vtm", [P, 4, 1024], BF16)
    sum_f = Tn("sum_f", [P, CH, T], F32)
    wbuf = Tn("wbuf", [P, 3, 8192], BF16)
    t1 = Tn("t1", [P, 4, 512], BF16)
    vs_b = {k: Tn("v_" + k, [P, n], F32) for k, n in VECN.items()}
    stats6 = Tn("stats6", [P, 2, 12], F32)
    mv = Tn("mv", [P, 2, 2], F32)
    rstd = Tn("rstdb", [P, 2, 1], F32)
    lnt = Tn("lnt", [P, 2, 1], F32)
    epsv = Tn("epsv", [P, 1], F32)
    id_bf = Tn("id_bf", [P, P], BF16)
    m16 = Tn("m16", [16, 16], F32)
    mask = Tn("mask", [P, P], F32)
    sm = Tn("smb", [P, 3, 2, P], F32)
    ee = Tn("eeb", [P, 3, 2, P], F32)
    an = Tn("anb", [P, 3, 2, P], BF16)
    at_sb = Tn("at_sb", [P, 5, 2, P], BF16)
    vt_sb = Tn("vt_sb", [P, 2, P], BF16)
    nmax = Tn("nmax", [P, 3, 2], F32)
    rsum = Tn("rsum", [P, 3, 2], F32)
    rrec = Tn("rrec", [P, 3, 2], F32)

    PSF = es.enter_context(nc.psum_tensor("PSF", [P, 7, 512], F32))
    PSB = es.enter_context(nc.psum_tensor("PSB", [P, 1, 1024], BF16))

    with nc.Block() as block, es:
        ts = Sem(es, nc, "ts")
        vs = Sem(es, nc, "vs")
        as_ = Sem(es, nc, "as_")
        ps = Sem(es, nc, "ps")
        ws6 = [Sem(es, nc, "ws%d" % i) for i in range(6)]
        xlA, xlB = Sem(es, nc, "xlA"), Sem(es, nc, "xlB")
        osA, osB = Sem(es, nc, "osA"), Sem(es, nc, "osB")
        xoA, xoB, xoC, xoD = (Sem(es, nc, "xoA"), Sem(es, nc, "xoB"),
                              Sem(es, nc, "xoC"), Sem(es, nc, "xoD"))
        wbA, wbB, wbC = Sem(es, nc, "wbA"), Sem(es, nc, "wbB"), Sem(es, nc, "wbC")
        wcA, wcB, wcC = Sem(es, nc, "wcA"), Sem(es, nc, "wcB"), Sem(es, nc, "wcC")
        lv = Sem(es, nc, "lv")
        mk = Sem(es, nc, "mk")
        wbs = [wbA, wbB, wbC]
        wsc = [wcA, wcB, wcC]
        xls = [xlA, xlB]
        oss = [osA, osB]
        xos = [xoA, xoB, xoC, xoD]

        E = Emit(block)

        def W(sem, val):
            return lambda e: e.wait_ge(sem.h, val)

        bank_mark = {b: None for b in range(7)}
        psb_mark = {b: None for b in range(1)}
        mark = {}

        def WB(b, table=None):
            m = (table if table is not None else bank_mark)[b]
            return None if m is None else W(m[0], m[1])

        def WM(key):
            m = mark.get(key)
            return None if m is None else W(m[0], m[1])

        steps = []
        for k in vecs_in:
            a = lv.inc(16)
            steps.append(lambda g, a=a, k=k: a(g.dma_start(out=vs_b[k][:, :], in_=vecs_in[k])))
        for tile, val in ((id_bf, 0.0), (epsv, 1e-5), (m16, 0.0), (mask, -1e30)):
            a = ps.inc()
            steps.append(lambda g, a=a, tile=tile, val=val: a(g.memset(tile[:, :], val)))
        steps.append(W(ps, ps.n))
        for tile, cmp_, fill, pat in (
                (id_bf, ALU.not_equal, 1.0, [[-1, P]]),
                (m16, ALU.is_ge, -1e30, [[-4, 4], [0, 4]])):
            a = ps.inc()
            steps.append(lambda g, a=a, tile=tile, cmp_=cmp_, fill=fill, pat=pat: a(
                g.affine_select(out=tile[:, :], in_=tile[:, :], compare_op=cmp_, fill=fill,
                                base=0, pattern=pat, channel_multiplier=1)))
        steps.append(W(ps, ps.n))
        for j in range(8):
            a = mk.inc(16)
            steps.append(lambda g, a=a, j=j: a(
                g.dma_start(out=mask[16 * j:16 * (j + 1), 16 * j:16 * (j + 1)], in_=m16[:, :])))
        E.add("gpsimd", *steps)
        setup_ps, setup_lv, setup_mk = ps.n, lv.n, mk.n

        E.flush()

        NU = sum(O // OU for (O, K, OU) in WSPECS.values())
        iu_ctr = {"i": 0}
        unit_consumed = {}
        wb_val = {}
        last_wb = {0: 0, 1: 0, 2: 0}

        def stream_one(name, j):
            O, K, OU = WSPECS[name]
            iu = iu_ctr["i"]
            slot = iu % 3
            sz = (K // P) * OU
            wbw = (W(wbs[slot], last_wb[slot])
                   if (iu >= 3 and iu - 3 < NU) else None)
            if iu < NU:
                a = wsc[slot].inc(16)
                E.add("gpsimd",
                      W(ts, unit_consumed[iu - 3]) if iu >= 3 else None,
                      wbw,
                      lambda e, a=a, name=name, j=j, slot=slot, sz=sz: a(
                          e.dma_start(out=wbuf[:, slot, 0:sz], in_=w_in[name][j, :, :])))
                wv = (wsc[slot], wsc[slot].n)
                aw = wbs[slot].inc(16)
                E.add("sync",
                      W(*wv),
                      lambda g, aw=aw, name=name, j=j, slot=slot, sz=sz: aw(
                          g.dma_start(out=w_sc[name][j, :, :], in_=wbuf[:, slot, 0:sz])))
                last_wb[slot] = wbs[slot].n
                wb_val[(name, j)] = (wbs[slot], wbs[slot].n)
            else:
                sem = ws6[iu % 6]
                a = sem.inc(16)
                E.add("sync" if iu % 2 == 0 else "gpsimd",
                      W(ts, unit_consumed[iu - 3]),
                      wbw,
                      W(*wb_val[(name, j)]) if iu < 2 * NU else None,
                      lambda e, a=a, name=name, j=j, slot=slot, sz=sz: a(
                          e.dma_start(out=wbuf[:, slot, 0:sz], in_=w_sc[name][j, :, :])))
                wv = (sem, sem.n)
            iu_ctr["i"] += 1
            return iu, slot, wv, n_oc_of(OU), K, OU

        def n_oc_of(OU):
            return OU // P

        def lhsT_of(slot, kc, jj, OU):
            return wbuf[:, slot, kc * OU + jj * P: kc * OU + (jj + 1) * P]

        def MM(attach, out, lhsT, rhs, start, stop, tile_position=None):
            return lambda e: attach(e.matmul(out=out, lhsT=lhsT, rhs=rhs, start=start,
                                             stop=stop, tile_position=tile_position))

        def MM_noinc(out, lhsT, rhs, start, stop, tile_position=None):
            return lambda e: e.matmul(out=out, lhsT=lhsT, rhs=rhs, start=start,
                                      stop=stop, tile_position=tile_position)

        def mm_group(slot, jj, K, OU, out, rhs_tile, waits, kc_waits=None):
            steps = list(waits)
            nk = K // P
            for kc in range(nk):
                if kc_waits is not None and kc_waits[kc] is not None:
                    steps.append(kc_waits[kc])
                a = ts.inc() if kc == nk - 1 else None
                steps.append(MM(a, out, lhsT_of(slot, kc, jj, OU), rhs_tile[kc],
                                kc == 0, kc == nk - 1) if a else
                             MM_noinc(out, lhsT_of(slot, kc, jj, OU), rhs_tile[kc],
                                      kc == 0, kc == nk - 1))
            E.add("tensor", *steps)
            return ts.n

        def phase_L_pre_group(t, g):
                slot = (t * G + g) % 2
                a = xls[slot].inc(16)
                E.add("gpsimd", WM(("xt", slot)),
                      lambda e, a=a, t=t, g=g, slot=slot: a(e.dma_start(
                          out=xt[:, slot, :],
                          in_=x_in[(t * T + g * P):(t * T + (g + 1) * P), :])))
                xl_val = xls[slot].n

                a1, a2 = vs.inc(), vs.inc()
                wmid = W(vs, vs.n)
                a3 = vs.inc()
                E.add("vector",
                      W(xls[slot], xl_val), WM(("stats6", slot)),
                      lambda v, a=a1, slot=slot: a(v.bn_stats(out=stats6[:, slot, 0:6],
                                                              in_=xt[:, slot, 0:512])),
                      lambda v, a=a2, slot=slot: a(v.bn_stats(out=stats6[:, slot, 6:12],
                                                              in_=xt[:, slot, 512:1024])),
                      wmid, WM(("mv", slot)),
                      lambda v, a=a3, slot=slot: a(v.bn_aggr(out=mv[:, slot, :],
                                                             in_=stats6[:, slot, :])))
                mark[("stats6", slot)] = (vs, vs.n)
                mv_done = vs.n

                a1 = as_.inc()
                wmid = W(as_, as_.n)
                a2 = as_.inc()
                E.add("scalar",
                      W(vs, mv_done), W(ps, setup_ps), WM(("rstd", slot)),
                      lambda s, a=a1, slot=slot: a(s.activation(
                          out=lnt[:, slot, :], in_=mv[:, slot, 1:2], func=AF.Ln, bias=epsv[:, :])),
                      wmid,
                      lambda s, a=a2, slot=slot: a(s.activation(
                          out=rstd[:, slot, :], in_=lnt[:, slot, :], func=AF.Exp, scale=-0.5)))
                rstd_done = as_.n

                xslot = g % 4
                a1 = vs.inc()
                E.add("vector",
                      W(as_, rstd_done), WM(("xn_tm", xslot)),
                      lambda v, a=a1, slot=slot, xslot=xslot: a(v.tensor_scalar(
                          out=xn_tm[:, xslot, :], in0=xt[:, slot, :], scalar1=mv[:, slot, 0:1],
                          scalar2=rstd[:, slot, 0:1], op0=ALU.subtract, op1=ALU.mult)))
                mark[("xt", slot)] = (vs, vs.n)
                mark[("mv", slot)] = (vs, vs.n)
                mark[("rstd", slot)] = (vs, vs.n)
                return vs.n

        def phase_L_pre(t):
            return [phase_L_pre_group(t, g) for g in range(G)]

        def phase_L_tr_group(t, applies, g):
                xslot = g % 4
                apply_done = applies[g]
                pb = 0
                steps = [W(vs, apply_done), W(ps, setup_ps), WB(pb, psb_mark)]
                for j in range(CH):
                    a = ts.inc() if j == CH - 1 else None
                    steps.append(lambda e, a=a, j=j, pb=pb, xslot=xslot: (
                        a(e.transpose(out=PSB[:, pb, j * P:(j + 1) * P],
                                      in_=xn_tm[:, xslot, j * P:(j + 1) * P],
                                      identity=id_bf[:, :])) if a else
                        e.transpose(out=PSB[:, pb, j * P:(j + 1) * P],
                                    in_=xn_tm[:, xslot, j * P:(j + 1) * P],
                                    identity=id_bf[:, :])))
                E.add("tensor", *steps)
                mark[("xn_tm", xslot)] = (ts, ts.n)
                tr_done = ts.n

                a1 = vs.inc()
                E.add("vector",
                      W(ts, tr_done), WM("xn_f"),
                      lambda v, a=a1, g=g, pb=pb: a(v.tensor_copy(
                          out=xn_f[:, :, g * P:(g + 1) * P],
                          in_=PSB[:, pb, :].rearrange("p (c t) -> p c t", c=CH))))
                psb_mark[pb] = (vs, vs.n)
                return vs.n

        def phase_L_tr(t, applies):
            for g in range(G):
                phase_L_tr_group(t, applies, g)
            return vs.n

        def phase_Q(t, ln_all):
            for j in range(4):
                iu, slot, wval, n_oc, K, OU = stream_one("wqkv", j)
                for jj in range(n_oc):
                    oc = j * n_oc + jj
                    b = oc % 6
                    waits = [W(*wval)] if jj == 0 else []
                    if j == 0 and jj == 0:
                        waits.append(W(vs, ln_all))
                    waits.append(WB(b))
                    mm_group(slot, jj, K, OU, PSF[:, b, 0:T],
                             [xn_f[:, kc, 0:T] for kc in range(K // P)], waits)
                unit_consumed[iu] = ts.n
                mm_done = ts.n

                steps = [W(ts, mm_done)]
                if j == 0:
                    steps += [W(lv, setup_lv), WM("Br")]
                for jj in range(n_oc):
                    oc = j * n_oc + jj
                    b = oc % 6
                    a = vs.inc()
                    if oc < 8:
                        steps.append(lambda v, a=a, oc=oc, b=b: a(v.tensor_scalar(
                            out=Br[:, 16 + oc, 0:T], in0=PSF[:, b, 0:T],
                            scalar1=vs_b["bqkv"][:, oc:oc + 1], scalar2=0.125,
                            op0=ALU.add, op1=ALU.mult)))
                    else:
                        steps.append(lambda v, a=a, oc=oc, b=b: a(v.tensor_scalar(
                            out=Br[:, 16 + oc, 0:T], in0=PSF[:, b, 0:T],
                            scalar1=vs_b["bqkv"][:, oc:oc + 1], scalar2=None, op0=ALU.add)))
                    bank_mark[b] = (vs, vs.n)
                E.add("vector", *steps)
            for j in range(4, 6):
                iu, slot, wval, n_oc, K, OU = stream_one("wqkv", j)
                vh = j - 4
                for g in range(G):
                    b = (g % 2)
                    waits = [W(*wval)] if g == 0 else []
                    waits.append(WB(b))
                    if j == 4 and g == 0:
                        waits.append(WM("Vtm"))
                    nk = K // P
                    steps = list(w for w in waits if w is not None)
                    for kc in range(nk):
                        aa = ts.inc() if kc == nk - 1 else None
                        lh = xn_f[:, kc, g * P:(g + 1) * P]
                        rh = wbuf[:, slot, kc * OU: (kc + 1) * OU]
                        if aa:
                            steps.append(lambda e, aa=aa, lh=lh, rh=rh, b=b, kc=kc, nk=nk: aa(
                                e.matmul(out=PSF[:, b, 0:512], lhsT=lh, rhs=rh,
                                         start=(kc == 0), stop=(kc == nk - 1))))
                        else:
                            steps.append(lambda e, lh=lh, rh=rh, b=b, kc=kc, nk=nk:
                                e.matmul(out=PSF[:, b, 0:512], lhsT=lh, rhs=rh,
                                         start=(kc == 0), stop=(kc == nk - 1)))
                    E.add("tensor", *steps)
                    mmv = ts.n
                    a = vs.inc()
                    E.add("vector", W(ts, mmv),
                          lambda v, a=a, g=g, vh=vh, b=b: a(v.tensor_copy(
                              out=Vtm[:, g, vh * 512:(vh + 1) * 512], in_=PSF[:, b, 0:512])))
                    bank_mark[b] = (vs, vs.n)
                unit_consumed[iu] = ts.n
            return vs.n

        def plan_scores(c, qkv_done):
            g, hp = divmod(c, HPAIRS)
            tok = slice(g * P, (g + 1) * P)
            waits = []
            if c == 0:
                waits += [W(vs, qkv_done), W(mk, setup_mk)]
            B0 = 2 + 2 * (c % 2)
            waits.append(WB(B0))
            waits.append(WB(B0 + 1))
            a1, a2 = ts.inc(), ts.inc()
            E.add("tensor", *waits,
                  MM(a1, PSF[:, B0, 0:128], Br[0:64, 16 + hp, tok],
                     Br[0:64, 24 + hp, tok], True, True),
                  MM(a2, PSF[:, B0 + 1, 0:128], Br[64:128, 16 + hp, tok],
                     Br[64:128, 24 + hp, tok], True, True))
            return ts.n

        def plan_smmax(c, sc_done):
            pr = c % 3
            B0 = 2 + 2 * (c % 2)
            for h in range(2):
                a1 = vs.inc()
                wmid = W(vs, vs.n)
                a2 = vs.inc()
                E.add("vector",
                      W(ts, sc_done - (1 - h)), WM(("sm", pr, h)),
                      lambda v, a=a1, pr=pr, h=h, b=B0 + h: a(v.tensor_tensor(
                          out=sm[:, pr, h, :], in0=PSF[:, b, 0:128],
                          in1=mask[:, :], op=ALU.add)),
                      wmid, WM(("nmax", pr, h)),
                      lambda v, a=a2, pr=pr, h=h: a(v.reduce_max(
                          out=nmax[:, pr, h:h + 1], in_=sm[:, pr, h, :], axis=AX.X, negate=True)))
                bank_mark[B0 + h] = (vs, vs.n - 1)
            sm_done = vs.n
            for h in range(2):
                a3 = as_.inc()
                E.add("scalar",
                      W(vs, sm_done) if h == 0 else None,
                      WM(("ee", pr, h)), WM(("rsum", pr, h)),
                      lambda s, a=a3, pr=pr, h=h: a(s.activation(
                          out=ee[:, pr, h, :], in_=sm[:, pr, h, :], func=AF.Exp,
                          bias=nmax[:, pr, h:h + 1], accum_out=rsum[:, pr, h:h + 1])))
                mark[("sm", pr, h)] = (as_, as_.n)
                mark[("nmax", pr, h)] = (as_, as_.n)
            return as_.n

        def plan_norm(c, exp_done):
            pr = c % 3
            for h in range(2):
                a4 = vs.inc()
                wmid = W(vs, vs.n)
                a5 = vs.inc()
                E.add("vector",
                      W(as_, exp_done) if h == 0 else None,
                      WM(("rrec", pr, h)),
                      lambda v, a=a4, pr=pr, h=h: a(v.reciprocal(
                          out=rrec[:, pr, h:h + 1], in_=rsum[:, pr, h:h + 1])),
                      wmid, WM(("an", pr, h)),
                      lambda v, a=a5, pr=pr, h=h: a(v.tensor_scalar(
                          out=an[:, pr, h, :], in0=ee[:, pr, h, :],
                          scalar1=rrec[:, pr, h:h + 1], scalar2=None, op0=ALU.mult)))
                mark[("ee", pr, h)] = (vs, vs.n)
                mark[("rsum", pr, h)] = (vs, vs.n)
                mark[("rrec", pr, h)] = (vs, vs.n)
            return vs.n

        def plan_tr3(c, norm_done):
            pr = c % 3
            off = (c % 2) * 2 * P
            a3 = ts.inc()
            E.add("tensor",
                  W(vs, norm_done), WB(0, psb_mark),
                  lambda e, pr=pr, off=off: e.transpose(
                      out=PSB[:, 0, off:off + P], in_=an[:, pr, 0, :], identity=id_bf[:, :]),
                  lambda e, a=a3, pr=pr, off=off: a(e.transpose(
                      out=PSB[:, 0, off + P:off + 2 * P], in_=an[:, pr, 1, :],
                      identity=id_bf[:, :])))
            mark[("an", pr, 0)] = (ts, ts.n)
            mark[("an", pr, 1)] = (ts, ts.n)
            return ts.n

        def plan_trevac(c, tr3_done):
            pr = c % 5
            off = (c % 2) * 2 * P
            a1 = vs.inc()
            E.add("vector",
                  W(ts, tr3_done), WM(("at", pr)),
                  lambda v, a=a1, pr=pr, off=off: a(v.tensor_copy(
                      out=at_sb[:, pr, :, :],
                      in_=PSB[:, 0, off:off + 2 * P].rearrange("p (h t) -> p h t", h=2))))
            psb_mark[0] = (vs, vs.n)
            return vs.n

        def plan_av(c, ev_done):
            pr = c % 5
            g, hp = divmod(c, HPAIRS)
            BA = 6
            a2 = ts.inc()
            E.add("tensor",
                  W(vs, ev_done), WB(BA),
                  MM_noinc(PSF[0:64, BA, 0:128], Vtm[:, g, hp * 128: hp * 128 + 64],
                           at_sb[:, pr, 0, :], True, True, (0, 0)),
                  MM(a2, PSF[64:128, BA, 0:128], Vtm[:, g, hp * 128 + 64: hp * 128 + 128],
                     at_sb[:, pr, 1, :], True, True, (0, 64)))
            mark[("at", pr)] = (ts, ts.n)
            return ts.n

        def plan_avevac(c, av_done):
            g, hp = divmod(c, HPAIRS)
            a1 = vs.inc()
            E.add("vector",
                  W(ts, av_done), WM("attnout") if c == 0 else None,
                  lambda v, a=a1, g=g, hp=hp, BA=6: a(v.tensor_scalar(
                      out=Ao[:, hp, g * P:(g + 1) * P], in0=PSF[:, BA, 0:128],
                      scalar1=vs_b["bqkv"][:, 16 + hp:17 + hp], scalar2=None,
                      op0=ALU.add)))
            bank_mark[6] = (vs, vs.n)
            return vs.n

        def phase_F1(t):
            NOC = 32
            f1_state = {}
            cur = {}
            for k in range(2 * NOC):
                a_ss, w_ss = k // 8, k % 8
                if w_ss == 0 and a_ss % 2 == 0:
                    cur["u1"] = stream_one("w1", a_ss // 2)
                    cur["u2"] = stream_one("w2", a_ss // 2)
                (iu1, slot1, wval1, n_oc, K, OU) = cur["u1"]
                (iu2, slot2, wval2, _, _, _) = cur["u2"]
                half = 4 * (a_ss % 2)

                if w_ss < 4:
                    oc = a_ss * 4 + w_ss
                    b = (oc + 2) % 4
                    waits = [W(*wval1)] if (w_ss == 0 and half == 0) else []
                    waits.append(WB(b))
                    mmv = mm_group(slot1, half + w_ss, K, OU, PSF[:, b, 0:T],
                                   [xn_f[:, kc, 0:T] for kc in range(K // P)], waits)
                    f1_state[("u1", oc)] = mmv
                    if w_ss == 3:
                        if half == 4:
                            unit_consumed[iu1] = ts.n
                        steps = []
                        for q in range(4):
                            occ = a_ss * 4 + q
                            a1 = as_.inc()
                            steps += [W(ts, f1_state[("u1", occ)]),
                                      WM(("t1", occ % 4)),
                                      lambda s, a=a1, occ=occ, bq=(occ + 2) % 4: a(
                                          s.activation(
                                              out=t1[:, occ % 4, 0:T], in_=PSF[:, bq, 0:T],
                                              func=AF.Silu,
                                              scale=vs_b["w1s"][:, occ:occ + 1]))]
                            f1_state[("silu", occ)] = as_.n
                            bank_mark[(occ + 2) % 4] = (as_, as_.n)
                        E.add("scalar", *steps)
                else:
                    oc = a_ss * 4 + (w_ss - 4)
                    b = (oc + 2) % 4
                    waits = [W(*wval2)] if (w_ss == 4 and half == 0) else []
                    waits.append(W(as_, f1_state[("silu", oc)]))
                    waits.append(WB(b))
                    mmv = mm_group(slot2, half + w_ss - 4, K, OU, PSF[:, b, 0:T],
                                   [xn_f[:, kc, 0:T] for kc in range(K // P)], waits)
                    if w_ss == 7 and half == 4:
                        unit_consumed[iu2] = ts.n
                    a2 = vs.inc()
                    E.add("vector",
                          W(ts, mmv), W(as_, f1_state[("silu", oc)]),
                          lambda v, a=a2, oc=oc, b=b: a(v.scalar_tensor_tensor(
                              out=Ar[:, oc, 0:T], in0=PSF[:, b, 0:T],
                              scalar=vs_b["w2s"][:, oc:oc + 1], in1=t1[:, oc % 4, 0:T],
                              op0=ALU.mult, op1=ALU.mult)))
                    bank_mark[b] = (vs, vs.n)
                    mark[("t1", oc % 4)] = (vs, vs.n)
            return vs.n, ts.n

        def phase_H1A(t, qkv_done, u_done, proj_mm_prev, h2_mm_prev):
            applies_next = []
            gelu_marks = []
            """h1 (Br = gelu(h1 @ Ar), 32 groups, banks 0-3) with the NCH
            attention chains woven in (banks 4-5, DVE/ACT otherwise idle
            here). q/k live in Br[16:32] and are guarded per chunk against
            h1's gelu evacs; all chains finish by group ~19."""
            mark["attnout"] = (ts, proj_mm_prev)
            O, K, OU = WSPECS["h1"]
            sc_d, exp_d, norm_d, tr3_d, ev_d, av_d = {}, {}, {}, {}, {}, {}
            cur = [None]
            for gi in range(32):
                jj = gi % 2
                if jj == 0:
                    cur[0] = stream_one("h1", gi // 2)
                iu, slot, wval, n_oc, K, OU = cur[0]

                prc = [c for c in (2 * (gi - 2), 2 * (gi - 2) + 1) if 0 <= c < NCH]
                for c in prc:
                    tr3_d[c] = plan_tr3(c, norm_d[c])
                for c in prc:
                    ev_d[c] = plan_trevac(c, tr3_d[prc[-1]])
                for c in (2 * (gi - 1), 2 * (gi - 1) + 1):
                    if 0 <= c < NCH:
                        norm_d[c] = plan_norm(c, exp_d[c])
                for c in (2 * gi, 2 * gi + 1):
                    if c < NCH:
                        sc_d[c] = plan_scores(c, qkv_done)
                        exp_d[c] = plan_smmax(c, sc_d[c])
                c_odd = 2 * (gi - 4) + 1
                if 0 <= c_odd < NCH:
                    av_d[c_odd] = plan_av(c_odd, ev_d[c_odd])
                    plan_avevac(c_odd, av_d[c_odd])

                if gi in (20, 23, 26, 29) and t + 1 < NT:
                    applies_next.append(phase_L_pre_group(t + 1, (gi - 20) // 3))

                oc = gi
                b = oc % 2
                waits = [W(*wval)] if jj == 0 else []
                if gi == 0:
                    waits.append(W(vs, u_done))
                waits.append(WB(b))
                mmv = mm_group(slot, jj, K, OU, PSF[:, b, 0:T],
                               [Ar[:, kc, 0:T] for kc in range(K // P)], waits)
                if jj == n_oc - 1:
                    unit_consumed[iu] = ts.n
                c_ev = 2 * (gi - 3)
                if 0 <= c_ev < NCH:
                    av_d[c_ev] = plan_av(c_ev, ev_d[c_ev])
                    plan_avevac(c_ev, av_d[c_ev])
                steps = [W(ts, mmv)]
                if gi == 0:
                    steps.append(W(ts, h2_mm_prev))
                if oc >= 16:
                    steps.append(W(ts, sc_d[24 + (oc - 16) % 8]))
                a = as_.inc()
                steps.append(lambda s, a=a, oc=oc, b=b: a(
                    s.activation(out=Br[:, oc, 0:T], in_=PSF[:, b, 0:T], func=AF.Gelu,
                                 scale=vs_b["h1s"][:, oc:oc + 1])))
                E.add("scalar", *steps)
                bank_mark[b] = (as_, as_.n)
                gelu_marks.append((as_, as_.n))
            for c in range(NCH):
                if c not in av_d:
                    av_d[c] = plan_av(c, ev_d[c])
                    plan_avevac(c, av_d[c])
            return vs.n, ts.n, as_.n, applies_next, gelu_marks

        def phase_P(t, attn_done, fillers=()):
            fillers = list(fillers)
            for j in range(WSPECS["wproj"][0] // WSPECS["wproj"][2]):
                iu, slot, wval, n_oc, K, OU = stream_one("wproj", j)
                for jj in range(n_oc):
                    oc = j * n_oc + jj
                    b = oc % 6
                    waits = [W(*wval)] if jj == 0 else []
                    if j == 0 and jj == 0:
                        waits.append(W(vs, attn_done))
                    waits.append(WB(b))
                    mmv = mm_group(slot, jj, K, OU, PSF[:, b, 0:T],
                                   [Ao[:, kc, 0:T] for kc in range(K // P)], waits)
                    if fillers:
                        fillers.pop(0)()
                    steps = [W(ts, mmv)]
                    if j == 0 and jj == 0:
                        steps.append(WM("sum"))
                    a = vs.inc()
                    steps.append(lambda v, a=a, oc=oc, b=b: a(v.tensor_scalar(
                        out=sum_f[:, oc, :], in0=PSF[:, b, 0:T],
                        scalar1=vs_b["bproj"][:, oc:oc + 1], scalar2=None, op0=ALU.add)))
                    bank_mark[b] = (vs, vs.n)
                    E.add("vector", *steps)
                unit_consumed[iu] = ts.n
            return ts.n

        def phase_H(t, wname, sname, src, dst, src_ready, dst_free,
                    src_chunk_marks=None):
            O, K, OU = WSPECS[wname]
            for j in range(O // OU):
                iu, slot, wval, n_oc, K, OU = stream_one(wname, j)
                for jj in range(n_oc):
                    oc = j * n_oc + jj
                    b = oc % 6
                    waits = [W(*wval)] if jj == 0 else []
                    kcw = None
                    if j == 0 and jj == 0:
                        if src_chunk_marks is not None:
                            kcw = [W(src_chunk_marks[kc][0], src_chunk_marks[kc][1])
                                   for kc in range(K // P)]
                        else:
                            waits.append(W(src_ready[0], src_ready[1]))
                    waits.append(WB(b))
                    mmv = mm_group(slot, jj, K, OU, PSF[:, b, 0:T],
                                   [src[:, kc, 0:T] for kc in range(K // P)], waits,
                                   kc_waits=kcw)
                    steps = [W(ts, mmv)]
                    if j == 0 and jj == 0:
                        steps.append(W(dst_free[0], dst_free[1]))
                    a = as_.inc()
                    steps.append(lambda s, a=a, oc=oc, b=b, sname=sname, dst=dst: a(
                        s.activation(out=dst[:, oc, 0:T], in_=PSF[:, b, 0:T], func=AF.Gelu,
                                     scale=vs_b[sname][:, oc:oc + 1])))
                    E.add("scalar", *steps)
                    bank_mark[b] = (as_, as_.n)
                unit_consumed[iu] = ts.n
            return ts.n, as_.n

        def phase_W3(t, src_ready):
            for j in range(1024 // 256):
                iu, slot, wval, n_oc, K, OU = stream_one("w3", j)
                for jj in range(n_oc):
                    oc = j * n_oc + jj
                    b = oc % 6
                    waits = [W(*wval)] if jj == 0 else []
                    if j == 0 and jj == 0:
                        waits.append(W(src_ready[0], src_ready[1]))
                    waits.append(WB(b))
                    mmv = mm_group(slot, jj, K, OU, PSF[:, b, 0:T],
                                   [Ar[:, kc, 0:T] for kc in range(K // P)], waits)
                    a = vs.inc()
                    E.add("vector", W(ts, mmv),
                          lambda v, a=a, oc=oc, b=b: a(v.scalar_tensor_tensor(
                              out=sum_f[:, oc, :], in0=PSF[:, b, 0:T],
                              scalar=vs_b["w3s"][:, oc:oc + 1], in1=sum_f[:, oc, :],
                              op0=ALU.mult, op1=ALU.add)))
                    bank_mark[b] = (vs, vs.n)
                unit_consumed[iu] = ts.n
            return ts.n, vs.n

        def plan_xf_load(t):
            a = xos[t % 2].inc(16)
            E.add("scalar", WM("xf"),
                  lambda e, a=a, t=t: a(e.dma_start(
                      out=xf[:, :, :],
                      in_=xf_in[:, t * T:(t + 1) * T].rearrange(
                          "(c p) t -> p c t", p=P))))
            return (xos[t % 2], xos[t % 2].n)

        def phase_O(t, sum_done, xf_val):
            steps = [W(vs, sum_done), W(*xf_val)]
            for oc in range(CH):
                a = vs.inc()
                steps.append(lambda v, a=a, oc=oc: a(v.tensor_tensor(
                    out=xf[:, oc, :], in0=sum_f[:, oc, :], in1=xf[:, oc, :],
                    op=ALU.add)))
            E.add("vector", *steps)
            adds_done = vs.n
            mark["sum"] = (vs, adds_done)

            oslot = t % 2
            a = oss[oslot].inc(16)
            E.add("scalar",
                  W(vs, adds_done),
                  lambda e, a=a, t=t: a(e.dma_start(
                      out=out_d[:, t * T:(t + 1) * T].rearrange(
                          "(c p) t -> p c t", p=P),
                      in_=xf[:, :, :])))
            mark["xf"] = (oss[oslot], oss[oslot].n)

        proj_mm_prev = 0
        h2_mm_prev = 0
        ln_next = phase_L_tr(0, phase_L_pre(0))
        for t in range(NT):
            ln_all = ln_next
            qkv_done = phase_Q(t, ln_all)
            u_done, f1_mm_done = phase_F1(t)
            mark["xn_f"] = (ts, f1_mm_done)
            attn_done, h1_mm, h1_ev, applies_next, gelu_marks = phase_H1A(
                t, qkv_done, u_done, proj_mm_prev, h2_mm_prev)
            if t + 1 < NT:
                fil = [lambda g=g: phase_L_tr_group(t + 1, applies_next, g)
                       for g in range(G)]
            else:
                fil = []
            proj_mm_prev = phase_P(t, attn_done, fil)
            if t + 1 < NT:
                ln_next = vs.n
            xf_val = plan_xf_load(t)
            h2_mm, h2_ev = phase_H(t, "h2", "h2s", Br, Ar, (as_, h1_ev), (ts, h1_mm),
                                   src_chunk_marks=gelu_marks)
            h2_mm_prev = h2_mm
            mark["Br"] = (ts, h2_mm)
            w3_mm_prev, sum_done = phase_W3(t, (as_, h2_ev))
            phase_O(t, sum_done, xf_val)
            E.flush()

        a_fin = None
        E.add("scalar", W(osA, osA.n), W(osB, osB.n))
        E.flush()

    return nc


def _layout_weight(w, OU):
    O, K = w.shape
    return np.ascontiguousarray(
        w.reshape(O // OU, OU, K // P, P).transpose(0, 3, 2, 1).reshape(O // OU, P, (K // P) * OU))


def _layout_vec(v):
    n = v.shape[0] // P
    return np.ascontiguousarray(v.reshape(n, P).T)


def kernel(L0_attn_x, L0_attn_Wqkv, L0_attn_bqkv, L0_attn_Wproj, L0_attn_bproj,
           L0_ffn_w1_w, L0_ffn_w1_s, L0_ffn_w2_w, L0_ffn_w2_s, L0_ffn_h1_w,
           L0_ffn_h1_s, L0_ffn_h2_w, L0_ffn_h2_s, L0_ffn_w3_w, L0_ffn_w3_s):
    NCORES = 8
    x = np.asarray(L0_attn_x)
    n, b, d = x.shape
    b_core = b // NCORES
    nc = build_nc(b_core=b_core, T=512)

    wmats = {
        "wqkv": np.asarray(L0_attn_Wqkv), "wproj": np.asarray(L0_attn_Wproj),
        "w1": np.asarray(L0_ffn_w1_w), "w2": np.asarray(L0_ffn_w2_w),
        "h1": np.asarray(L0_ffn_h1_w), "h2": np.asarray(L0_ffn_h2_w),
        "w3": np.asarray(L0_ffn_w3_w),
    }
    wblocked = {k: _layout_weight(w, WSPECS[k][2]) for k, w in wmats.items()}
    vecs = {
        "bqkv": _layout_vec(np.asarray(L0_attn_bqkv)),
        "bproj": _layout_vec(np.asarray(L0_attn_bproj)),
        "w1s": _layout_vec(np.asarray(L0_ffn_w1_s)),
        "w2s": _layout_vec(np.asarray(L0_ffn_w2_s)),
        "h1s": _layout_vec(np.asarray(L0_ffn_h1_s)),
        "h2s": _layout_vec(np.asarray(L0_ffn_h2_s)),
        "w3s": _layout_vec(np.asarray(L0_ffn_w3_s)),
    }
    in_maps = []
    for c in range(NCORES):
        xc = np.ascontiguousarray(
            x[:, c * b_core:(c + 1) * b_core, :].transpose(1, 0, 2).reshape(b_core * n, d))
        m = {"x": xc, "xf": np.ascontiguousarray(xc.T)}
        m.update(wblocked)
        m.update(vecs)
        in_maps.append(m)

    res = run_bass_kernel_spmd(nc, in_maps, core_ids=list(range(NCORES)))
    out = np.empty((n, b, d), np.float32)
    for c in range(NCORES):
        oc = np.ascontiguousarray(res.results[c]["out"].T).reshape(
            b_core, n, d).transpose(1, 0, 2)
        out[:, c * b_core:(c + 1) * b_core, :] = oc
    return out


# version 51
# speedup vs baseline: 1.1106x; 1.0834x over previous
import contextlib
import sys

sys.path.insert(0, "/opt/trn_rl_repo")

import numpy as np
import ml_dtypes
import concourse.bass as bass
import concourse.mybir as mybir
from concourse.bass_utils import run_bass_kernel_spmd

F32 = mybir.dt.float32
BF16 = mybir.dt.bfloat16
AF = mybir.ActivationFunctionType
ALU = mybir.AluOpType
AX = mybir.AxisListType

P = 128
D = 1024
CH = D // P
NSEQ = 16
HPAIRS = 8

WSPECS = {
    "wqkv": (3072, 1024, 512),
    "wproj": (1024, 1024, 1024),
    "w1": (4096, 1024, 1024),
    "w2": (4096, 1024, 1024),
    "h1": (4096, 4096, 256),
    "h2": (4096, 4096, 256),
    "w3": (1024, 4096, 256),
}

ENGS = ("sync", "gpsimd", "scalar", "vector", "tensor")


class Sem:
    def __init__(self, es, nc, name):
        self.h = es.enter_context(nc.semaphore(name))
        self.n = 0

    def inc(self, k=1):
        self.n += k
        h = self.h
        return lambda instr: instr.then_inc(h, k)


class Emit:
    def __init__(self, block):
        self.block = block
        self.q = {e: [] for e in ENGS}

    def add(self, eng, *steps):
        self.q[eng].extend(s for s in steps if s is not None)

    def flush(self):
        for eng in ENGS:
            steps = self.q[eng]
            if steps:
                ss = tuple(steps)

                def body(e, ss=ss):
                    for f in ss:
                        f(e)
                getattr(self.block, eng)(body)
                self.q[eng] = []


def build_nc(b_core=256, T=512):
    TOK = b_core * NSEQ
    NT = TOK // T
    G = T // P
    NCH = G * HPAIRS
    assert T % P == 0 and TOK % T == 0 and T <= 512

    nc = bass.Bass("TRN2", target_bir_lowering=False, debug=False)
    es = contextlib.ExitStack()

    x_in = nc.dram_tensor("x", [TOK, D], F32, kind="ExternalInput").ap()
    xf_in = nc.dram_tensor("xf", [D, TOK], F32, kind="ExternalInput").ap()
    out_d = nc.dram_tensor("out", [D, TOK], F32, kind="ExternalOutput").ap()
    w_in, w_sc = {}, {}
    for name, (O, K, OU) in WSPECS.items():
        w_in[name] = nc.dram_tensor(name, [O // OU, P, (K // P) * OU], F32, kind="ExternalInput").ap()
        w_sc[name] = nc.dram_tensor(name + "_bf", [O // OU, P, (K // P) * OU], BF16, kind="Internal").ap()
    VECN = {"bqkv": 24, "bproj": 8, "w1s": 32, "w2s": 32, "h1s": 32, "h2s": 32, "w3s": 8}
    vecs_in = {k: nc.dram_tensor(k, [P, n], F32, kind="ExternalInput").ap() for k, n in VECN.items()}

    Tn = lambda *a: es.enter_context(nc.sbuf_tensor(*a))
    xt = Tn("xt", [P, 2, D], F32)
    xf = Tn("xf_sb", [P, CH, T], F32)
    xn_tm = Tn("xn_tm", [P, 4, D], BF16)
    xn_f = Tn("xn_f", [P, CH, T], BF16)
    Ar = Tn("Ar", [P, 32, T], BF16)
    Br = Tn("Br", [P, 32, T], BF16)
    Ao = Tn("Ao", [P, CH, T], BF16)
    Vtm = Tn("Vtm", [P, 4, 1024], BF16)
    sum_f = Tn("sum_f", [P, CH, T], F32)
    wbuf = Tn("wbuf", [P, 3, 8192], BF16)
    t1 = Tn("t1", [P, 4, 512], BF16)
    vs_b = {k: Tn("v_" + k, [P, n], F32) for k, n in VECN.items()}
    stats6 = Tn("stats6", [P, 2, 12], F32)
    mv = Tn("mv", [P, 2, 2], F32)
    rstd = Tn("rstdb", [P, 2, 1], F32)
    lnt = Tn("lnt", [P, 2, 1], F32)
    epsv = Tn("epsv", [P, 1], F32)
    id_bf = Tn("id_bf", [P, P], BF16)
    m16 = Tn("m16", [16, 16], F32)
    mask = Tn("mask", [P, P], F32)
    sm = Tn("smb", [P, 3, 2, P], F32)
    ee = Tn("eeb", [P, 3, 2, P], F32)
    an = Tn("anb", [P, 3, 2, P], BF16)
    at_sb = Tn("at_sb", [P, 5, 2, P], BF16)
    vt_sb = Tn("vt_sb", [P, 2, P], BF16)
    nmax = Tn("nmax", [P, 3, 2], F32)
    rsum = Tn("rsum", [P, 3, 2], F32)
    rrec = Tn("rrec", [P, 3, 2], F32)

    PSF = es.enter_context(nc.psum_tensor("PSF", [P, 7, 512], F32))
    PSB = es.enter_context(nc.psum_tensor("PSB", [P, 1, 1024], BF16))

    with nc.Block() as block, es:
        ts = Sem(es, nc, "ts")
        vs = Sem(es, nc, "vs")
        as_ = Sem(es, nc, "as_")
        ps = Sem(es, nc, "ps")
        ws6 = [Sem(es, nc, "ws%d" % i) for i in range(6)]
        xlA, xlB = Sem(es, nc, "xlA"), Sem(es, nc, "xlB")
        osA, osB = Sem(es, nc, "osA"), Sem(es, nc, "osB")
        xoA, xoB, xoC, xoD = (Sem(es, nc, "xoA"), Sem(es, nc, "xoB"),
                              Sem(es, nc, "xoC"), Sem(es, nc, "xoD"))
        wbA, wbB, wbC = Sem(es, nc, "wbA"), Sem(es, nc, "wbB"), Sem(es, nc, "wbC")
        wcA, wcB, wcC = Sem(es, nc, "wcA"), Sem(es, nc, "wcB"), Sem(es, nc, "wcC")
        lv = Sem(es, nc, "lv")
        mk = Sem(es, nc, "mk")
        wbs = [wbA, wbB, wbC]
        wsc = [wcA, wcB, wcC]
        xls = [xlA, xlB]
        oss = [osA, osB]
        xos = [xoA, xoB, xoC, xoD]

        E = Emit(block)

        def W(sem, val):
            return lambda e: e.wait_ge(sem.h, val)

        bank_mark = {b: None for b in range(7)}
        psb_mark = {b: None for b in range(1)}
        mark = {}

        def WB(b, table=None):
            m = (table if table is not None else bank_mark)[b]
            return None if m is None else W(m[0], m[1])

        def WM(key):
            m = mark.get(key)
            return None if m is None else W(m[0], m[1])

        steps = []
        for k in vecs_in:
            a = lv.inc(16)
            steps.append(lambda g, a=a, k=k: a(g.dma_start(out=vs_b[k][:, :], in_=vecs_in[k])))
        for tile, val in ((id_bf, 0.0), (epsv, 1e-5), (m16, 0.0), (mask, -1e30)):
            a = ps.inc()
            steps.append(lambda g, a=a, tile=tile, val=val: a(g.memset(tile[:, :], val)))
        steps.append(W(ps, ps.n))
        for tile, cmp_, fill, pat in (
                (id_bf, ALU.not_equal, 1.0, [[-1, P]]),
                (m16, ALU.is_ge, -1e30, [[-4, 4], [0, 4]])):
            a = ps.inc()
            steps.append(lambda g, a=a, tile=tile, cmp_=cmp_, fill=fill, pat=pat: a(
                g.affine_select(out=tile[:, :], in_=tile[:, :], compare_op=cmp_, fill=fill,
                                base=0, pattern=pat, channel_multiplier=1)))
        steps.append(W(ps, ps.n))
        for j in range(8):
            a = mk.inc(16)
            steps.append(lambda g, a=a, j=j: a(
                g.dma_start(out=mask[16 * j:16 * (j + 1), 16 * j:16 * (j + 1)], in_=m16[:, :])))
        E.add("gpsimd", *steps)
        setup_ps, setup_lv, setup_mk = ps.n, lv.n, mk.n

        E.flush()

        NU = sum(O // OU for (O, K, OU) in WSPECS.values())
        iu_ctr = {"i": 0}
        unit_consumed = {}
        wb_val = {}
        last_wb = {0: 0, 1: 0, 2: 0}

        def stream_one(name, j):
            O, K, OU = WSPECS[name]
            iu = iu_ctr["i"]
            slot = iu % 3
            sz = (K // P) * OU
            wbw = (W(wbs[slot], last_wb[slot])
                   if (iu >= 3 and iu - 3 < NU) else None)
            if iu < NU:
                a = wsc[slot].inc(16)
                E.add("gpsimd",
                      W(ts, unit_consumed[iu - 3]) if iu >= 3 else None,
                      wbw,
                      lambda e, a=a, name=name, j=j, slot=slot, sz=sz: a(
                          e.dma_start(out=wbuf[:, slot, 0:sz], in_=w_in[name][j, :, :])))
                wv = (wsc[slot], wsc[slot].n)
                aw = wbs[slot].inc(16)
                E.add("sync",
                      W(*wv),
                      lambda g, aw=aw, name=name, j=j, slot=slot, sz=sz: aw(
                          g.dma_start(out=w_sc[name][j, :, :], in_=wbuf[:, slot, 0:sz])))
                last_wb[slot] = wbs[slot].n
                wb_val[(name, j)] = (wbs[slot], wbs[slot].n)
            else:
                sem = ws6[iu % 6]
                a = sem.inc(16)
                E.add("sync" if iu % 2 == 0 else "gpsimd",
                      W(ts, unit_consumed[iu - 3]),
                      wbw,
                      W(*wb_val[(name, j)]) if iu < 2 * NU else None,
                      lambda e, a=a, name=name, j=j, slot=slot, sz=sz: a(
                          e.dma_start(out=wbuf[:, slot, 0:sz], in_=w_sc[name][j, :, :])))
                wv = (sem, sem.n)
            iu_ctr["i"] += 1
            return iu, slot, wv, n_oc_of(OU), K, OU

        def n_oc_of(OU):
            return OU // P

        def lhsT_of(slot, kc, jj, OU):
            return wbuf[:, slot, kc * OU + jj * P: kc * OU + (jj + 1) * P]

        def MM(attach, out, lhsT, rhs, start, stop, tile_position=None):
            return lambda e: attach(e.matmul(out=out, lhsT=lhsT, rhs=rhs, start=start,
                                             stop=stop, tile_position=tile_position))

        def MM_noinc(out, lhsT, rhs, start, stop, tile_position=None):
            return lambda e: e.matmul(out=out, lhsT=lhsT, rhs=rhs, start=start,
                                      stop=stop, tile_position=tile_position)

        def mm_group(slot, jj, K, OU, out, rhs_tile, waits, kc_waits=None):
            steps = list(waits)
            nk = K // P
            for kc in range(nk):
                if kc_waits is not None and kc_waits[kc] is not None:
                    steps.append(kc_waits[kc])
                a = ts.inc() if kc == nk - 1 else None
                steps.append(MM(a, out, lhsT_of(slot, kc, jj, OU), rhs_tile[kc],
                                kc == 0, kc == nk - 1) if a else
                             MM_noinc(out, lhsT_of(slot, kc, jj, OU), rhs_tile[kc],
                                      kc == 0, kc == nk - 1))
            E.add("tensor", *steps)
            return ts.n

        def phase_L_pre_group(t, g):
                slot = (t * G + g) % 2
                a = xls[slot].inc(16)
                E.add("scalar", WM(("xt", slot)),
                      lambda e, a=a, t=t, g=g, slot=slot: a(e.dma_start(
                          out=xt[:, slot, :],
                          in_=x_in[(t * T + g * P):(t * T + (g + 1) * P), :])))
                xl_val = xls[slot].n

                a1, a2 = vs.inc(), vs.inc()
                wmid = W(vs, vs.n)
                a3 = vs.inc()
                E.add("vector",
                      W(xls[slot], xl_val), WM(("stats6", slot)),
                      lambda v, a=a1, slot=slot: a(v.bn_stats(out=stats6[:, slot, 0:6],
                                                              in_=xt[:, slot, 0:512])),
                      lambda v, a=a2, slot=slot: a(v.bn_stats(out=stats6[:, slot, 6:12],
                                                              in_=xt[:, slot, 512:1024])),
                      wmid, WM(("mv", slot)),
                      lambda v, a=a3, slot=slot: a(v.bn_aggr(out=mv[:, slot, :],
                                                             in_=stats6[:, slot, :])))
                mark[("stats6", slot)] = (vs, vs.n)
                mv_done = vs.n

                a1 = as_.inc()
                wmid = W(as_, as_.n)
                a2 = as_.inc()
                E.add("scalar",
                      W(vs, mv_done), W(ps, setup_ps), WM(("rstd", slot)),
                      lambda s, a=a1, slot=slot: a(s.activation(
                          out=lnt[:, slot, :], in_=mv[:, slot, 1:2], func=AF.Ln, bias=epsv[:, :])),
                      wmid,
                      lambda s, a=a2, slot=slot: a(s.activation(
                          out=rstd[:, slot, :], in_=lnt[:, slot, :], func=AF.Exp, scale=-0.5)))
                rstd_done = as_.n

                xslot = g % 4
                a1 = vs.inc()
                E.add("vector",
                      W(as_, rstd_done), WM(("xn_tm", xslot)),
                      lambda v, a=a1, slot=slot, xslot=xslot: a(v.tensor_scalar(
                          out=xn_tm[:, xslot, :], in0=xt[:, slot, :], scalar1=mv[:, slot, 0:1],
                          scalar2=rstd[:, slot, 0:1], op0=ALU.subtract, op1=ALU.mult)))
                mark[("xt", slot)] = (vs, vs.n)
                mark[("mv", slot)] = (vs, vs.n)
                mark[("rstd", slot)] = (vs, vs.n)
                return vs.n

        def phase_L_pre(t):
            return [phase_L_pre_group(t, g) for g in range(G)]

        def phase_L_tr_group(t, applies, g):
                xslot = g % 4
                apply_done = applies[g]
                pb = 0
                steps = [W(vs, apply_done), W(ps, setup_ps), WB(pb, psb_mark)]
                for j in range(CH):
                    a = ts.inc() if j == CH - 1 else None
                    steps.append(lambda e, a=a, j=j, pb=pb, xslot=xslot: (
                        a(e.transpose(out=PSB[:, pb, j * P:(j + 1) * P],
                                      in_=xn_tm[:, xslot, j * P:(j + 1) * P],
                                      identity=id_bf[:, :])) if a else
                        e.transpose(out=PSB[:, pb, j * P:(j + 1) * P],
                                    in_=xn_tm[:, xslot, j * P:(j + 1) * P],
                                    identity=id_bf[:, :])))
                E.add("tensor", *steps)
                mark[("xn_tm", xslot)] = (ts, ts.n)
                tr_done = ts.n

                a1 = vs.inc()
                E.add("vector",
                      W(ts, tr_done), WM("xn_f"),
                      lambda v, a=a1, g=g, pb=pb: a(v.tensor_copy(
                          out=xn_f[:, :, g * P:(g + 1) * P],
                          in_=PSB[:, pb, :].rearrange("p (c t) -> p c t", c=CH))))
                psb_mark[pb] = (vs, vs.n)
                return vs.n

        def phase_L_tr(t, applies):
            for g in range(G):
                phase_L_tr_group(t, applies, g)
            return vs.n

        def phase_Q(t, ln_all):
            for j in range(4):
                iu, slot, wval, n_oc, K, OU = stream_one("wqkv", j)
                for jj in range(n_oc):
                    oc = j * n_oc + jj
                    b = oc % 6
                    waits = [W(*wval)] if jj == 0 else []
                    if j == 0 and jj == 0:
                        waits.append(W(vs, ln_all))
                    waits.append(WB(b))
                    mm_group(slot, jj, K, OU, PSF[:, b, 0:T],
                             [xn_f[:, kc, 0:T] for kc in range(K // P)], waits)
                unit_consumed[iu] = ts.n
                mm_done = ts.n

                steps = [W(ts, mm_done)]
                if j == 0:
                    steps += [W(lv, setup_lv), WM("Br")]
                for jj in range(n_oc):
                    oc = j * n_oc + jj
                    b = oc % 6
                    a = vs.inc()
                    if oc < 8:
                        steps.append(lambda v, a=a, oc=oc, b=b: a(v.tensor_scalar(
                            out=Br[:, 16 + oc, 0:T], in0=PSF[:, b, 0:T],
                            scalar1=vs_b["bqkv"][:, oc:oc + 1], scalar2=0.125,
                            op0=ALU.add, op1=ALU.mult)))
                    else:
                        steps.append(lambda v, a=a, oc=oc, b=b: a(v.tensor_scalar(
                            out=Br[:, 16 + oc, 0:T], in0=PSF[:, b, 0:T],
                            scalar1=vs_b["bqkv"][:, oc:oc + 1], scalar2=None, op0=ALU.add)))
                    bank_mark[b] = (vs, vs.n)
                E.add("vector", *steps)
            for j in range(4, 6):
                iu, slot, wval, n_oc, K, OU = stream_one("wqkv", j)
                vh = j - 4
                for g in range(G):
                    b = (g % 2)
                    waits = [W(*wval)] if g == 0 else []
                    waits.append(WB(b))
                    if j == 4 and g == 0:
                        waits.append(WM("Vtm"))
                    nk = K // P
                    steps = list(w for w in waits if w is not None)
                    for kc in range(nk):
                        aa = ts.inc() if kc == nk - 1 else None
                        lh = xn_f[:, kc, g * P:(g + 1) * P]
                        rh = wbuf[:, slot, kc * OU: (kc + 1) * OU]
                        if aa:
                            steps.append(lambda e, aa=aa, lh=lh, rh=rh, b=b, kc=kc, nk=nk: aa(
                                e.matmul(out=PSF[:, b, 0:512], lhsT=lh, rhs=rh,
                                         start=(kc == 0), stop=(kc == nk - 1))))
                        else:
                            steps.append(lambda e, lh=lh, rh=rh, b=b, kc=kc, nk=nk:
                                e.matmul(out=PSF[:, b, 0:512], lhsT=lh, rhs=rh,
                                         start=(kc == 0), stop=(kc == nk - 1)))
                    E.add("tensor", *steps)
                    mmv = ts.n
                    a = vs.inc()
                    E.add("vector", W(ts, mmv),
                          lambda v, a=a, g=g, vh=vh, b=b: a(v.tensor_copy(
                              out=Vtm[:, g, vh * 512:(vh + 1) * 512], in_=PSF[:, b, 0:512])))
                    bank_mark[b] = (vs, vs.n)
                unit_consumed[iu] = ts.n
            return vs.n

        def plan_scores(c, qkv_done):
            g, hp = divmod(c, HPAIRS)
            tok = slice(g * P, (g + 1) * P)
            waits = []
            if c == 0:
                waits += [W(vs, qkv_done), W(mk, setup_mk)]
            B0 = 2 + 2 * (c % 2)
            waits.append(WB(B0))
            waits.append(WB(B0 + 1))
            a1, a2 = ts.inc(), ts.inc()
            E.add("tensor", *waits,
                  MM(a1, PSF[:, B0, 0:128], Br[0:64, 16 + hp, tok],
                     Br[0:64, 24 + hp, tok], True, True),
                  MM(a2, PSF[:, B0 + 1, 0:128], Br[64:128, 16 + hp, tok],
                     Br[64:128, 24 + hp, tok], True, True))
            return ts.n

        def plan_smmax(c, sc_done):
            pr = c % 3
            B0 = 2 + 2 * (c % 2)
            for h in range(2):
                a1 = vs.inc()
                wmid = W(vs, vs.n)
                a2 = vs.inc()
                E.add("vector",
                      W(ts, sc_done - (1 - h)), WM(("sm", pr, h)),
                      lambda v, a=a1, pr=pr, h=h, b=B0 + h: a(v.tensor_tensor(
                          out=sm[:, pr, h, :], in0=PSF[:, b, 0:128],
                          in1=mask[:, :], op=ALU.add)),
                      wmid, WM(("nmax", pr, h)),
                      lambda v, a=a2, pr=pr, h=h: a(v.reduce_max(
                          out=nmax[:, pr, h:h + 1], in_=sm[:, pr, h, :], axis=AX.X, negate=True)))
                bank_mark[B0 + h] = (vs, vs.n - 1)
            sm_done = vs.n
            for h in range(2):
                a3 = as_.inc()
                E.add("scalar",
                      W(vs, sm_done) if h == 0 else None,
                      WM(("ee", pr, h)), WM(("rsum", pr, h)),
                      lambda s, a=a3, pr=pr, h=h: a(s.activation(
                          out=ee[:, pr, h, :], in_=sm[:, pr, h, :], func=AF.Exp,
                          bias=nmax[:, pr, h:h + 1], accum_out=rsum[:, pr, h:h + 1])))
                mark[("sm", pr, h)] = (as_, as_.n)
                mark[("nmax", pr, h)] = (as_, as_.n)
            return as_.n

        def plan_norm(c, exp_done):
            pr = c % 3
            for h in range(2):
                a4 = vs.inc()
                wmid = W(vs, vs.n)
                a5 = vs.inc()
                E.add("vector",
                      W(as_, exp_done) if h == 0 else None,
                      WM(("rrec", pr, h)),
                      lambda v, a=a4, pr=pr, h=h: a(v.reciprocal(
                          out=rrec[:, pr, h:h + 1], in_=rsum[:, pr, h:h + 1])),
                      wmid, WM(("an", pr, h)),
                      lambda v, a=a5, pr=pr, h=h: a(v.tensor_scalar(
                          out=an[:, pr, h, :], in0=ee[:, pr, h, :],
                          scalar1=rrec[:, pr, h:h + 1], scalar2=None, op0=ALU.mult)))
                mark[("ee", pr, h)] = (vs, vs.n)
                mark[("rsum", pr, h)] = (vs, vs.n)
                mark[("rrec", pr, h)] = (vs, vs.n)
            return vs.n

        def plan_tr3(c, norm_done):
            pr = c % 3
            off = (c % 2) * 2 * P
            a3 = ts.inc()
            E.add("tensor",
                  W(vs, norm_done), WB(0, psb_mark),
                  lambda e, pr=pr, off=off: e.transpose(
                      out=PSB[:, 0, off:off + P], in_=an[:, pr, 0, :], identity=id_bf[:, :]),
                  lambda e, a=a3, pr=pr, off=off: a(e.transpose(
                      out=PSB[:, 0, off + P:off + 2 * P], in_=an[:, pr, 1, :],
                      identity=id_bf[:, :])))
            mark[("an", pr, 0)] = (ts, ts.n)
            mark[("an", pr, 1)] = (ts, ts.n)
            return ts.n

        def plan_trevac(c, tr3_done):
            pr = c % 5
            off = (c % 2) * 2 * P
            a1 = vs.inc()
            E.add("vector",
                  W(ts, tr3_done), WM(("at", pr)),
                  lambda v, a=a1, pr=pr, off=off: a(v.tensor_copy(
                      out=at_sb[:, pr, :, :],
                      in_=PSB[:, 0, off:off + 2 * P].rearrange("p (h t) -> p h t", h=2))))
            psb_mark[0] = (vs, vs.n)
            return vs.n

        def plan_av(c, ev_done):
            pr = c % 5
            g, hp = divmod(c, HPAIRS)
            BA = 6
            a2 = ts.inc()
            E.add("tensor",
                  W(vs, ev_done), WB(BA),
                  MM_noinc(PSF[0:64, BA, 0:128], Vtm[:, g, hp * 128: hp * 128 + 64],
                           at_sb[:, pr, 0, :], True, True, (0, 0)),
                  MM(a2, PSF[64:128, BA, 0:128], Vtm[:, g, hp * 128 + 64: hp * 128 + 128],
                     at_sb[:, pr, 1, :], True, True, (0, 64)))
            mark[("at", pr)] = (ts, ts.n)
            return ts.n

        def plan_avevac(c, av_done):
            g, hp = divmod(c, HPAIRS)
            a1 = vs.inc()
            E.add("vector",
                  W(ts, av_done), WM("attnout") if c == 0 else None,
                  lambda v, a=a1, g=g, hp=hp, BA=6: a(v.tensor_scalar(
                      out=Ao[:, hp, g * P:(g + 1) * P], in0=PSF[:, BA, 0:128],
                      scalar1=vs_b["bqkv"][:, 16 + hp:17 + hp], scalar2=None,
                      op0=ALU.add)))
            bank_mark[6] = (vs, vs.n)
            return vs.n

        def phase_F1(t):
            NOC = 32
            f1_state = {}
            cur = {}
            for k in range(2 * NOC):
                a_ss, w_ss = k // 8, k % 8
                if w_ss == 0 and a_ss % 2 == 0:
                    cur["u1"] = stream_one("w1", a_ss // 2)
                    cur["u2"] = stream_one("w2", a_ss // 2)
                (iu1, slot1, wval1, n_oc, K, OU) = cur["u1"]
                (iu2, slot2, wval2, _, _, _) = cur["u2"]
                half = 4 * (a_ss % 2)

                if w_ss < 4:
                    oc = a_ss * 4 + w_ss
                    b = (oc + 2) % 4
                    waits = [W(*wval1)] if (w_ss == 0 and half == 0) else []
                    waits.append(WB(b))
                    mmv = mm_group(slot1, half + w_ss, K, OU, PSF[:, b, 0:T],
                                   [xn_f[:, kc, 0:T] for kc in range(K // P)], waits)
                    f1_state[("u1", oc)] = mmv
                    if w_ss == 3:
                        if half == 4:
                            unit_consumed[iu1] = ts.n
                        steps = []
                        for q in range(4):
                            occ = a_ss * 4 + q
                            a1 = as_.inc()
                            steps += [W(ts, f1_state[("u1", occ)]),
                                      WM(("t1", occ % 4)),
                                      lambda s, a=a1, occ=occ, bq=(occ + 2) % 4: a(
                                          s.activation(
                                              out=t1[:, occ % 4, 0:T], in_=PSF[:, bq, 0:T],
                                              func=AF.Silu,
                                              scale=vs_b["w1s"][:, occ:occ + 1]))]
                            f1_state[("silu", occ)] = as_.n
                            bank_mark[(occ + 2) % 4] = (as_, as_.n)
                        E.add("scalar", *steps)
                else:
                    oc = a_ss * 4 + (w_ss - 4)
                    b = (oc + 2) % 4
                    waits = [W(*wval2)] if (w_ss == 4 and half == 0) else []
                    waits.append(W(as_, f1_state[("silu", oc)]))
                    waits.append(WB(b))
                    mmv = mm_group(slot2, half + w_ss - 4, K, OU, PSF[:, b, 0:T],
                                   [xn_f[:, kc, 0:T] for kc in range(K // P)], waits)
                    if w_ss == 7 and half == 4:
                        unit_consumed[iu2] = ts.n
                    a2 = vs.inc()
                    E.add("vector",
                          W(ts, mmv), W(as_, f1_state[("silu", oc)]),
                          lambda v, a=a2, oc=oc, b=b: a(v.scalar_tensor_tensor(
                              out=Ar[:, oc, 0:T], in0=PSF[:, b, 0:T],
                              scalar=vs_b["w2s"][:, oc:oc + 1], in1=t1[:, oc % 4, 0:T],
                              op0=ALU.mult, op1=ALU.mult)))
                    bank_mark[b] = (vs, vs.n)
                    mark[("t1", oc % 4)] = (vs, vs.n)
            return vs.n, ts.n

        def phase_H1A(t, qkv_done, u_done, proj_mm_prev, h2_mm_prev):
            applies_next = []
            gelu_marks = []
            """h1 (Br = gelu(h1 @ Ar), 32 groups, banks 0-3) with the NCH
            attention chains woven in (banks 4-5, DVE/ACT otherwise idle
            here). q/k live in Br[16:32] and are guarded per chunk against
            h1's gelu evacs; all chains finish by group ~19."""
            mark["attnout"] = (ts, proj_mm_prev)
            O, K, OU = WSPECS["h1"]
            sc_d, exp_d, norm_d, tr3_d, ev_d, av_d = {}, {}, {}, {}, {}, {}
            cur = [None]
            for gi in range(32):
                jj = gi % 2
                if jj == 0:
                    cur[0] = stream_one("h1", gi // 2)
                iu, slot, wval, n_oc, K, OU = cur[0]

                prc = [c for c in (2 * (gi - 2), 2 * (gi - 2) + 1) if 0 <= c < NCH]
                for c in prc:
                    tr3_d[c] = plan_tr3(c, norm_d[c])
                for c in prc:
                    ev_d[c] = plan_trevac(c, tr3_d[prc[-1]])
                for c in (2 * (gi - 1), 2 * (gi - 1) + 1):
                    if 0 <= c < NCH:
                        norm_d[c] = plan_norm(c, exp_d[c])
                for c in (2 * gi, 2 * gi + 1):
                    if c < NCH:
                        sc_d[c] = plan_scores(c, qkv_done)
                        exp_d[c] = plan_smmax(c, sc_d[c])
                c_odd = 2 * (gi - 4) + 1
                if 0 <= c_odd < NCH:
                    av_d[c_odd] = plan_av(c_odd, ev_d[c_odd])
                    plan_avevac(c_odd, av_d[c_odd])

                if gi in (20, 23, 26, 29) and t + 1 < NT:
                    applies_next.append(phase_L_pre_group(t + 1, (gi - 20) // 3))

                oc = gi
                b = oc % 2
                waits = [W(*wval)] if jj == 0 else []
                if gi == 0:
                    waits.append(W(vs, u_done))
                waits.append(WB(b))
                mmv = mm_group(slot, jj, K, OU, PSF[:, b, 0:T],
                               [Ar[:, kc, 0:T] for kc in range(K // P)], waits)
                if jj == n_oc - 1:
                    unit_consumed[iu] = ts.n
                c_ev = 2 * (gi - 3)
                if 0 <= c_ev < NCH:
                    av_d[c_ev] = plan_av(c_ev, ev_d[c_ev])
                    plan_avevac(c_ev, av_d[c_ev])
                steps = [W(ts, mmv)]
                if gi == 0:
                    steps.append(W(ts, h2_mm_prev))
                if oc >= 16:
                    steps.append(W(ts, sc_d[24 + (oc - 16) % 8]))
                a = as_.inc()
                steps.append(lambda s, a=a, oc=oc, b=b: a(
                    s.activation(out=Br[:, oc, 0:T], in_=PSF[:, b, 0:T], func=AF.Gelu,
                                 scale=vs_b["h1s"][:, oc:oc + 1])))
                E.add("scalar", *steps)
                bank_mark[b] = (as_, as_.n)
                gelu_marks.append((as_, as_.n))
            for c in range(NCH):
                if c not in av_d:
                    av_d[c] = plan_av(c, ev_d[c])
                    plan_avevac(c, av_d[c])
            return vs.n, ts.n, as_.n, applies_next, gelu_marks

        def phase_P(t, attn_done, fillers=()):
            fillers = list(fillers)
            for j in range(WSPECS["wproj"][0] // WSPECS["wproj"][2]):
                iu, slot, wval, n_oc, K, OU = stream_one("wproj", j)
                for jj in range(n_oc):
                    oc = j * n_oc + jj
                    b = oc % 6
                    waits = [W(*wval)] if jj == 0 else []
                    if j == 0 and jj == 0:
                        waits.append(W(vs, attn_done))
                    waits.append(WB(b))
                    mmv = mm_group(slot, jj, K, OU, PSF[:, b, 0:T],
                                   [Ao[:, kc, 0:T] for kc in range(K // P)], waits)
                    if fillers:
                        fillers.pop(0)()
                    steps = [W(ts, mmv)]
                    if j == 0 and jj == 0:
                        steps.append(WM("sum"))
                    a = vs.inc()
                    steps.append(lambda v, a=a, oc=oc, b=b: a(v.tensor_scalar(
                        out=sum_f[:, oc, :], in0=PSF[:, b, 0:T],
                        scalar1=vs_b["bproj"][:, oc:oc + 1], scalar2=None, op0=ALU.add)))
                    bank_mark[b] = (vs, vs.n)
                    E.add("vector", *steps)
                unit_consumed[iu] = ts.n
            return ts.n

        def phase_H(t, wname, sname, src, dst, src_ready, dst_free,
                    src_chunk_marks=None):
            O, K, OU = WSPECS[wname]
            for j in range(O // OU):
                iu, slot, wval, n_oc, K, OU = stream_one(wname, j)
                for jj in range(n_oc):
                    oc = j * n_oc + jj
                    b = oc % 6
                    waits = [W(*wval)] if jj == 0 else []
                    kcw = None
                    if j == 0 and jj == 0:
                        if src_chunk_marks is not None:
                            kcw = [W(src_chunk_marks[kc][0], src_chunk_marks[kc][1])
                                   for kc in range(K // P)]
                        else:
                            waits.append(W(src_ready[0], src_ready[1]))
                    waits.append(WB(b))
                    mmv = mm_group(slot, jj, K, OU, PSF[:, b, 0:T],
                                   [src[:, kc, 0:T] for kc in range(K // P)], waits,
                                   kc_waits=kcw)
                    steps = [W(ts, mmv)]
                    if j == 0 and jj == 0:
                        steps.append(W(dst_free[0], dst_free[1]))
                    a = as_.inc()
                    steps.append(lambda s, a=a, oc=oc, b=b, sname=sname, dst=dst: a(
                        s.activation(out=dst[:, oc, 0:T], in_=PSF[:, b, 0:T], func=AF.Gelu,
                                     scale=vs_b[sname][:, oc:oc + 1])))
                    E.add("scalar", *steps)
                    bank_mark[b] = (as_, as_.n)
                unit_consumed[iu] = ts.n
            return ts.n, as_.n

        def phase_W3(t, src_ready):
            for j in range(1024 // 256):
                iu, slot, wval, n_oc, K, OU = stream_one("w3", j)
                for jj in range(n_oc):
                    oc = j * n_oc + jj
                    b = oc % 6
                    waits = [W(*wval)] if jj == 0 else []
                    if j == 0 and jj == 0:
                        waits.append(W(src_ready[0], src_ready[1]))
                    waits.append(WB(b))
                    mmv = mm_group(slot, jj, K, OU, PSF[:, b, 0:T],
                                   [Ar[:, kc, 0:T] for kc in range(K // P)], waits)
                    a = vs.inc()
                    E.add("vector", W(ts, mmv),
                          lambda v, a=a, oc=oc, b=b: a(v.scalar_tensor_tensor(
                              out=sum_f[:, oc, :], in0=PSF[:, b, 0:T],
                              scalar=vs_b["w3s"][:, oc:oc + 1], in1=sum_f[:, oc, :],
                              op0=ALU.mult, op1=ALU.add)))
                    bank_mark[b] = (vs, vs.n)
                unit_consumed[iu] = ts.n
            return ts.n, vs.n

        def plan_xf_load(t):
            a = xos[t % 2].inc(16)
            E.add("scalar", WM("xf"),
                  lambda e, a=a, t=t: a(e.dma_start(
                      out=xf[:, :, :],
                      in_=xf_in[:, t * T:(t + 1) * T].rearrange(
                          "(c p) t -> p c t", p=P))))
            return (xos[t % 2], xos[t % 2].n)

        def phase_O(t, sum_done, xf_val):
            steps = [W(vs, sum_done), W(*xf_val)]
            for oc in range(CH):
                a = vs.inc()
                steps.append(lambda v, a=a, oc=oc: a(v.tensor_tensor(
                    out=xf[:, oc, :], in0=sum_f[:, oc, :], in1=xf[:, oc, :],
                    op=ALU.add)))
            E.add("vector", *steps)
            adds_done = vs.n
            mark["sum"] = (vs, adds_done)

            oslot = t % 2
            a = oss[oslot].inc(16)
            E.add("scalar",
                  W(vs, adds_done),
                  lambda e, a=a, t=t: a(e.dma_start(
                      out=out_d[:, t * T:(t + 1) * T].rearrange(
                          "(c p) t -> p c t", p=P),
                      in_=xf[:, :, :])))
            mark["xf"] = (oss[oslot], oss[oslot].n)

        proj_mm_prev = 0
        h2_mm_prev = 0
        ln_next = phase_L_tr(0, phase_L_pre(0))
        for t in range(NT):
            ln_all = ln_next
            qkv_done = phase_Q(t, ln_all)
            u_done, f1_mm_done = phase_F1(t)
            mark["xn_f"] = (ts, f1_mm_done)
            attn_done, h1_mm, h1_ev, applies_next, gelu_marks = phase_H1A(
                t, qkv_done, u_done, proj_mm_prev, h2_mm_prev)
            if t + 1 < NT:
                fil = [lambda g=g: phase_L_tr_group(t + 1, applies_next, g)
                       for g in range(G)]
            else:
                fil = []
            proj_mm_prev = phase_P(t, attn_done, fil)
            if t + 1 < NT:
                ln_next = vs.n
            xf_val = plan_xf_load(t)
            h2_mm, h2_ev = phase_H(t, "h2", "h2s", Br, Ar, (as_, h1_ev), (ts, h1_mm),
                                   src_chunk_marks=gelu_marks)
            h2_mm_prev = h2_mm
            mark["Br"] = (ts, h2_mm)
            w3_mm_prev, sum_done = phase_W3(t, (as_, h2_ev))
            phase_O(t, sum_done, xf_val)
            E.flush()

        a_fin = None
        E.add("scalar", W(osA, osA.n), W(osB, osB.n))
        E.flush()

    return nc


def _layout_weight(w, OU):
    O, K = w.shape
    return np.ascontiguousarray(
        w.reshape(O // OU, OU, K // P, P).transpose(0, 3, 2, 1).reshape(O // OU, P, (K // P) * OU))


def _layout_vec(v):
    n = v.shape[0] // P
    return np.ascontiguousarray(v.reshape(n, P).T)


def kernel(L0_attn_x, L0_attn_Wqkv, L0_attn_bqkv, L0_attn_Wproj, L0_attn_bproj,
           L0_ffn_w1_w, L0_ffn_w1_s, L0_ffn_w2_w, L0_ffn_w2_s, L0_ffn_h1_w,
           L0_ffn_h1_s, L0_ffn_h2_w, L0_ffn_h2_s, L0_ffn_w3_w, L0_ffn_w3_s):
    NCORES = 8
    x = np.asarray(L0_attn_x)
    n, b, d = x.shape
    b_core = b // NCORES
    nc = build_nc(b_core=b_core, T=512)

    wmats = {
        "wqkv": np.asarray(L0_attn_Wqkv), "wproj": np.asarray(L0_attn_Wproj),
        "w1": np.asarray(L0_ffn_w1_w), "w2": np.asarray(L0_ffn_w2_w),
        "h1": np.asarray(L0_ffn_h1_w), "h2": np.asarray(L0_ffn_h2_w),
        "w3": np.asarray(L0_ffn_w3_w),
    }
    wblocked = {k: _layout_weight(w, WSPECS[k][2]) for k, w in wmats.items()}
    vecs = {
        "bqkv": _layout_vec(np.asarray(L0_attn_bqkv)),
        "bproj": _layout_vec(np.asarray(L0_attn_bproj)),
        "w1s": _layout_vec(np.asarray(L0_ffn_w1_s)),
        "w2s": _layout_vec(np.asarray(L0_ffn_w2_s)),
        "h1s": _layout_vec(np.asarray(L0_ffn_h1_s)),
        "h2s": _layout_vec(np.asarray(L0_ffn_h2_s)),
        "w3s": _layout_vec(np.asarray(L0_ffn_w3_s)),
    }
    in_maps = []
    for c in range(NCORES):
        xc = np.ascontiguousarray(
            x[:, c * b_core:(c + 1) * b_core, :].transpose(1, 0, 2).reshape(b_core * n, d))
        m = {"x": xc, "xf": np.ascontiguousarray(xc.T)}
        m.update(wblocked)
        m.update(vecs)
        in_maps.append(m)

    res = run_bass_kernel_spmd(nc, in_maps, core_ids=list(range(NCORES)))
    out = np.empty((n, b, d), np.float32)
    for c in range(NCORES):
        oc = np.ascontiguousarray(res.results[c]["out"].T).reshape(
            b_core, n, d).transpose(1, 0, 2)
        out[:, c * b_core:(c + 1) * b_core, :] = oc
    return out


# version 52
# speedup vs baseline: 1.2032x; 1.0833x over previous
import contextlib
import sys

sys.path.insert(0, "/opt/trn_rl_repo")

import numpy as np
import ml_dtypes
import concourse.bass as bass
import concourse.mybir as mybir
from concourse.bass_utils import run_bass_kernel_spmd

F32 = mybir.dt.float32
BF16 = mybir.dt.bfloat16
AF = mybir.ActivationFunctionType
ALU = mybir.AluOpType
AX = mybir.AxisListType

P = 128
D = 1024
CH = D // P
NSEQ = 16
HPAIRS = 8

WSPECS = {
    "wqkv": (3072, 1024, 512),
    "wproj": (1024, 1024, 1024),
    "w1": (4096, 1024, 1024),
    "w2": (4096, 1024, 1024),
    "h1": (4096, 4096, 256),
    "h2": (4096, 4096, 256),
    "w3": (1024, 4096, 256),
}

ENGS = ("sync", "gpsimd", "scalar", "vector", "tensor")


class Sem:
    def __init__(self, es, nc, name):
        self.h = es.enter_context(nc.semaphore(name))
        self.n = 0

    def inc(self, k=1):
        self.n += k
        h = self.h
        return lambda instr: instr.then_inc(h, k)


class Emit:
    def __init__(self, block):
        self.block = block
        self.q = {e: [] for e in ENGS}

    def add(self, eng, *steps):
        self.q[eng].extend(s for s in steps if s is not None)

    def flush(self):
        for eng in ENGS:
            steps = self.q[eng]
            if steps:
                ss = tuple(steps)

                def body(e, ss=ss):
                    for f in ss:
                        f(e)
                getattr(self.block, eng)(body)
                self.q[eng] = []


def build_nc(b_core=256, T=512):
    TOK = b_core * NSEQ
    NT = TOK // T
    G = T // P
    NCH = G * HPAIRS
    assert T % P == 0 and TOK % T == 0 and T <= 512

    nc = bass.Bass("TRN2", target_bir_lowering=False, debug=False)
    es = contextlib.ExitStack()

    x_in = nc.dram_tensor("x", [TOK, D], F32, kind="ExternalInput").ap()
    xf_in = nc.dram_tensor("xf", [D, TOK], F32, kind="ExternalInput").ap()
    out_d = nc.dram_tensor("out", [D, TOK], F32, kind="ExternalOutput").ap()
    w_in, w_sc = {}, {}
    for name, (O, K, OU) in WSPECS.items():
        w_in[name] = nc.dram_tensor(name, [O // OU, P, (K // P) * OU], F32, kind="ExternalInput").ap()
        w_sc[name] = nc.dram_tensor(name + "_bf", [O // OU, P, (K // P) * OU], BF16, kind="Internal").ap()
    VECN = {"bqkv": 24, "bproj": 8, "w1s": 32, "w2s": 32, "h1s": 32, "h2s": 32, "w3s": 8}
    vecs_in = {k: nc.dram_tensor(k, [P, n], F32, kind="ExternalInput").ap() for k, n in VECN.items()}

    Tn = lambda *a: es.enter_context(nc.sbuf_tensor(*a))
    xt = Tn("xt", [P, 2, D], F32)
    xf = Tn("xf_sb", [P, CH, T], F32)
    xn_tm = Tn("xn_tm", [P, 4, D], BF16)
    xn_f = Tn("xn_f", [P, CH, T], BF16)
    Ar = Tn("Ar", [P, 32, T], BF16)
    Br = Tn("Br", [P, 32, T], BF16)
    Ao = Tn("Ao", [P, CH, T], BF16)
    Vtm = Tn("Vtm", [P, 4, 1024], BF16)
    sum_f = Tn("sum_f", [P, CH, T], F32)
    wbuf = Tn("wbuf", [P, 3, 8192], BF16)
    t1 = Tn("t1", [P, 4, 512], BF16)
    vs_b = {k: Tn("v_" + k, [P, n], F32) for k, n in VECN.items()}
    stats6 = Tn("stats6", [P, 2, 12], F32)
    mv = Tn("mv", [P, 2, 2], F32)
    rstd = Tn("rstdb", [P, 2, 1], F32)
    lnt = Tn("lnt", [P, 2, 1], F32)
    epsv = Tn("epsv", [P, 1], F32)
    id_bf = Tn("id_bf", [P, P], BF16)
    m16 = Tn("m16", [16, 16], F32)
    mask = Tn("mask", [P, P], F32)
    sm = Tn("smb", [P, 3, 2, P], F32)
    ee = Tn("eeb", [P, 3, 2, P], F32)
    an = Tn("anb", [P, 3, 2, P], BF16)
    at_sb = Tn("at_sb", [P, 5, 2, P], BF16)
    vt_sb = Tn("vt_sb", [P, 2, P], BF16)
    nmax = Tn("nmax", [P, 3, 2], F32)
    rsum = Tn("rsum", [P, 3, 2], F32)
    rrec = Tn("rrec", [P, 3, 2], F32)

    PSF = es.enter_context(nc.psum_tensor("PSF", [P, 7, 512], F32))
    PSB = es.enter_context(nc.psum_tensor("PSB", [P, 1, 1024], BF16))

    with nc.Block() as block, es:
        ts = Sem(es, nc, "ts")
        vs = Sem(es, nc, "vs")
        as_ = Sem(es, nc, "as_")
        ps = Sem(es, nc, "ps")
        ws6 = [Sem(es, nc, "ws%d" % i) for i in range(6)]
        xlA, xlB = Sem(es, nc, "xlA"), Sem(es, nc, "xlB")
        osA, osB = Sem(es, nc, "osA"), Sem(es, nc, "osB")
        xoA, xoB, xoC, xoD = (Sem(es, nc, "xoA"), Sem(es, nc, "xoB"),
                              Sem(es, nc, "xoC"), Sem(es, nc, "xoD"))
        wbA, wbB, wbC = Sem(es, nc, "wbA"), Sem(es, nc, "wbB"), Sem(es, nc, "wbC")
        wcA, wcB, wcC = Sem(es, nc, "wcA"), Sem(es, nc, "wcB"), Sem(es, nc, "wcC")
        lv = Sem(es, nc, "lv")
        mk = Sem(es, nc, "mk")
        wbs = [wbA, wbB, wbC]
        wsc = [wcA, wcB, wcC]
        xls = [xlA, xlB]
        oss = [osA, osB]
        xos = [xoA, xoB, xoC, xoD]

        E = Emit(block)

        def W(sem, val):
            return lambda e: e.wait_ge(sem.h, val)

        bank_mark = {b: None for b in range(7)}
        psb_mark = {b: None for b in range(1)}
        mark = {}

        def WB(b, table=None):
            m = (table if table is not None else bank_mark)[b]
            return None if m is None else W(m[0], m[1])

        def WM(key):
            m = mark.get(key)
            return None if m is None else W(m[0], m[1])

        steps = []
        for k in vecs_in:
            a = lv.inc(16)
            steps.append(lambda g, a=a, k=k: a(g.dma_start(out=vs_b[k][:, :], in_=vecs_in[k])))
        for tile, val in ((id_bf, 0.0), (epsv, 1e-5), (m16, 0.0), (mask, -1e30)):
            a = ps.inc()
            steps.append(lambda g, a=a, tile=tile, val=val: a(g.memset(tile[:, :], val)))
        steps.append(W(ps, ps.n))
        for tile, cmp_, fill, pat in (
                (id_bf, ALU.not_equal, 1.0, [[-1, P]]),
                (m16, ALU.is_ge, -1e30, [[-4, 4], [0, 4]])):
            a = ps.inc()
            steps.append(lambda g, a=a, tile=tile, cmp_=cmp_, fill=fill, pat=pat: a(
                g.affine_select(out=tile[:, :], in_=tile[:, :], compare_op=cmp_, fill=fill,
                                base=0, pattern=pat, channel_multiplier=1)))
        steps.append(W(ps, ps.n))
        for j in range(8):
            a = mk.inc(16)
            steps.append(lambda g, a=a, j=j: a(
                g.dma_start(out=mask[16 * j:16 * (j + 1), 16 * j:16 * (j + 1)], in_=m16[:, :])))
        E.add("gpsimd", *steps)
        setup_ps, setup_lv, setup_mk = ps.n, lv.n, mk.n

        E.flush()

        NU = sum(O // OU for (O, K, OU) in WSPECS.values())
        iu_ctr = {"i": 0}
        unit_consumed = {}
        wb_val = {}
        last_wb = {0: 0, 1: 0, 2: 0}

        def stream_one(name, j):
            O, K, OU = WSPECS[name]
            iu = iu_ctr["i"]
            slot = iu % 3
            sz = (K // P) * OU
            wbw = (W(wbs[slot], last_wb[slot])
                   if (iu >= 3 and iu - 3 < NU) else None)
            if iu < NU:
                a = wsc[slot].inc(16)
                E.add("gpsimd",
                      W(ts, unit_consumed[iu - 3]) if iu >= 3 else None,
                      wbw,
                      lambda e, a=a, name=name, j=j, slot=slot, sz=sz: a(
                          e.dma_start(out=wbuf[:, slot, 0:sz], in_=w_in[name][j, :, :])))
                wv = (wsc[slot], wsc[slot].n)
                aw = wbs[slot].inc(16)
                E.add("sync",
                      W(*wv),
                      lambda g, aw=aw, name=name, j=j, slot=slot, sz=sz: aw(
                          g.dma_start(out=w_sc[name][j, :, :], in_=wbuf[:, slot, 0:sz])))
                last_wb[slot] = wbs[slot].n
                wb_val[(name, j)] = (wbs[slot], wbs[slot].n)
            else:
                sem = ws6[iu % 6]
                a = sem.inc(16)
                E.add("sync" if iu % 2 == 0 else "gpsimd",
                      W(ts, unit_consumed[iu - 3]),
                      wbw,
                      W(*wb_val[(name, j)]) if iu < 2 * NU else None,
                      lambda e, a=a, name=name, j=j, slot=slot, sz=sz: a(
                          e.dma_start(out=wbuf[:, slot, 0:sz], in_=w_sc[name][j, :, :])))
                wv = (sem, sem.n)
            iu_ctr["i"] += 1
            return iu, slot, wv, n_oc_of(OU), K, OU

        def n_oc_of(OU):
            return OU // P

        def lhsT_of(slot, kc, jj, OU):
            return wbuf[:, slot, kc * OU + jj * P: kc * OU + (jj + 1) * P]

        def MM(attach, out, lhsT, rhs, start, stop, tile_position=None):
            return lambda e: attach(e.matmul(out=out, lhsT=lhsT, rhs=rhs, start=start,
                                             stop=stop, tile_position=tile_position))

        def MM_noinc(out, lhsT, rhs, start, stop, tile_position=None):
            return lambda e: e.matmul(out=out, lhsT=lhsT, rhs=rhs, start=start,
                                      stop=stop, tile_position=tile_position)

        def mm_group(slot, jj, K, OU, out, rhs_tile, waits, kc_waits=None):
            steps = list(waits)
            nk = K // P
            for kc in range(nk):
                if kc_waits is not None and kc_waits[kc] is not None:
                    steps.append(kc_waits[kc])
                a = ts.inc() if kc == nk - 1 else None
                steps.append(MM(a, out, lhsT_of(slot, kc, jj, OU), rhs_tile[kc],
                                kc == 0, kc == nk - 1) if a else
                             MM_noinc(out, lhsT_of(slot, kc, jj, OU), rhs_tile[kc],
                                      kc == 0, kc == nk - 1))
            E.add("tensor", *steps)
            return ts.n

        def phase_L_pre_group(t, g):
                slot = (t * G + g) % 2
                a = xls[slot].inc(16)
                E.add("gpsimd", WM(("xt", slot)),
                      lambda e, a=a, t=t, g=g, slot=slot: a(e.dma_start(
                          out=xt[:, slot, :],
                          in_=x_in[(t * T + g * P):(t * T + (g + 1) * P), :])))
                xl_val = xls[slot].n

                a1, a2 = vs.inc(), vs.inc()
                wmid = W(vs, vs.n)
                a3 = vs.inc()
                E.add("vector",
                      W(xls[slot], xl_val), WM(("stats6", slot)),
                      lambda v, a=a1, slot=slot: a(v.bn_stats(out=stats6[:, slot, 0:6],
                                                              in_=xt[:, slot, 0:512])),
                      lambda v, a=a2, slot=slot: a(v.bn_stats(out=stats6[:, slot, 6:12],
                                                              in_=xt[:, slot, 512:1024])),
                      wmid, WM(("mv", slot)),
                      lambda v, a=a3, slot=slot: a(v.bn_aggr(out=mv[:, slot, :],
                                                             in_=stats6[:, slot, :])))
                mark[("stats6", slot)] = (vs, vs.n)
                mv_done = vs.n

                a1 = as_.inc()
                wmid = W(as_, as_.n)
                a2 = as_.inc()
                E.add("scalar",
                      W(vs, mv_done), W(ps, setup_ps), WM(("rstd", slot)),
                      lambda s, a=a1, slot=slot: a(s.activation(
                          out=lnt[:, slot, :], in_=mv[:, slot, 1:2], func=AF.Ln, bias=epsv[:, :])),
                      wmid,
                      lambda s, a=a2, slot=slot: a(s.activation(
                          out=rstd[:, slot, :], in_=lnt[:, slot, :], func=AF.Exp, scale=-0.5)))
                rstd_done = as_.n

                xslot = g % 4
                a1 = vs.inc()
                E.add("vector",
                      W(as_, rstd_done), WM(("xn_tm", xslot)),
                      lambda v, a=a1, slot=slot, xslot=xslot: a(v.tensor_scalar(
                          out=xn_tm[:, xslot, :], in0=xt[:, slot, :], scalar1=mv[:, slot, 0:1],
                          scalar2=rstd[:, slot, 0:1], op0=ALU.subtract, op1=ALU.mult)))
                mark[("xt", slot)] = (vs, vs.n)
                mark[("mv", slot)] = (vs, vs.n)
                mark[("rstd", slot)] = (vs, vs.n)
                return vs.n

        def phase_L_pre(t):
            return [phase_L_pre_group(t, g) for g in range(G)]

        def phase_L_tr_group(t, applies, g):
                xslot = g % 4
                apply_done = applies[g]
                pb = 0
                steps = [W(vs, apply_done), W(ps, setup_ps), WB(pb, psb_mark)]
                for j in range(CH):
                    a = ts.inc() if j == CH - 1 else None
                    steps.append(lambda e, a=a, j=j, pb=pb, xslot=xslot: (
                        a(e.transpose(out=PSB[:, pb, j * P:(j + 1) * P],
                                      in_=xn_tm[:, xslot, j * P:(j + 1) * P],
                                      identity=id_bf[:, :])) if a else
                        e.transpose(out=PSB[:, pb, j * P:(j + 1) * P],
                                    in_=xn_tm[:, xslot, j * P:(j + 1) * P],
                                    identity=id_bf[:, :])))
                E.add("tensor", *steps)
                mark[("xn_tm", xslot)] = (ts, ts.n)
                tr_done = ts.n

                a1 = vs.inc()
                E.add("vector",
                      W(ts, tr_done), WM("xn_f"),
                      lambda v, a=a1, g=g, pb=pb: a(v.tensor_copy(
                          out=xn_f[:, :, g * P:(g + 1) * P],
                          in_=PSB[:, pb, :].rearrange("p (c t) -> p c t", c=CH))))
                psb_mark[pb] = (vs, vs.n)
                return vs.n

        def phase_L_tr(t, applies):
            for g in range(G):
                phase_L_tr_group(t, applies, g)
            return vs.n

        def phase_Q(t, ln_all):
            for j in range(4):
                iu, slot, wval, n_oc, K, OU = stream_one("wqkv", j)
                for jj in range(n_oc):
                    oc = j * n_oc + jj
                    b = oc % 7
                    waits = [W(*wval)] if jj == 0 else []
                    if j == 0 and jj == 0:
                        waits.append(W(vs, ln_all))
                    waits.append(WB(b))
                    mm_group(slot, jj, K, OU, PSF[:, b, 0:T],
                             [xn_f[:, kc, 0:T] for kc in range(K // P)], waits)
                unit_consumed[iu] = ts.n
                mm_done = ts.n

                steps = [W(ts, mm_done)]
                if j == 0:
                    steps += [W(lv, setup_lv), WM("Br")]
                for jj in range(n_oc):
                    oc = j * n_oc + jj
                    b = oc % 7
                    a = vs.inc()
                    if oc < 8:
                        steps.append(lambda v, a=a, oc=oc, b=b: a(v.tensor_scalar(
                            out=Br[:, 16 + oc, 0:T], in0=PSF[:, b, 0:T],
                            scalar1=vs_b["bqkv"][:, oc:oc + 1], scalar2=0.125,
                            op0=ALU.add, op1=ALU.mult)))
                    else:
                        steps.append(lambda v, a=a, oc=oc, b=b: a(v.tensor_scalar(
                            out=Br[:, 16 + oc, 0:T], in0=PSF[:, b, 0:T],
                            scalar1=vs_b["bqkv"][:, oc:oc + 1], scalar2=None, op0=ALU.add)))
                    bank_mark[b] = (vs, vs.n)
                E.add("vector", *steps)
            for j in range(4, 6):
                iu, slot, wval, n_oc, K, OU = stream_one("wqkv", j)
                vh = j - 4
                for g in range(G):
                    b = (g % 2)
                    waits = [W(*wval)] if g == 0 else []
                    waits.append(WB(b))
                    if j == 4 and g == 0:
                        waits.append(WM("Vtm"))
                    nk = K // P
                    steps = list(w for w in waits if w is not None)
                    for kc in range(nk):
                        aa = ts.inc() if kc == nk - 1 else None
                        lh = xn_f[:, kc, g * P:(g + 1) * P]
                        rh = wbuf[:, slot, kc * OU: (kc + 1) * OU]
                        if aa:
                            steps.append(lambda e, aa=aa, lh=lh, rh=rh, b=b, kc=kc, nk=nk: aa(
                                e.matmul(out=PSF[:, b, 0:512], lhsT=lh, rhs=rh,
                                         start=(kc == 0), stop=(kc == nk - 1))))
                        else:
                            steps.append(lambda e, lh=lh, rh=rh, b=b, kc=kc, nk=nk:
                                e.matmul(out=PSF[:, b, 0:512], lhsT=lh, rhs=rh,
                                         start=(kc == 0), stop=(kc == nk - 1)))
                    E.add("tensor", *steps)
                    mmv = ts.n
                    a = vs.inc()
                    E.add("vector", W(ts, mmv),
                          lambda v, a=a, g=g, vh=vh, b=b: a(v.tensor_copy(
                              out=Vtm[:, g, vh * 512:(vh + 1) * 512], in_=PSF[:, b, 0:512])))
                    bank_mark[b] = (vs, vs.n)
                unit_consumed[iu] = ts.n
            return vs.n

        def plan_scores(c, qkv_done):
            g, hp = divmod(c, HPAIRS)
            tok = slice(g * P, (g + 1) * P)
            waits = []
            if c == 0:
                waits += [W(vs, qkv_done), W(mk, setup_mk)]
            B0 = 2 + 2 * (c % 2)
            waits.append(WB(B0))
            waits.append(WB(B0 + 1))
            a1, a2 = ts.inc(), ts.inc()
            E.add("tensor", *waits,
                  MM(a1, PSF[:, B0, 0:128], Br[0:64, 16 + hp, tok],
                     Br[0:64, 24 + hp, tok], True, True),
                  MM(a2, PSF[:, B0 + 1, 0:128], Br[64:128, 16 + hp, tok],
                     Br[64:128, 24 + hp, tok], True, True))
            return ts.n

        def plan_smmax(c, sc_done):
            pr = c % 3
            B0 = 2 + 2 * (c % 2)
            for h in range(2):
                a1 = vs.inc()
                wmid = W(vs, vs.n)
                a2 = vs.inc()
                E.add("vector",
                      W(ts, sc_done - (1 - h)), WM(("sm", pr, h)),
                      lambda v, a=a1, pr=pr, h=h, b=B0 + h: a(v.tensor_tensor(
                          out=sm[:, pr, h, :], in0=PSF[:, b, 0:128],
                          in1=mask[:, :], op=ALU.add)),
                      wmid, WM(("nmax", pr, h)),
                      lambda v, a=a2, pr=pr, h=h: a(v.reduce_max(
                          out=nmax[:, pr, h:h + 1], in_=sm[:, pr, h, :], axis=AX.X, negate=True)))
                bank_mark[B0 + h] = (vs, vs.n - 1)
            sm_done = vs.n
            for h in range(2):
                a3 = as_.inc()
                E.add("scalar",
                      W(vs, sm_done) if h == 0 else None,
                      WM(("ee", pr, h)), WM(("rsum", pr, h)),
                      lambda s, a=a3, pr=pr, h=h: a(s.activation(
                          out=ee[:, pr, h, :], in_=sm[:, pr, h, :], func=AF.Exp,
                          bias=nmax[:, pr, h:h + 1], accum_out=rsum[:, pr, h:h + 1])))
                mark[("sm", pr, h)] = (as_, as_.n)
                mark[("nmax", pr, h)] = (as_, as_.n)
            return as_.n

        def plan_norm(c, exp_done):
            pr = c % 3
            for h in range(2):
                a4 = vs.inc()
                wmid = W(vs, vs.n)
                a5 = vs.inc()
                E.add("vector",
                      W(as_, exp_done) if h == 0 else None,
                      WM(("rrec", pr, h)),
                      lambda v, a=a4, pr=pr, h=h: a(v.reciprocal(
                          out=rrec[:, pr, h:h + 1], in_=rsum[:, pr, h:h + 1])),
                      wmid, WM(("an", pr, h)),
                      lambda v, a=a5, pr=pr, h=h: a(v.tensor_scalar(
                          out=an[:, pr, h, :], in0=ee[:, pr, h, :],
                          scalar1=rrec[:, pr, h:h + 1], scalar2=None, op0=ALU.mult)))
                mark[("ee", pr, h)] = (vs, vs.n)
                mark[("rsum", pr, h)] = (vs, vs.n)
                mark[("rrec", pr, h)] = (vs, vs.n)
            return vs.n

        def plan_tr3(c, norm_done):
            pr = c % 3
            off = (c % 2) * 2 * P
            a3 = ts.inc()
            E.add("tensor",
                  W(vs, norm_done), WB(0, psb_mark),
                  lambda e, pr=pr, off=off: e.transpose(
                      out=PSB[:, 0, off:off + P], in_=an[:, pr, 0, :], identity=id_bf[:, :]),
                  lambda e, a=a3, pr=pr, off=off: a(e.transpose(
                      out=PSB[:, 0, off + P:off + 2 * P], in_=an[:, pr, 1, :],
                      identity=id_bf[:, :])))
            mark[("an", pr, 0)] = (ts, ts.n)
            mark[("an", pr, 1)] = (ts, ts.n)
            return ts.n

        def plan_trevac(c, tr3_done):
            pr = c % 5
            off = (c % 2) * 2 * P
            a1 = vs.inc()
            E.add("vector",
                  W(ts, tr3_done), WM(("at", pr)),
                  lambda v, a=a1, pr=pr, off=off: a(v.tensor_copy(
                      out=at_sb[:, pr, :, :],
                      in_=PSB[:, 0, off:off + 2 * P].rearrange("p (h t) -> p h t", h=2))))
            psb_mark[0] = (vs, vs.n)
            return vs.n

        def plan_av(c, ev_done):
            pr = c % 5
            g, hp = divmod(c, HPAIRS)
            BA = 6
            a2 = ts.inc()
            E.add("tensor",
                  W(vs, ev_done), WB(BA),
                  MM_noinc(PSF[0:64, BA, 0:128], Vtm[:, g, hp * 128: hp * 128 + 64],
                           at_sb[:, pr, 0, :], True, True, (0, 0)),
                  MM(a2, PSF[64:128, BA, 0:128], Vtm[:, g, hp * 128 + 64: hp * 128 + 128],
                     at_sb[:, pr, 1, :], True, True, (0, 64)))
            mark[("at", pr)] = (ts, ts.n)
            return ts.n

        def plan_avevac(c, av_done):
            g, hp = divmod(c, HPAIRS)
            a1 = vs.inc()
            E.add("vector",
                  W(ts, av_done), WM("attnout") if c == 0 else None,
                  lambda v, a=a1, g=g, hp=hp, BA=6: a(v.tensor_scalar(
                      out=Ao[:, hp, g * P:(g + 1) * P], in0=PSF[:, BA, 0:128],
                      scalar1=vs_b["bqkv"][:, 16 + hp:17 + hp], scalar2=None,
                      op0=ALU.add)))
            bank_mark[6] = (vs, vs.n)
            return vs.n

        def phase_F1(t):
            NOC = 32
            f1_state = {}
            cur = {}
            for k in range(2 * NOC):
                a_ss, w_ss = k // 8, k % 8
                if w_ss == 0 and a_ss % 2 == 0:
                    cur["u1"] = stream_one("w1", a_ss // 2)
                    cur["u2"] = stream_one("w2", a_ss // 2)
                (iu1, slot1, wval1, n_oc, K, OU) = cur["u1"]
                (iu2, slot2, wval2, _, _, _) = cur["u2"]
                half = 4 * (a_ss % 2)

                if w_ss < 4:
                    oc = a_ss * 4 + w_ss
                    b = (oc + 2) % 4
                    waits = [W(*wval1)] if (w_ss == 0 and half == 0) else []
                    waits.append(WB(b))
                    mmv = mm_group(slot1, half + w_ss, K, OU, PSF[:, b, 0:T],
                                   [xn_f[:, kc, 0:T] for kc in range(K // P)], waits)
                    f1_state[("u1", oc)] = mmv
                    if w_ss == 3:
                        if half == 4:
                            unit_consumed[iu1] = ts.n
                        steps = []
                        for q in range(4):
                            occ = a_ss * 4 + q
                            a1 = as_.inc()
                            steps += [W(ts, f1_state[("u1", occ)]),
                                      WM(("t1", occ % 4)),
                                      lambda s, a=a1, occ=occ, bq=(occ + 2) % 4: a(
                                          s.activation(
                                              out=t1[:, occ % 4, 0:T], in_=PSF[:, bq, 0:T],
                                              func=AF.Silu,
                                              scale=vs_b["w1s"][:, occ:occ + 1]))]
                            f1_state[("silu", occ)] = as_.n
                            bank_mark[(occ + 2) % 4] = (as_, as_.n)
                        E.add("scalar", *steps)
                else:
                    oc = a_ss * 4 + (w_ss - 4)
                    b = (oc + 2) % 4
                    waits = [W(*wval2)] if (w_ss == 4 and half == 0) else []
                    waits.append(W(as_, f1_state[("silu", oc)]))
                    waits.append(WB(b))
                    mmv = mm_group(slot2, half + w_ss - 4, K, OU, PSF[:, b, 0:T],
                                   [xn_f[:, kc, 0:T] for kc in range(K // P)], waits)
                    if w_ss == 7 and half == 4:
                        unit_consumed[iu2] = ts.n
                    a2 = vs.inc()
                    E.add("vector",
                          W(ts, mmv), W(as_, f1_state[("silu", oc)]),
                          lambda v, a=a2, oc=oc, b=b: a(v.scalar_tensor_tensor(
                              out=Ar[:, oc, 0:T], in0=PSF[:, b, 0:T],
                              scalar=vs_b["w2s"][:, oc:oc + 1], in1=t1[:, oc % 4, 0:T],
                              op0=ALU.mult, op1=ALU.mult)))
                    bank_mark[b] = (vs, vs.n)
                    mark[("t1", oc % 4)] = (vs, vs.n)
            return vs.n, ts.n

        def phase_H1A(t, qkv_done, u_done, proj_mm_prev, h2_mm_prev):
            applies_next = []
            gelu_marks = []
            """h1 (Br = gelu(h1 @ Ar), 32 groups, banks 0-3) with the NCH
            attention chains woven in (banks 4-5, DVE/ACT otherwise idle
            here). q/k live in Br[16:32] and are guarded per chunk against
            h1's gelu evacs; all chains finish by group ~19."""
            mark["attnout"] = (ts, proj_mm_prev)
            O, K, OU = WSPECS["h1"]
            sc_d, exp_d, norm_d, tr3_d, ev_d, av_d = {}, {}, {}, {}, {}, {}
            cur = [None]
            for gi in range(32):
                jj = gi % 2
                if jj == 0:
                    cur[0] = stream_one("h1", gi // 2)
                iu, slot, wval, n_oc, K, OU = cur[0]

                prc = [c for c in (2 * (gi - 2), 2 * (gi - 2) + 1) if 0 <= c < NCH]
                for c in prc:
                    tr3_d[c] = plan_tr3(c, norm_d[c])
                for c in prc:
                    ev_d[c] = plan_trevac(c, tr3_d[prc[-1]])
                for c in (2 * (gi - 1), 2 * (gi - 1) + 1):
                    if 0 <= c < NCH:
                        norm_d[c] = plan_norm(c, exp_d[c])
                for c in (2 * gi, 2 * gi + 1):
                    if c < NCH:
                        sc_d[c] = plan_scores(c, qkv_done)
                        exp_d[c] = plan_smmax(c, sc_d[c])
                c_odd = 2 * (gi - 4) + 1
                if 0 <= c_odd < NCH:
                    av_d[c_odd] = plan_av(c_odd, ev_d[c_odd])
                    plan_avevac(c_odd, av_d[c_odd])

                if gi in (20, 23, 26, 29) and t + 1 < NT:
                    applies_next.append(phase_L_pre_group(t + 1, (gi - 20) // 3))

                oc = gi
                b = oc % 2
                waits = [W(*wval)] if jj == 0 else []
                if gi == 0:
                    waits.append(W(vs, u_done))
                waits.append(WB(b))
                mmv = mm_group(slot, jj, K, OU, PSF[:, b, 0:T],
                               [Ar[:, kc, 0:T] for kc in range(K // P)], waits)
                if jj == n_oc - 1:
                    unit_consumed[iu] = ts.n
                c_ev = 2 * (gi - 3)
                if 0 <= c_ev < NCH:
                    av_d[c_ev] = plan_av(c_ev, ev_d[c_ev])
                    plan_avevac(c_ev, av_d[c_ev])
                steps = [W(ts, mmv)]
                if gi == 0:
                    steps.append(W(ts, h2_mm_prev))
                if oc >= 16:
                    steps.append(W(ts, sc_d[24 + (oc - 16) % 8]))
                a = as_.inc()
                steps.append(lambda s, a=a, oc=oc, b=b: a(
                    s.activation(out=Br[:, oc, 0:T], in_=PSF[:, b, 0:T], func=AF.Gelu,
                                 scale=vs_b["h1s"][:, oc:oc + 1])))
                E.add("scalar", *steps)
                bank_mark[b] = (as_, as_.n)
                gelu_marks.append((as_, as_.n))
            for c in range(NCH):
                if c not in av_d:
                    av_d[c] = plan_av(c, ev_d[c])
                    plan_avevac(c, av_d[c])
            return vs.n, ts.n, as_.n, applies_next, gelu_marks

        def phase_P(t, attn_done, fillers=()):
            fillers = list(fillers)
            for j in range(WSPECS["wproj"][0] // WSPECS["wproj"][2]):
                iu, slot, wval, n_oc, K, OU = stream_one("wproj", j)
                for jj in range(n_oc):
                    oc = j * n_oc + jj
                    b = oc % 7
                    waits = [W(*wval)] if jj == 0 else []
                    if j == 0 and jj == 0:
                        waits.append(W(vs, attn_done))
                    waits.append(WB(b))
                    mmv = mm_group(slot, jj, K, OU, PSF[:, b, 0:T],
                                   [Ao[:, kc, 0:T] for kc in range(K // P)], waits)
                    if fillers:
                        fillers.pop(0)()
                    steps = [W(ts, mmv)]
                    if j == 0 and jj == 0:
                        steps.append(WM("sum"))
                    a = vs.inc()
                    steps.append(lambda v, a=a, oc=oc, b=b: a(v.tensor_scalar(
                        out=sum_f[:, oc, :], in0=PSF[:, b, 0:T],
                        scalar1=vs_b["bproj"][:, oc:oc + 1], scalar2=None, op0=ALU.add)))
                    bank_mark[b] = (vs, vs.n)
                    E.add("vector", *steps)
                unit_consumed[iu] = ts.n
            return ts.n

        def phase_H(t, wname, sname, src, dst, src_ready, dst_free,
                    src_chunk_marks=None):
            O, K, OU = WSPECS[wname]
            for j in range(O // OU):
                iu, slot, wval, n_oc, K, OU = stream_one(wname, j)
                for jj in range(n_oc):
                    oc = j * n_oc + jj
                    b = oc % 7
                    waits = [W(*wval)] if jj == 0 else []
                    kcw = None
                    if j == 0 and jj == 0:
                        if src_chunk_marks is not None:
                            kcw = [W(src_chunk_marks[kc][0], src_chunk_marks[kc][1])
                                   for kc in range(K // P)]
                        else:
                            waits.append(W(src_ready[0], src_ready[1]))
                    waits.append(WB(b))
                    mmv = mm_group(slot, jj, K, OU, PSF[:, b, 0:T],
                                   [src[:, kc, 0:T] for kc in range(K // P)], waits,
                                   kc_waits=kcw)
                    steps = [W(ts, mmv)]
                    if j == 0 and jj == 0:
                        steps.append(W(dst_free[0], dst_free[1]))
                    a = as_.inc()
                    steps.append(lambda s, a=a, oc=oc, b=b, sname=sname, dst=dst: a(
                        s.activation(out=dst[:, oc, 0:T], in_=PSF[:, b, 0:T], func=AF.Gelu,
                                     scale=vs_b[sname][:, oc:oc + 1])))
                    E.add("scalar", *steps)
                    bank_mark[b] = (as_, as_.n)
                unit_consumed[iu] = ts.n
            return ts.n, as_.n

        def phase_W3(t, src_ready):
            for j in range(1024 // 256):
                iu, slot, wval, n_oc, K, OU = stream_one("w3", j)
                for jj in range(n_oc):
                    oc = j * n_oc + jj
                    b = oc % 7
                    waits = [W(*wval)] if jj == 0 else []
                    if j == 0 and jj == 0:
                        waits.append(W(src_ready[0], src_ready[1]))
                    waits.append(WB(b))
                    mmv = mm_group(slot, jj, K, OU, PSF[:, b, 0:T],
                                   [Ar[:, kc, 0:T] for kc in range(K // P)], waits)
                    a = vs.inc()
                    E.add("vector", W(ts, mmv),
                          lambda v, a=a, oc=oc, b=b: a(v.scalar_tensor_tensor(
                              out=sum_f[:, oc, :], in0=PSF[:, b, 0:T],
                              scalar=vs_b["w3s"][:, oc:oc + 1], in1=sum_f[:, oc, :],
                              op0=ALU.mult, op1=ALU.add)))
                    bank_mark[b] = (vs, vs.n)
                unit_consumed[iu] = ts.n
            return ts.n, vs.n

        def plan_xf_load(t):
            a = xos[t % 2].inc(16)
            E.add("scalar", WM("xf"),
                  lambda e, a=a, t=t: a(e.dma_start(
                      out=xf[:, :, :],
                      in_=xf_in[:, t * T:(t + 1) * T].rearrange(
                          "(c p) t -> p c t", p=P))))
            return (xos[t % 2], xos[t % 2].n)

        def phase_O(t, sum_done, xf_val):
            steps = [W(vs, sum_done), W(*xf_val)]
            for oc in range(CH):
                a = vs.inc()
                steps.append(lambda v, a=a, oc=oc: a(v.tensor_tensor(
                    out=xf[:, oc, :], in0=sum_f[:, oc, :], in1=xf[:, oc, :],
                    op=ALU.add)))
            E.add("vector", *steps)
            adds_done = vs.n
            mark["sum"] = (vs, adds_done)

            oslot = t % 2
            a = oss[oslot].inc(16)
            E.add("scalar",
                  W(vs, adds_done),
                  lambda e, a=a, t=t: a(e.dma_start(
                      out=out_d[:, t * T:(t + 1) * T].rearrange(
                          "(c p) t -> p c t", p=P),
                      in_=xf[:, :, :])))
            mark["xf"] = (oss[oslot], oss[oslot].n)

        proj_mm_prev = 0
        h2_mm_prev = 0
        ln_next = phase_L_tr(0, phase_L_pre(0))
        for t in range(NT):
            ln_all = ln_next
            qkv_done = phase_Q(t, ln_all)
            u_done, f1_mm_done = phase_F1(t)
            mark["xn_f"] = (ts, f1_mm_done)
            attn_done, h1_mm, h1_ev, applies_next, gelu_marks = phase_H1A(
                t, qkv_done, u_done, proj_mm_prev, h2_mm_prev)
            if t + 1 < NT:
                fil = [lambda g=g: phase_L_tr_group(t + 1, applies_next, g)
                       for g in range(G)]
            else:
                fil = []
            proj_mm_prev = phase_P(t, attn_done, fil)
            if t + 1 < NT:
                ln_next = vs.n
            xf_val = plan_xf_load(t)
            h2_mm, h2_ev = phase_H(t, "h2", "h2s", Br, Ar, (as_, h1_ev), (ts, h1_mm),
                                   src_chunk_marks=gelu_marks)
            h2_mm_prev = h2_mm
            mark["Br"] = (ts, h2_mm)
            w3_mm_prev, sum_done = phase_W3(t, (as_, h2_ev))
            phase_O(t, sum_done, xf_val)
            E.flush()

        a_fin = None
        E.add("scalar", W(osA, osA.n), W(osB, osB.n))
        E.flush()

    return nc


def _layout_weight(w, OU):
    O, K = w.shape
    return np.ascontiguousarray(
        w.reshape(O // OU, OU, K // P, P).transpose(0, 3, 2, 1).reshape(O // OU, P, (K // P) * OU))


def _layout_vec(v):
    n = v.shape[0] // P
    return np.ascontiguousarray(v.reshape(n, P).T)


def kernel(L0_attn_x, L0_attn_Wqkv, L0_attn_bqkv, L0_attn_Wproj, L0_attn_bproj,
           L0_ffn_w1_w, L0_ffn_w1_s, L0_ffn_w2_w, L0_ffn_w2_s, L0_ffn_h1_w,
           L0_ffn_h1_s, L0_ffn_h2_w, L0_ffn_h2_s, L0_ffn_w3_w, L0_ffn_w3_s):
    NCORES = 8
    x = np.asarray(L0_attn_x)
    n, b, d = x.shape
    b_core = b // NCORES
    nc = build_nc(b_core=b_core, T=512)

    wmats = {
        "wqkv": np.asarray(L0_attn_Wqkv), "wproj": np.asarray(L0_attn_Wproj),
        "w1": np.asarray(L0_ffn_w1_w), "w2": np.asarray(L0_ffn_w2_w),
        "h1": np.asarray(L0_ffn_h1_w), "h2": np.asarray(L0_ffn_h2_w),
        "w3": np.asarray(L0_ffn_w3_w),
    }
    wblocked = {k: _layout_weight(w, WSPECS[k][2]) for k, w in wmats.items()}
    vecs = {
        "bqkv": _layout_vec(np.asarray(L0_attn_bqkv)),
        "bproj": _layout_vec(np.asarray(L0_attn_bproj)),
        "w1s": _layout_vec(np.asarray(L0_ffn_w1_s)),
        "w2s": _layout_vec(np.asarray(L0_ffn_w2_s)),
        "h1s": _layout_vec(np.asarray(L0_ffn_h1_s)),
        "h2s": _layout_vec(np.asarray(L0_ffn_h2_s)),
        "w3s": _layout_vec(np.asarray(L0_ffn_w3_s)),
    }
    in_maps = []
    for c in range(NCORES):
        xc = np.ascontiguousarray(
            x[:, c * b_core:(c + 1) * b_core, :].transpose(1, 0, 2).reshape(b_core * n, d))
        m = {"x": xc, "xf": np.ascontiguousarray(xc.T)}
        m.update(wblocked)
        m.update(vecs)
        in_maps.append(m)

    res = run_bass_kernel_spmd(nc, in_maps, core_ids=list(range(NCORES)))
    out = np.empty((n, b, d), np.float32)
    for c in range(NCORES):
        oc = np.ascontiguousarray(res.results[c]["out"].T).reshape(
            b_core, n, d).transpose(1, 0, 2)
        out[:, c * b_core:(c + 1) * b_core, :] = oc
    return out


# version 53
# speedup vs baseline: 1.2087x; 1.0046x over previous
import contextlib
import sys

sys.path.insert(0, "/opt/trn_rl_repo")

import numpy as np
import ml_dtypes
import concourse.bass as bass
import concourse.mybir as mybir
from concourse.bass_utils import run_bass_kernel_spmd

F32 = mybir.dt.float32
BF16 = mybir.dt.bfloat16
AF = mybir.ActivationFunctionType
ALU = mybir.AluOpType
AX = mybir.AxisListType

P = 128
D = 1024
CH = D // P
NSEQ = 16
HPAIRS = 8

WSPECS = {
    "wqkv": (3072, 1024, 512),
    "wproj": (1024, 1024, 1024),
    "w1": (4096, 1024, 1024),
    "w2": (4096, 1024, 1024),
    "h1": (4096, 4096, 256),
    "h2": (4096, 4096, 256),
    "w3": (1024, 4096, 256),
}

ENGS = ("sync", "gpsimd", "scalar", "vector", "tensor")


class Sem:
    def __init__(self, es, nc, name):
        self.h = es.enter_context(nc.semaphore(name))
        self.n = 0

    def inc(self, k=1):
        self.n += k
        h = self.h
        return lambda instr: instr.then_inc(h, k)


class Emit:
    def __init__(self, block):
        self.block = block
        self.q = {e: [] for e in ENGS}

    def add(self, eng, *steps):
        self.q[eng].extend(s for s in steps if s is not None)

    def flush(self):
        for eng in ENGS:
            steps = self.q[eng]
            if steps:
                ss = tuple(steps)

                def body(e, ss=ss):
                    for f in ss:
                        f(e)
                getattr(self.block, eng)(body)
                self.q[eng] = []


def build_nc(b_core=256, T=512):
    TOK = b_core * NSEQ
    NT = TOK // T
    G = T // P
    NCH = G * HPAIRS
    assert T % P == 0 and TOK % T == 0 and T <= 512

    nc = bass.Bass("TRN2", target_bir_lowering=False, debug=False)
    es = contextlib.ExitStack()

    x_in = nc.dram_tensor("x", [TOK, D], F32, kind="ExternalInput").ap()
    xf_in = nc.dram_tensor("xf", [D, TOK], F32, kind="ExternalInput").ap()
    out_d = nc.dram_tensor("out", [D, TOK], F32, kind="ExternalOutput").ap()
    w_in, w_sc = {}, {}
    for name, (O, K, OU) in WSPECS.items():
        w_in[name] = nc.dram_tensor(name, [O // OU, P, (K // P) * OU], F32, kind="ExternalInput").ap()
        w_sc[name] = nc.dram_tensor(name + "_bf", [O // OU, P, (K // P) * OU], BF16, kind="Internal").ap()
    VECN = {"bqkv": 24, "bproj": 8, "w1s": 32, "w2s": 32, "h1s": 32, "h2s": 32, "w3s": 8}
    vecs_in = {k: nc.dram_tensor(k, [P, n], F32, kind="ExternalInput").ap() for k, n in VECN.items()}

    Tn = lambda *a: es.enter_context(nc.sbuf_tensor(*a))
    xt = Tn("xt", [P, 2, D], F32)
    xf = Tn("xf_sb", [P, CH, T], F32)
    xn_tm = Tn("xn_tm", [P, 4, D], BF16)
    xn_f = Tn("xn_f", [P, CH, T], BF16)
    Ar = Tn("Ar", [P, 32, T], BF16)
    Br = Tn("Br", [P, 32, T], BF16)
    Ao = Tn("Ao", [P, CH, T], BF16)
    Vtm = Tn("Vtm", [P, 4, 1024], BF16)
    sum_f = Tn("sum_f", [P, CH, T], F32)
    wbuf = Tn("wbuf", [P, 3, 8192], BF16)
    t1 = Tn("t1", [P, 4, 512], BF16)
    vs_b = {k: Tn("v_" + k, [P, n], F32) for k, n in VECN.items()}
    stats6 = Tn("stats6", [P, 2, 12], F32)
    mv = Tn("mv", [P, 2, 2], F32)
    rstd = Tn("rstdb", [P, 2, 1], F32)
    lnt = Tn("lnt", [P, 2, 1], F32)
    epsv = Tn("epsv", [P, 1], F32)
    id_bf = Tn("id_bf", [P, P], BF16)
    m16 = Tn("m16", [16, 16], F32)
    mask = Tn("mask", [P, P], F32)
    sm = Tn("smb", [P, 3, 2, P], F32)
    ee = Tn("eeb", [P, 3, 2, P], F32)
    an = Tn("anb", [P, 3, 2, P], BF16)
    at_sb = Tn("at_sb", [P, 5, 2, P], BF16)
    nmax = Tn("nmax", [P, 3, 2], F32)
    rsum = Tn("rsum", [P, 3, 2], F32)
    rrec = Tn("rrec", [P, 3, 2], F32)

    PSF = es.enter_context(nc.psum_tensor("PSF", [P, 7, 512], F32))
    PSB = es.enter_context(nc.psum_tensor("PSB", [P, 1, 1024], BF16))

    with nc.Block() as block, es:
        ts = Sem(es, nc, "ts")
        vs = Sem(es, nc, "vs")
        as_ = Sem(es, nc, "as_")
        ps = Sem(es, nc, "ps")
        ws6 = [Sem(es, nc, "ws%d" % i) for i in range(6)]
        xlA, xlB = Sem(es, nc, "xlA"), Sem(es, nc, "xlB")
        osA, osB = Sem(es, nc, "osA"), Sem(es, nc, "osB")
        xoA, xoB, xoC, xoD = (Sem(es, nc, "xoA"), Sem(es, nc, "xoB"),
                              Sem(es, nc, "xoC"), Sem(es, nc, "xoD"))
        wbA, wbB, wbC = Sem(es, nc, "wbA"), Sem(es, nc, "wbB"), Sem(es, nc, "wbC")
        wcA, wcB, wcC = Sem(es, nc, "wcA"), Sem(es, nc, "wcB"), Sem(es, nc, "wcC")
        lv = Sem(es, nc, "lv")
        mk = Sem(es, nc, "mk")
        wbs = [wbA, wbB, wbC]
        wsc = [wcA, wcB, wcC]
        xls = [xlA, xlB]
        oss = [osA, osB]
        xos = [xoA, xoB, xoC, xoD]

        E = Emit(block)

        def W(sem, val):
            return lambda e: e.wait_ge(sem.h, val)

        bank_mark = {b: None for b in range(7)}
        psb_mark = {b: None for b in range(1)}
        mark = {}

        def WB(b, table=None):
            m = (table if table is not None else bank_mark)[b]
            return None if m is None else W(m[0], m[1])

        def WM(key):
            m = mark.get(key)
            return None if m is None else W(m[0], m[1])

        steps = []
        for k in vecs_in:
            a = lv.inc(16)
            steps.append(lambda g, a=a, k=k: a(g.dma_start(out=vs_b[k][:, :], in_=vecs_in[k])))
        for tile, val in ((id_bf, 0.0), (epsv, 1e-5), (m16, 0.0), (mask, -1e30)):
            a = ps.inc()
            steps.append(lambda g, a=a, tile=tile, val=val: a(g.memset(tile[:, :], val)))
        steps.append(W(ps, ps.n))
        for tile, cmp_, fill, pat in (
                (id_bf, ALU.not_equal, 1.0, [[-1, P]]),
                (m16, ALU.is_ge, -1e30, [[-4, 4], [0, 4]])):
            a = ps.inc()
            steps.append(lambda g, a=a, tile=tile, cmp_=cmp_, fill=fill, pat=pat: a(
                g.affine_select(out=tile[:, :], in_=tile[:, :], compare_op=cmp_, fill=fill,
                                base=0, pattern=pat, channel_multiplier=1)))
        steps.append(W(ps, ps.n))
        for j in range(8):
            a = mk.inc(16)
            steps.append(lambda g, a=a, j=j: a(
                g.dma_start(out=mask[16 * j:16 * (j + 1), 16 * j:16 * (j + 1)], in_=m16[:, :])))
        E.add("gpsimd", *steps)
        setup_ps, setup_lv, setup_mk = ps.n, lv.n, mk.n

        E.flush()

        NU = sum(O // OU for (O, K, OU) in WSPECS.values())
        iu_ctr = {"i": 0}
        unit_consumed = {}
        wb_val = {}
        last_wb = {0: 0, 1: 0, 2: 0}

        def stream_one(name, j):
            O, K, OU = WSPECS[name]
            iu = iu_ctr["i"]
            slot = iu % 3
            sz = (K // P) * OU
            wbw = (W(wbs[slot], last_wb[slot])
                   if (iu >= 3 and iu - 3 < NU) else None)
            if iu < NU:
                a = wsc[slot].inc(16)
                E.add("gpsimd",
                      W(ts, unit_consumed[iu - 3]) if iu >= 3 else None,
                      wbw,
                      lambda e, a=a, name=name, j=j, slot=slot, sz=sz: a(
                          e.dma_start(out=wbuf[:, slot, 0:sz], in_=w_in[name][j, :, :])))
                wv = (wsc[slot], wsc[slot].n)
                aw = wbs[slot].inc(16)
                E.add("sync",
                      W(*wv),
                      lambda g, aw=aw, name=name, j=j, slot=slot, sz=sz: aw(
                          g.dma_start(out=w_sc[name][j, :, :], in_=wbuf[:, slot, 0:sz])))
                last_wb[slot] = wbs[slot].n
                wb_val[(name, j)] = (wbs[slot], wbs[slot].n)
            else:
                sem = ws6[iu % 6]
                a = sem.inc(16)
                E.add("sync" if iu % 2 == 0 else "gpsimd",
                      W(ts, unit_consumed[iu - 3]),
                      wbw,
                      W(*wb_val[(name, j)]) if iu < 2 * NU else None,
                      lambda e, a=a, name=name, j=j, slot=slot, sz=sz: a(
                          e.dma_start(out=wbuf[:, slot, 0:sz], in_=w_sc[name][j, :, :])))
                wv = (sem, sem.n)
            iu_ctr["i"] += 1
            return iu, slot, wv, n_oc_of(OU), K, OU

        def n_oc_of(OU):
            return OU // P

        def lhsT_of(slot, kc, jj, OU):
            return wbuf[:, slot, kc * OU + jj * P: kc * OU + (jj + 1) * P]

        def MM(attach, out, lhsT, rhs, start, stop, tile_position=None):
            return lambda e: attach(e.matmul(out=out, lhsT=lhsT, rhs=rhs, start=start,
                                             stop=stop, tile_position=tile_position))

        def MM_noinc(out, lhsT, rhs, start, stop, tile_position=None):
            return lambda e: e.matmul(out=out, lhsT=lhsT, rhs=rhs, start=start,
                                      stop=stop, tile_position=tile_position)

        def mm_group(slot, jj, K, OU, out, rhs_tile, waits, kc_waits=None):
            steps = list(waits)
            nk = K // P
            for kc in range(nk):
                if kc_waits is not None and kc_waits[kc] is not None:
                    steps.append(kc_waits[kc])
                a = ts.inc() if kc == nk - 1 else None
                steps.append(MM(a, out, lhsT_of(slot, kc, jj, OU), rhs_tile[kc],
                                kc == 0, kc == nk - 1) if a else
                             MM_noinc(out, lhsT_of(slot, kc, jj, OU), rhs_tile[kc],
                                      kc == 0, kc == nk - 1))
            E.add("tensor", *steps)
            return ts.n

        def phase_L_pre_group(t, g):
                slot = (t * G + g) % 2
                a = xls[slot].inc(16)
                E.add("gpsimd", WM(("xt", slot)),
                      lambda e, a=a, t=t, g=g, slot=slot: a(e.dma_start(
                          out=xt[:, slot, :],
                          in_=x_in[(t * T + g * P):(t * T + (g + 1) * P), :])))
                xl_val = xls[slot].n

                a1, a2 = vs.inc(), vs.inc()
                wmid = W(vs, vs.n)
                a3 = vs.inc()
                E.add("vector",
                      W(xls[slot], xl_val), WM(("stats6", slot)),
                      lambda v, a=a1, slot=slot: a(v.bn_stats(out=stats6[:, slot, 0:6],
                                                              in_=xt[:, slot, 0:512])),
                      lambda v, a=a2, slot=slot: a(v.bn_stats(out=stats6[:, slot, 6:12],
                                                              in_=xt[:, slot, 512:1024])),
                      wmid, WM(("mv", slot)),
                      lambda v, a=a3, slot=slot: a(v.bn_aggr(out=mv[:, slot, :],
                                                             in_=stats6[:, slot, :])))
                mark[("stats6", slot)] = (vs, vs.n)
                mv_done = vs.n

                a1 = as_.inc()
                wmid = W(as_, as_.n)
                a2 = as_.inc()
                E.add("scalar",
                      W(vs, mv_done), W(ps, setup_ps), WM(("rstd", slot)),
                      lambda s, a=a1, slot=slot: a(s.activation(
                          out=lnt[:, slot, :], in_=mv[:, slot, 1:2], func=AF.Ln, bias=epsv[:, :])),
                      wmid,
                      lambda s, a=a2, slot=slot: a(s.activation(
                          out=rstd[:, slot, :], in_=lnt[:, slot, :], func=AF.Exp, scale=-0.5)))
                rstd_done = as_.n

                xslot = g % 4
                a1 = vs.inc()
                E.add("vector",
                      W(as_, rstd_done), WM(("xn_tm", xslot)),
                      lambda v, a=a1, slot=slot, xslot=xslot: a(v.tensor_scalar(
                          out=xn_tm[:, xslot, :], in0=xt[:, slot, :], scalar1=mv[:, slot, 0:1],
                          scalar2=rstd[:, slot, 0:1], op0=ALU.subtract, op1=ALU.mult)))
                mark[("xt", slot)] = (vs, vs.n)
                mark[("mv", slot)] = (vs, vs.n)
                mark[("rstd", slot)] = (vs, vs.n)
                return vs.n

        def phase_L_pre(t):
            return [phase_L_pre_group(t, g) for g in range(G)]

        def phase_L_tr_group(t, applies, g):
                xslot = g % 4
                apply_done = applies[g]
                pb = 0
                steps = [W(vs, apply_done), W(ps, setup_ps), WB(pb, psb_mark)]
                for j in range(CH):
                    a = ts.inc() if j == CH - 1 else None
                    steps.append(lambda e, a=a, j=j, pb=pb, xslot=xslot: (
                        a(e.transpose(out=PSB[:, pb, j * P:(j + 1) * P],
                                      in_=xn_tm[:, xslot, j * P:(j + 1) * P],
                                      identity=id_bf[:, :])) if a else
                        e.transpose(out=PSB[:, pb, j * P:(j + 1) * P],
                                    in_=xn_tm[:, xslot, j * P:(j + 1) * P],
                                    identity=id_bf[:, :])))
                E.add("tensor", *steps)
                mark[("xn_tm", xslot)] = (ts, ts.n)
                tr_done = ts.n

                a1 = vs.inc()
                E.add("vector",
                      W(ts, tr_done), WM("xn_f"),
                      lambda v, a=a1, g=g, pb=pb: a(v.tensor_copy(
                          out=xn_f[:, :, g * P:(g + 1) * P],
                          in_=PSB[:, pb, :].rearrange("p (c t) -> p c t", c=CH))))
                psb_mark[pb] = (vs, vs.n)
                return vs.n

        def phase_L_tr(t, applies):
            for g in range(G):
                phase_L_tr_group(t, applies, g)
            return vs.n

        def phase_Q(t, ln_all):
            for j in range(4):
                iu, slot, wval, n_oc, K, OU = stream_one("wqkv", j)
                for jj in range(n_oc):
                    oc = j * n_oc + jj
                    b = oc % 6
                    waits = [W(*wval)] if jj == 0 else []
                    if j == 0 and jj == 0:
                        waits.append(W(vs, ln_all))
                    waits.append(WB(b))
                    mm_group(slot, jj, K, OU, PSF[:, b, 0:T],
                             [xn_f[:, kc, 0:T] for kc in range(K // P)], waits)
                unit_consumed[iu] = ts.n
                mm_done = ts.n

                steps = [W(ts, mm_done)]
                if j == 0:
                    steps += [W(lv, setup_lv), WM("Br")]
                for jj in range(n_oc):
                    oc = j * n_oc + jj
                    b = oc % 6
                    a = vs.inc()
                    if oc < 8:
                        steps.append(lambda v, a=a, oc=oc, b=b: a(v.tensor_scalar(
                            out=Br[:, 16 + oc, 0:T], in0=PSF[:, b, 0:T],
                            scalar1=vs_b["bqkv"][:, oc:oc + 1], scalar2=0.125,
                            op0=ALU.add, op1=ALU.mult)))
                    else:
                        steps.append(lambda v, a=a, oc=oc, b=b: a(v.tensor_scalar(
                            out=Br[:, 16 + oc, 0:T], in0=PSF[:, b, 0:T],
                            scalar1=vs_b["bqkv"][:, oc:oc + 1], scalar2=None, op0=ALU.add)))
                    bank_mark[b] = (vs, vs.n)
                E.add("vector", *steps)
            for j in range(4, 6):
                iu, slot, wval, n_oc, K, OU = stream_one("wqkv", j)
                vh = j - 4
                for g in range(G):
                    b = (g % 2)
                    waits = [W(*wval)] if g == 0 else []
                    waits.append(WB(b))
                    if j == 4 and g == 0:
                        waits.append(WM("Vtm"))
                    nk = K // P
                    steps = list(w for w in waits if w is not None)
                    for kc in range(nk):
                        aa = ts.inc() if kc == nk - 1 else None
                        lh = xn_f[:, kc, g * P:(g + 1) * P]
                        rh = wbuf[:, slot, kc * OU: (kc + 1) * OU]
                        if aa:
                            steps.append(lambda e, aa=aa, lh=lh, rh=rh, b=b, kc=kc, nk=nk: aa(
                                e.matmul(out=PSF[:, b, 0:512], lhsT=lh, rhs=rh,
                                         start=(kc == 0), stop=(kc == nk - 1))))
                        else:
                            steps.append(lambda e, lh=lh, rh=rh, b=b, kc=kc, nk=nk:
                                e.matmul(out=PSF[:, b, 0:512], lhsT=lh, rhs=rh,
                                         start=(kc == 0), stop=(kc == nk - 1)))
                    E.add("tensor", *steps)
                    mmv = ts.n
                    a = vs.inc()
                    E.add("vector", W(ts, mmv),
                          lambda v, a=a, g=g, vh=vh, b=b: a(v.tensor_copy(
                              out=Vtm[:, g, vh * 512:(vh + 1) * 512], in_=PSF[:, b, 0:512])))
                    bank_mark[b] = (vs, vs.n)
                unit_consumed[iu] = ts.n
            return vs.n

        def plan_scores(c, qkv_done):
            g, hp = divmod(c, HPAIRS)
            tok = slice(g * P, (g + 1) * P)
            waits = []
            if c == 0:
                waits += [W(vs, qkv_done), W(mk, setup_mk)]
            B0 = 2 + 2 * (c % 2)
            waits.append(WB(B0))
            waits.append(WB(B0 + 1))
            a1, a2 = ts.inc(), ts.inc()
            E.add("tensor", *waits,
                  MM(a1, PSF[:, B0, 0:128], Br[0:64, 16 + hp, tok],
                     Br[0:64, 24 + hp, tok], True, True),
                  MM(a2, PSF[:, B0 + 1, 0:128], Br[64:128, 16 + hp, tok],
                     Br[64:128, 24 + hp, tok], True, True))
            return ts.n

        def plan_smmax(c, sc_done):
            pr = c % 3
            B0 = 2 + 2 * (c % 2)
            for h in range(2):
                a1 = vs.inc()
                wmid = W(vs, vs.n)
                a2 = vs.inc()
                E.add("vector",
                      W(ts, sc_done - (1 - h)), WM(("sm", pr, h)),
                      lambda v, a=a1, pr=pr, h=h, b=B0 + h: a(v.tensor_tensor(
                          out=sm[:, pr, h, :], in0=PSF[:, b, 0:128],
                          in1=mask[:, :], op=ALU.add)),
                      wmid, WM(("nmax", pr, h)),
                      lambda v, a=a2, pr=pr, h=h: a(v.reduce_max(
                          out=nmax[:, pr, h:h + 1], in_=sm[:, pr, h, :], axis=AX.X, negate=True)))
                bank_mark[B0 + h] = (vs, vs.n - 1)
            sm_done = vs.n
            for h in range(2):
                a3 = as_.inc()
                E.add("scalar",
                      W(vs, sm_done) if h == 0 else None,
                      WM(("ee", pr, h)), WM(("rsum", pr, h)),
                      lambda s, a=a3, pr=pr, h=h: a(s.activation(
                          out=ee[:, pr, h, :], in_=sm[:, pr, h, :], func=AF.Exp,
                          bias=nmax[:, pr, h:h + 1], accum_out=rsum[:, pr, h:h + 1])))
                mark[("sm", pr, h)] = (as_, as_.n)
                mark[("nmax", pr, h)] = (as_, as_.n)
            return as_.n

        def plan_norm(c, exp_done):
            pr = c % 3
            for h in range(2):
                a4 = vs.inc()
                wmid = W(vs, vs.n)
                a5 = vs.inc()
                E.add("vector",
                      W(as_, exp_done) if h == 0 else None,
                      WM(("rrec", pr, h)),
                      lambda v, a=a4, pr=pr, h=h: a(v.reciprocal(
                          out=rrec[:, pr, h:h + 1], in_=rsum[:, pr, h:h + 1])),
                      wmid, WM(("an", pr, h)),
                      lambda v, a=a5, pr=pr, h=h: a(v.tensor_scalar(
                          out=an[:, pr, h, :], in0=ee[:, pr, h, :],
                          scalar1=rrec[:, pr, h:h + 1], scalar2=None, op0=ALU.mult)))
                mark[("ee", pr, h)] = (vs, vs.n)
                mark[("rsum", pr, h)] = (vs, vs.n)
                mark[("rrec", pr, h)] = (vs, vs.n)
            return vs.n

        def plan_tr3(c, norm_done):
            pr = c % 3
            off = (c % 2) * 2 * P
            a3 = ts.inc()
            E.add("tensor",
                  W(vs, norm_done), WB(0, psb_mark),
                  lambda e, pr=pr, off=off: e.transpose(
                      out=PSB[:, 0, off:off + P], in_=an[:, pr, 0, :], identity=id_bf[:, :]),
                  lambda e, a=a3, pr=pr, off=off: a(e.transpose(
                      out=PSB[:, 0, off + P:off + 2 * P], in_=an[:, pr, 1, :],
                      identity=id_bf[:, :])))
            mark[("an", pr, 0)] = (ts, ts.n)
            mark[("an", pr, 1)] = (ts, ts.n)
            return ts.n

        def plan_trevac(c, tr3_done):
            pr = c % 5
            off = (c % 2) * 2 * P
            a1 = vs.inc()
            E.add("vector",
                  W(ts, tr3_done), WM(("at", pr)),
                  lambda v, a=a1, pr=pr, off=off: a(v.tensor_copy(
                      out=at_sb[:, pr, :, :],
                      in_=PSB[:, 0, off:off + 2 * P].rearrange("p (h t) -> p h t", h=2))))
            psb_mark[0] = (vs, vs.n)
            return vs.n

        def plan_av(c, ev_done):
            pr = c % 5
            g, hp = divmod(c, HPAIRS)
            BA = 6
            a2 = ts.inc()
            E.add("tensor",
                  W(vs, ev_done), WB(BA),
                  MM_noinc(PSF[0:64, BA, 0:128], Vtm[:, g, hp * 128: hp * 128 + 64],
                           at_sb[:, pr, 0, :], True, True, (0, 0)),
                  MM(a2, PSF[64:128, BA, 0:128], Vtm[:, g, hp * 128 + 64: hp * 128 + 128],
                     at_sb[:, pr, 1, :], True, True, (0, 64)))
            mark[("at", pr)] = (ts, ts.n)
            return ts.n

        def plan_avevac(c, av_done):
            g, hp = divmod(c, HPAIRS)
            a1 = vs.inc()
            E.add("vector",
                  W(ts, av_done), WM("attnout") if c == 0 else None,
                  lambda v, a=a1, g=g, hp=hp, BA=6: a(v.tensor_scalar(
                      out=Ao[:, hp, g * P:(g + 1) * P], in0=PSF[:, BA, 0:128],
                      scalar1=vs_b["bqkv"][:, 16 + hp:17 + hp], scalar2=None,
                      op0=ALU.add)))
            bank_mark[6] = (vs, vs.n)
            return vs.n

        def phase_F1(t):
            NOC = 32
            f1_state = {}
            cur = {}
            for k in range(2 * NOC):
                a_ss, w_ss = k // 8, k % 8
                if w_ss == 0 and a_ss % 2 == 0:
                    cur["u1"] = stream_one("w1", a_ss // 2)
                    cur["u2"] = stream_one("w2", a_ss // 2)
                (iu1, slot1, wval1, n_oc, K, OU) = cur["u1"]
                (iu2, slot2, wval2, _, _, _) = cur["u2"]
                half = 4 * (a_ss % 2)

                if w_ss < 4:
                    oc = a_ss * 4 + w_ss
                    b = (oc + 2) % 4
                    waits = [W(*wval1)] if (w_ss == 0 and half == 0) else []
                    waits.append(WB(b))
                    mmv = mm_group(slot1, half + w_ss, K, OU, PSF[:, b, 0:T],
                                   [xn_f[:, kc, 0:T] for kc in range(K // P)], waits)
                    f1_state[("u1", oc)] = mmv
                    if w_ss == 3:
                        if half == 4:
                            unit_consumed[iu1] = ts.n
                        steps = []
                        for q in range(4):
                            occ = a_ss * 4 + q
                            a1 = as_.inc()
                            steps += [W(ts, f1_state[("u1", occ)]),
                                      WM(("t1", occ % 4)),
                                      lambda s, a=a1, occ=occ, bq=(occ + 2) % 4: a(
                                          s.activation(
                                              out=t1[:, occ % 4, 0:T], in_=PSF[:, bq, 0:T],
                                              func=AF.Silu,
                                              scale=vs_b["w1s"][:, occ:occ + 1]))]
                            f1_state[("silu", occ)] = as_.n
                            bank_mark[(occ + 2) % 4] = (as_, as_.n)
                        E.add("scalar", *steps)
                else:
                    oc = a_ss * 4 + (w_ss - 4)
                    b = (oc + 2) % 4
                    waits = [W(*wval2)] if (w_ss == 4 and half == 0) else []
                    waits.append(W(as_, f1_state[("silu", oc)]))
                    waits.append(WB(b))
                    mmv = mm_group(slot2, half + w_ss - 4, K, OU, PSF[:, b, 0:T],
                                   [xn_f[:, kc, 0:T] for kc in range(K // P)], waits)
                    if w_ss == 7 and half == 4:
                        unit_consumed[iu2] = ts.n
                    a2 = vs.inc()
                    E.add("vector",
                          W(ts, mmv), W(as_, f1_state[("silu", oc)]),
                          lambda v, a=a2, oc=oc, b=b: a(v.scalar_tensor_tensor(
                              out=Ar[:, oc, 0:T], in0=PSF[:, b, 0:T],
                              scalar=vs_b["w2s"][:, oc:oc + 1], in1=t1[:, oc % 4, 0:T],
                              op0=ALU.mult, op1=ALU.mult)))
                    bank_mark[b] = (vs, vs.n)
                    mark[("t1", oc % 4)] = (vs, vs.n)
            return vs.n, ts.n

        def phase_H1A(t, qkv_done, u_done, proj_mm_prev, h2_mm_prev):
            applies_next = []
            gelu_marks = []
            """h1 (Br = gelu(h1 @ Ar), 32 groups, banks 0-3) with the NCH
            attention chains woven in (banks 4-5, DVE/ACT otherwise idle
            here). q/k live in Br[16:32] and are guarded per chunk against
            h1's gelu evacs; all chains finish by group ~19."""
            mark["attnout"] = (ts, proj_mm_prev)
            O, K, OU = WSPECS["h1"]
            sc_d, exp_d, norm_d, tr3_d, ev_d, av_d = {}, {}, {}, {}, {}, {}
            cur = [None]
            for gi in range(32):
                jj = gi % 2
                if jj == 0:
                    cur[0] = stream_one("h1", gi // 2)
                iu, slot, wval, n_oc, K, OU = cur[0]

                prc = [c for c in (2 * (gi - 2), 2 * (gi - 2) + 1) if 0 <= c < NCH]
                for c in prc:
                    tr3_d[c] = plan_tr3(c, norm_d[c])
                for c in prc:
                    ev_d[c] = plan_trevac(c, tr3_d[prc[-1]])
                for c in (2 * (gi - 1), 2 * (gi - 1) + 1):
                    if 0 <= c < NCH:
                        norm_d[c] = plan_norm(c, exp_d[c])
                for c in (2 * gi, 2 * gi + 1):
                    if c < NCH:
                        sc_d[c] = plan_scores(c, qkv_done)
                        exp_d[c] = plan_smmax(c, sc_d[c])
                c_odd = 2 * (gi - 4) + 1
                if 0 <= c_odd < NCH:
                    av_d[c_odd] = plan_av(c_odd, ev_d[c_odd])
                    plan_avevac(c_odd, av_d[c_odd])

                if gi in (20, 23, 26, 29) and t + 1 < NT:
                    applies_next.append(phase_L_pre_group(t + 1, (gi - 20) // 3))

                oc = gi
                b = oc % 2
                waits = [W(*wval)] if jj == 0 else []
                if gi == 0:
                    waits.append(W(vs, u_done))
                waits.append(WB(b))
                mmv = mm_group(slot, jj, K, OU, PSF[:, b, 0:T],
                               [Ar[:, kc, 0:T] for kc in range(K // P)], waits)
                if jj == n_oc - 1:
                    unit_consumed[iu] = ts.n
                c_ev = 2 * (gi - 3)
                if 0 <= c_ev < NCH:
                    av_d[c_ev] = plan_av(c_ev, ev_d[c_ev])
                    plan_avevac(c_ev, av_d[c_ev])
                steps = [W(ts, mmv)]
                if gi == 0:
                    steps.append(W(ts, h2_mm_prev))
                if oc >= 16:
                    steps.append(W(ts, sc_d[24 + (oc - 16) % 8]))
                a = as_.inc()
                steps.append(lambda s, a=a, oc=oc, b=b: a(
                    s.activation(out=Br[:, oc, 0:T], in_=PSF[:, b, 0:T], func=AF.Gelu,
                                 scale=vs_b["h1s"][:, oc:oc + 1])))
                E.add("scalar", *steps)
                bank_mark[b] = (as_, as_.n)
                gelu_marks.append((as_, as_.n))
            for c in range(NCH):
                if c not in av_d:
                    av_d[c] = plan_av(c, ev_d[c])
                    plan_avevac(c, av_d[c])
            return vs.n, ts.n, as_.n, applies_next, gelu_marks

        def phase_P(t, attn_done, fillers=()):
            fillers = list(fillers)
            for j in range(WSPECS["wproj"][0] // WSPECS["wproj"][2]):
                iu, slot, wval, n_oc, K, OU = stream_one("wproj", j)
                for jj in range(n_oc):
                    oc = j * n_oc + jj
                    b = oc % 6
                    waits = [W(*wval)] if jj == 0 else []
                    if j == 0 and jj == 0:
                        waits.append(W(vs, attn_done))
                    waits.append(WB(b))
                    mmv = mm_group(slot, jj, K, OU, PSF[:, b, 0:T],
                                   [Ao[:, kc, 0:T] for kc in range(K // P)], waits)
                    if fillers:
                        fillers.pop(0)()
                    steps = [W(ts, mmv)]
                    if j == 0 and jj == 0:
                        steps.append(WM("sum"))
                    a = vs.inc()
                    steps.append(lambda v, a=a, oc=oc, b=b: a(v.tensor_scalar(
                        out=sum_f[:, oc, :], in0=PSF[:, b, 0:T],
                        scalar1=vs_b["bproj"][:, oc:oc + 1], scalar2=None, op0=ALU.add)))
                    bank_mark[b] = (vs, vs.n)
                    E.add("vector", *steps)
                unit_consumed[iu] = ts.n
            return ts.n

        def phase_H(t, wname, sname, src, dst, src_ready, dst_free,
                    src_chunk_marks=None):
            O, K, OU = WSPECS[wname]
            for j in range(O // OU):
                iu, slot, wval, n_oc, K, OU = stream_one(wname, j)
                for jj in range(n_oc):
                    oc = j * n_oc + jj
                    b = oc % 6
                    waits = [W(*wval)] if jj == 0 else []
                    kcw = None
                    if j == 0 and jj == 0:
                        if src_chunk_marks is not None:
                            kcw = [W(src_chunk_marks[kc][0], src_chunk_marks[kc][1])
                                   for kc in range(K // P)]
                        else:
                            waits.append(W(src_ready[0], src_ready[1]))
                    waits.append(WB(b))
                    mmv = mm_group(slot, jj, K, OU, PSF[:, b, 0:T],
                                   [src[:, kc, 0:T] for kc in range(K // P)], waits,
                                   kc_waits=kcw)
                    steps = [W(ts, mmv)]
                    if j == 0 and jj == 0:
                        steps.append(W(dst_free[0], dst_free[1]))
                    a = as_.inc()
                    steps.append(lambda s, a=a, oc=oc, b=b, sname=sname, dst=dst: a(
                        s.activation(out=dst[:, oc, 0:T], in_=PSF[:, b, 0:T], func=AF.Gelu,
                                     scale=vs_b[sname][:, oc:oc + 1])))
                    E.add("scalar", *steps)
                    bank_mark[b] = (as_, as_.n)
                unit_consumed[iu] = ts.n
            return ts.n, as_.n

        def phase_W3(t, src_ready):
            for j in range(1024 // 256):
                iu, slot, wval, n_oc, K, OU = stream_one("w3", j)
                for jj in range(n_oc):
                    oc = j * n_oc + jj
                    b = oc % 6
                    waits = [W(*wval)] if jj == 0 else []
                    if j == 0 and jj == 0:
                        waits.append(W(src_ready[0], src_ready[1]))
                    waits.append(WB(b))
                    mmv = mm_group(slot, jj, K, OU, PSF[:, b, 0:T],
                                   [Ar[:, kc, 0:T] for kc in range(K // P)], waits)
                    a = vs.inc()
                    E.add("vector", W(ts, mmv),
                          lambda v, a=a, oc=oc, b=b: a(v.scalar_tensor_tensor(
                              out=sum_f[:, oc, :], in0=PSF[:, b, 0:T],
                              scalar=vs_b["w3s"][:, oc:oc + 1], in1=sum_f[:, oc, :],
                              op0=ALU.mult, op1=ALU.add)))
                    bank_mark[b] = (vs, vs.n)
                unit_consumed[iu] = ts.n
            return ts.n, vs.n

        def plan_xf_load(t):
            a = xos[t % 2].inc(16)
            E.add("scalar", WM("xf"),
                  lambda e, a=a, t=t: a(e.dma_start(
                      out=xf[:, :, :],
                      in_=xf_in[:, t * T:(t + 1) * T].rearrange(
                          "(c p) t -> p c t", p=P))))
            return (xos[t % 2], xos[t % 2].n)

        def phase_O(t, sum_done, xf_val):
            steps = [W(vs, sum_done), W(*xf_val)]
            for oc in range(CH):
                a = vs.inc()
                steps.append(lambda v, a=a, oc=oc: a(v.tensor_tensor(
                    out=xf[:, oc, :], in0=sum_f[:, oc, :], in1=xf[:, oc, :],
                    op=ALU.add)))
            E.add("vector", *steps)
            adds_done = vs.n
            mark["sum"] = (vs, adds_done)

            oslot = t % 2
            a = oss[oslot].inc(16)
            E.add("scalar",
                  W(vs, adds_done),
                  lambda e, a=a, t=t: a(e.dma_start(
                      out=out_d[:, t * T:(t + 1) * T].rearrange(
                          "(c p) t -> p c t", p=P),
                      in_=xf[:, :, :])))
            mark["xf"] = (oss[oslot], oss[oslot].n)

        proj_mm_prev = 0
        h2_mm_prev = 0
        ln_next = phase_L_tr(0, phase_L_pre(0))
        for t in range(NT):
            ln_all = ln_next
            qkv_done = phase_Q(t, ln_all)
            u_done, f1_mm_done = phase_F1(t)
            mark["xn_f"] = (ts, f1_mm_done)
            attn_done, h1_mm, h1_ev, applies_next, gelu_marks = phase_H1A(
                t, qkv_done, u_done, proj_mm_prev, h2_mm_prev)
            if t + 1 < NT:
                fil = [lambda g=g: phase_L_tr_group(t + 1, applies_next, g)
                       for g in range(G)]
            else:
                fil = []
            proj_mm_prev = phase_P(t, attn_done, fil)
            if t + 1 < NT:
                ln_next = vs.n
            xf_val = plan_xf_load(t)
            h2_mm, h2_ev = phase_H(t, "h2", "h2s", Br, Ar, (as_, h1_ev), (ts, h1_mm),
                                   src_chunk_marks=gelu_marks)
            h2_mm_prev = h2_mm
            mark["Br"] = (ts, h2_mm)
            w3_mm_prev, sum_done = phase_W3(t, (as_, h2_ev))
            phase_O(t, sum_done, xf_val)
            E.flush()

        a_fin = None
        E.add("scalar", W(osA, osA.n), W(osB, osB.n))
        E.flush()

    return nc


def _layout_weight(w, OU):
    O, K = w.shape
    return np.ascontiguousarray(
        w.reshape(O // OU, OU, K // P, P).transpose(0, 3, 2, 1).reshape(O // OU, P, (K // P) * OU))


def _layout_vec(v):
    n = v.shape[0] // P
    return np.ascontiguousarray(v.reshape(n, P).T)


def kernel(L0_attn_x, L0_attn_Wqkv, L0_attn_bqkv, L0_attn_Wproj, L0_attn_bproj,
           L0_ffn_w1_w, L0_ffn_w1_s, L0_ffn_w2_w, L0_ffn_w2_s, L0_ffn_h1_w,
           L0_ffn_h1_s, L0_ffn_h2_w, L0_ffn_h2_s, L0_ffn_w3_w, L0_ffn_w3_s):
    NCORES = 8
    x = np.asarray(L0_attn_x)
    n, b, d = x.shape
    b_core = b // NCORES
    nc = build_nc(b_core=b_core, T=512)

    wmats = {
        "wqkv": np.asarray(L0_attn_Wqkv), "wproj": np.asarray(L0_attn_Wproj),
        "w1": np.asarray(L0_ffn_w1_w), "w2": np.asarray(L0_ffn_w2_w),
        "h1": np.asarray(L0_ffn_h1_w), "h2": np.asarray(L0_ffn_h2_w),
        "w3": np.asarray(L0_ffn_w3_w),
    }
    wblocked = {k: _layout_weight(w, WSPECS[k][2]) for k, w in wmats.items()}
    vecs = {
        "bqkv": _layout_vec(np.asarray(L0_attn_bqkv)),
        "bproj": _layout_vec(np.asarray(L0_attn_bproj)),
        "w1s": _layout_vec(np.asarray(L0_ffn_w1_s)),
        "w2s": _layout_vec(np.asarray(L0_ffn_w2_s)),
        "h1s": _layout_vec(np.asarray(L0_ffn_h1_s)),
        "h2s": _layout_vec(np.asarray(L0_ffn_h2_s)),
        "w3s": _layout_vec(np.asarray(L0_ffn_w3_s)),
    }
    in_maps = []
    for c in range(NCORES):
        xc = np.ascontiguousarray(
            x[:, c * b_core:(c + 1) * b_core, :].transpose(1, 0, 2).reshape(b_core * n, d))
        m = {"x": xc, "xf": np.ascontiguousarray(xc.T)}
        m.update(wblocked)
        m.update(vecs)
        in_maps.append(m)

    res = run_bass_kernel_spmd(nc, in_maps, core_ids=list(range(NCORES)))
    out = np.empty((n, b, d), np.float32)
    for c in range(NCORES):
        oc = np.ascontiguousarray(res.results[c]["out"].T).reshape(
            b_core, n, d).transpose(1, 0, 2)
        out[:, c * b_core:(c + 1) * b_core, :] = oc
    return out
